# version 08
# baseline (speedup 1.0000x reference)
import numpy as np
import ml_dtypes
from contextlib import ExitStack

import concourse.bass as bass
import concourse.mybir as mybir
from concourse.bass_utils import run_bass_kernel_spmd

F32 = mybir.dt.float32
BF16 = mybir.dt.bfloat16
FP8 = mybir.dt.float8e4
I32 = mybir.dt.int32
Alu = mybir.AluOpType
Act = mybir.ActivationFunctionType
AXX = mybir.AxisListType.X

D = 2048
IFF = 8192
NTOK = 1024
NCH = 8
TCH = 4
NHC = 16
NIC = 64
IB = 512
NIB = IFF // IB
HSL = D // 8
ISL = IFF // 8
NH, KVH, HD = 16, 4, 128
EPS_LN = 1e-5
EPS_AT = 1e-6
CLIP_A = 15.0 / 128.0
MAGIC_W = 1.5 * 2.0 ** 22
TINY = 2.0 ** -100
AGN = {"q": 1, "k": 2, "v": 3, "o": 4, "g": 5, "u": 6, "d": 7, "kv": 8}

ENGS = ("sync", "scalar", "vector", "gpsimd", "tensor")


class Sem:
    __slots__ = ("h", "v", "name")

    def __init__(self, h, name):
        self.h = h
        self.v = 0
        self.name = name

    def inc(self, n=1):
        self.v += n
        return self.v


class Bank:
    def __init__(self, ap, wr, rd):
        self.ap = ap
        self.wr = wr
        self.rd = rd


class K:
    def __init__(self, nc):
        self.nc = nc
        self.q = {e: [] for e in ENGS}
        self._es = ExitStack()

    def sem(self, name):
        return Sem(self._es.enter_context(self.nc.semaphore(name)), name)

    def sb(self, name, shape, dt):
        return self._es.enter_context(self.nc.sbuf_tensor(name, shape, dt))

    def ps(self, name, shape, dt):
        return self._es.enter_context(self.nc.psum_tensor(name, shape, dt))

    def o(self, eng, fn, waits=(), inc=None, n=1, drain=False):
        wl = tuple((s.h, int(v)) for (s, v) in waits if v > 0)
        if inc is not None:
            inc.inc(n)
            semh = inc.h

            def run(e, fn=fn, wl=wl, semh=semh, n=n, drain=drain):
                for h, v in wl:
                    e.wait_ge(h, v)
                fn(e).then_inc(semh, n)
                if drain:
                    e.drain()
        else:
            def run(e, fn=fn, wl=wl, drain=drain):
                for h, v in wl:
                    e.wait_ge(h, v)
                fn(e)
                if drain:
                    e.drain()
        self.q[eng].append(run)

    def w(self, eng, *waits):
        wl = tuple((s.h, int(v)) for (s, v) in waits if v > 0)
        if not wl:
            return

        def run(e, wl=wl):
            for h, v in wl:
                e.wait_ge(h, v)
        self.q[eng].append(run)

    def nop(self, eng, inc, n=1, waits=()):
        self.q[eng].append(lambda e: e.drain())
        self.o(eng, lambda e: e.nop(), waits=waits, inc=inc, n=n)

    def v(self, fn, waits=(), inc=None, n=1):
        self.o("vector", fn, waits=waits, inc=inc, n=n, drain=True)

    def a(self, fn, waits=(), inc=None, n=1):
        self.o("scalar", fn, waits=waits, inc=inc, n=n, drain=True)

    def g(self, fn, waits=(), inc=None, n=1):
        self.o("gpsimd", fn, waits=waits, inc=inc, n=n, drain=True)

    def t(self, fn, waits=(), inc=None, n=1):
        self.o("tensor", fn, waits=waits, inc=inc, n=n)

    def dma(self, eng, out, in_, sem, waits=(), slow=False):
        if slow:
            self.o(eng, lambda e, o=out, i=in_: e.dma_start(
                out=o, in_=i, allow_slow_non_contiguous=True),
                waits=waits, inc=sem, n=16)
        else:
            self.o(eng, lambda e, o=out, i=in_: e.dma_start(out=o, in_=i),
                   waits=waits, inc=sem, n=16)


def build_nc():
    nc = bass.Bass("TRN2", target_bir_lowering=False, debug=False)

    x_sl = nc.dram_tensor("x_sl", [NTOK, D], F32, kind="ExternalInput").ap()
    wg_sl = nc.dram_tensor("wg_sl", [IFF, HSL], F32, kind="ExternalInput").ap()
    wu_sl = nc.dram_tensor("wu_sl", [IFF, HSL], F32, kind="ExternalInput").ap()
    wd_sl = nc.dram_tensor("wd_sl", [D, ISL], F32, kind="ExternalInput").ap()
    wq_sl = nc.dram_tensor("wq_sl", [D, HSL], F32, kind="ExternalInput").ap()
    wk_sl = nc.dram_tensor("wk_sl", [512, HSL], F32, kind="ExternalInput").ap()
    wv_sl = nc.dram_tensor("wv_sl", [512, HSL], F32, kind="ExternalInput").ap()
    wo_sl = nc.dram_tensor("wo_sl", [D, HSL], F32, kind="ExternalInput").ap()
    bq_in = nc.dram_tensor("bq_in", [D], F32, kind="ExternalInput").ap()
    bk_in = nc.dram_tensor("bk_in", [512], F32, kind="ExternalInput").ap()
    bv_in = nc.dram_tensor("bv_in", [512], F32, kind="ExternalInput").ap()
    id_in = nc.dram_tensor("id_in", [128, 128], BF16, kind="ExternalInput").ap()
    out = nc.dram_tensor("out", [NTOK, D], F32, kind="ExternalOutput").ap()

    agin = {
        "g": nc.dram_tensor("agin_g", [HSL, IFF], BF16),
        "u": nc.dram_tensor("agin_u", [HSL, IFF], BF16),
        "d": nc.dram_tensor("agin_d", [ISL, D], BF16),
        "q": nc.dram_tensor("agin_q", [HSL, D], BF16),
        "k": nc.dram_tensor("agin_k", [HSL, 512], BF16),
        "v": nc.dram_tensor("agin_v", [HSL, 512], BF16),
        "o": nc.dram_tensor("agin_o", [HSL, D], BF16),
    }
    agout = {
        "g": nc.dram_tensor("wgT", [D, IFF], BF16),
        "u": nc.dram_tensor("wuT", [D, IFF], BF16),
        "d": nc.dram_tensor("wdT", [IFF, D], BF16),
        "q": nc.dram_tensor("wqT", [D, D], BF16),
        "k": nc.dram_tensor("wkT", [D, 512], BF16),
        "v": nc.dram_tensor("wvT", [D, 512], BF16),
        "o": nc.dram_tensor("woT", [D, D], BF16),
    }
    kvin = nc.dram_tensor("kvin", [4 * 128, 132], F32)
    kvout = nc.dram_tensor("kvout", [16 * 128, 132], F32)

    k = K(nc)
    with k._es:
        RES = k.sb("RES", [128, NCH, D], F32)
        X1T = k.sb("X1T", [128, 8192], BF16)
        IQT = k.sb("IQT", [128, 32768], BF16)
        WBUF = k.sb("WBUF", [128, 16384], BF16)
        SCR1 = k.sb("SCR1", [128, D], F32)
        SCR2 = k.sb("SCR2", [128, D], F32)
        QF8 = k.sb("QF8", [128, D], FP8)
        QBF = k.sb("QBF", [128, D], BF16)
        SMALL = k.sb("SMALL", [128, 512], F32)
        IDENT = k.sb("IDENT", [128, 128], BF16)
        BBK = k.sb("BBK", [128, 512], F32)
        BBV = k.sb("BBV", [128, 512], F32)
        BQF = k.sb("BQF", [128, 16], F32)
        KVB = k.sb("KVB", [128, 4 * 132], BF16)
        ONESB = k.sb("ONESB", [128, 8], BF16)
        EPSC = k.sb("EPSC", [128, 2], F32)

        pa_t = k.ps("pa", [128, 4, 512], F32)
        pb_t = k.ps("pb", [128, 2, 512], F32)
        pt_t = k.ps("pt", [128, 2, 1024], BF16)

        dx = k.sem("dx")
        dxx = k.sem("dxx")
        dwp = [k.sem("dwp0"), k.sem("dwp1")]
        dgl = [k.sem("dgl0"), k.sem("dgl1")]
        ddl = [k.sem("ddl0"), k.sem("ddl1")]
        d_gS = k.sem("d_gS")
        d_uS = k.sem("d_uS")
        d_wd = [k.sem("d_wd0"), k.sem("d_wd1")]
        d_kS = k.sem("d_kS")
        d_vS = k.sem("d_vS")
        d_qo = [k.sem("d_qo0"), k.sem("d_qo1")]
        d_sh = {m: [k.sem(f"d_sh_{m}0"), k.sem(f"d_sh_{m}1")]
                for m in "gudqkvo"}
        d_kvs = k.sem("d_kvs")
        dgp = k.sem("dgp")
        ag = k.sem("ag")
        vq = k.sem("vq")
        gq = k.sem("gq")
        aq = k.sem("aq")
        pq = k.sem("pq")
        wf = k.sem("wf")
        paw = [k.sem(f"paw{i}") for i in range(4)]
        par = [k.sem(f"par{i}") for i in range(4)]
        pbw = [k.sem(f"pbw{i}") for i in range(2)]
        pbr = [k.sem(f"pbr{i}") for i in range(2)]
        ptw = [k.sem(f"ptw{i}") for i in range(2)]
        ptr = [k.sem(f"ptr{i}") for i in range(2)]
        pa = [Bank(pa_t[:, i, :], paw[i], par[i]) for i in range(4)]
        pb = [Bank(pb_t[:, i, :], pbw[i], pbr[i]) for i in range(2)]
        pt = [Bank(pt_t[:, i, :], ptw[i], ptr[i]) for i in range(2)]

        x1t_v = X1T[:].rearrange("p (c f) -> p c f", c=NHC)
        iqt_v = IQT[:].rearrange("p (c f) -> p c f", c=NIC)
        sk_v = X1T[:, 0:4096].rearrange("p (c f) -> p c f", c=8)
        vv_v = X1T[:, 4096:8192].rearrange("p (c f) -> p c f", c=8)
        kvs1 = X1T[:, 0:2112].bitcast(F32).rearrange("p (c f) -> p c f", c=8)
        kvs2 = X1T[:, 2112:4224].bitcast(F32).rearrange("p (c f) -> p c f", c=8)
        x2t_v = IQT[:, 0:16384].rearrange("p (c f) -> p c f", c=NHC)
        sqt_v = IQT[:, 16384:32768].rearrange("p (c f) -> p c f", c=NHC)
        atnT_v = x2t_v
        kvb_v = KVB[:].rearrange("p (c f) -> p c f", c=4)

        def wslot(off, nch, wid):
            return WBUF[:, off:off + nch * wid].rearrange(
                "p (c f) -> p c f", c=nch)

        gS_v = wslot(0, NHC, IB)
        uS_v = wslot(8192, NHC, IB)
        wdS_v = [wslot(0, 8, 512), wslot(4096, 8, 512)]
        kS_v = wslot(0, NHC, 512)
        vS_v = wslot(8192, NHC, 512)
        qoS_v = [wslot(0, NHC, 512), wslot(8192, NHC, 512)]

        wT_r = {m: agout[m].ap().rearrange("(c p) w -> p c w", p=128)
                for m in agout}

        S_SUM, S_SQ, S_NM, S_MSQ, S_VAR, S_STD, S_RSTD = 0, 8, 16, 24, 32, 40, 48
        S_NRM = 60
        S_AMAX, S_AE2, S_D, S_C = 64, 128, 192, 256
        S_IAM, S_IAE, S_ID, S_IC = 320, 352, 384, 416

        def sm(c0, n):
            return SMALL[:, c0:c0 + n]

        smi = SMALL[:].bitcast(I32)

        def smi_(c0, n):
            return smi[:, c0:c0 + n]

        scr2i = SCR2[:].bitcast(I32)
        marks = {}

        k.dma("sync", RES[:], x_sl.rearrange("(c p) h -> p c h", p=128), dxx)
        marks["res"] = (dxx, 16)
        k.dma("sync", IDENT[:], id_in, dx)
        k.dma("sync", BQF[:], bq_in.rearrange("(c p) -> p c", p=128), dx, slow=True)
        k.dma("sync", BBK[:], bk_in.unsqueeze(0).broadcast_to([128, 512]), dx)
        k.dma("sync", BBV[:], bv_in.unsqueeze(0).broadcast_to([128, 512]), dx)
        init_mark = (dx, dx.v)
        k.v(lambda e: e.memset(ONESB[:], 1.0), inc=vq)
        k.v(lambda e: e.memset(EPSC[:], EPS_LN), inc=vq)

        def stg(base):
            return dict(
                pb=[IQT[:, base:base + 4096].bitcast(F32),
                    IQT[:, base + 4096:base + 8192].bitcast(F32)],
                ex=IQT[:, base + 8192:base + 12288].bitcast(F32),
                m=IQT[:, base + 12288:base + 16384].bitcast(F32),
            )

        STG = [stg(0), stg(16384),
               dict(pb=[IQT[:, 0:2048].bitcast(F32),
                        IQT[:, 2048:4096].bitcast(F32)],
                    ex=IQT[:, 4096:6144].bitcast(F32),
                    m=IQT[:, 6144:8192].bitcast(F32))]
        Q_BUF = [[QBF[:, 0:2048], X1T[:, 0:2048]],
                 [IQT[:, 16384 + 8192:16384 + 10240],
                  IQT[:, 16384 + 10240:16384 + 12288]],
                 [IQT[:, 8192:9216], IQT[:, 9216:10240]]]
        T_BUF = [[SCR1[:].bitcast(BF16)[:, 0:2048],
                  SCR1[:].bitcast(BF16)[:, 2048:4096]],
                 [IQT[:, 16384 + 12288:16384 + 14336],
                  IQT[:, 16384 + 14336:16384 + 16384]],
                 [IQT[:, 10240:11264], IQT[:, 11264:12288]]]
        pp_idx = [0, 0, 0]
        PCAP = [2048, 2048, 1024]
        SMREG = [(SCR2[:], 0, 64), (SCR2[:], 1024, 64),
                 (SMALL[:], 64, 32)]

        MATS = {
            "g": (wg_sl, IFF, HSL, "w"),
            "u": (wu_sl, IFF, HSL, "w"),
            "d": (wd_sl, D, ISL, "w"),
            "q": (wq_sl, D, HSL, "c"),
            "k": (wk_sl, 512, HSL, "c"),
            "v": (wv_sl, 512, HSL, "c"),
            "o": (wo_sl, D, HSL, "c"),
        }

        def prep_quant_piece(name, p, path):
            src, rows, cols, mode = MATS[name]
            nrc = rows // 128
            prc = min(nrc, PCAP[path] // cols)
            st = STG[path]
            smt, smb, sp = SMREG[path]
            nblk = (prc * cols) // 32
            src_r = src.rearrange("(c p) w -> p c w", p=128)
            gi = pp_idx[path]
            pp_idx[path] += 1
            slot = gi % 2
            pbuf = st["pb"][slot]
            pb_v = pbuf[:, 0:prc * cols].rearrange("p (c w) -> p c w", c=prc)
            lw = [marks[f"pp_qdone_{path}_{slot}"]] \
                if f"pp_qdone_{path}_{slot}" in marks else []
            if path == 2 and gi < 2:
                lw = lw + [marks["prep_p0_done"]]
            ldsem = {0: dwp, 1: dgl, 2: ddl}[path][slot]
            if path != 1:
                k.dma("sync", pb_v, src_r[:, p * prc:(p + 1) * prc, :],
                      ldsem, waits=lw)
                ldm = (ldsem, ldsem.v)
                vf, vs = k.v, vq
            else:
                k.dma("gpsimd", pb_v, src_r[:, p * prc:(p + 1) * prc, :],
                      ldsem, waits=lw)
                ldm = (ldsem, ldsem.v)
                vf, vs = k.g, gq
            tposed = [marks[f"tposed_{path}_{slot}"]] \
                if f"tposed_{path}_{slot}" in marks else []
            if mode == "w":
                pb3 = pbuf[:, 0:prc * cols].rearrange("p (b c) -> p b c", c=32)
                smi2 = smt.bitcast(I32)
                am = smt[:, smb:smb + nblk]
                ae2 = smt[:, smb + sp:smb + sp + nblk]
                bf2 = smt[:, smb + 2 * sp:smb + 2 * sp + nblk]
                c6 = smt[:, smb + 3 * sp:smb + 3 * sp + nblk]
                ng6 = smt[:, smb + 4 * sp:smb + 4 * sp + nblk]
                ami = smi2[:, smb:smb + nblk]
                ae2i = smi2[:, smb + sp:smb + sp + nblk]
                exf = st["ex"][:, 0:prc * cols]
                pbf = pbuf[:, 0:prc * cols]
                mf = st["m"][:, 0:prc * cols]
                assert path != 1, "quantized weights must use a DVE path"
                vf(lambda e, a=am, b=pb3: e.tensor_reduce(
                    a, b, AXX, Alu.max, apply_absolute_value=True),
                    waits=[ldm])
                vf(lambda e, a=ae2i, b=ami: e.tensor_single_scalar(
                    a, b, 0x7F800000, Alu.bitwise_and))
                vf(lambda e, a=exf.bitcast(I32), b=pbf.bitcast(I32):
                   e.tensor_single_scalar(a, b, 0x7F800000, Alu.bitwise_and))
                vf(lambda e, a=ae2: e.tensor_scalar_max(a, a, TINY))
                vf(lambda e, a=bf2, b=ae2: e.tensor_scalar_mul(
                    a, b, 1.5 * 2.0 ** 20))
                vf(lambda e, a=c6, b=ae2: e.tensor_scalar_mul(a, b, 1.5))
                vf(lambda e, a=ng6, b=c6: e.tensor_scalar_mul(a, b, -1.0))
                bf2b = bf2.unsqueeze(2).broadcast_to([128, nblk, 32])
                ex3 = exf.rearrange("p (b c) -> p b c", c=32)
                m3 = mf.rearrange("p (b c) -> p b c", c=32)
                vf(lambda e, a=m3, b=ex3, c=bf2b: e.scalar_tensor_tensor(
                    a, b, MAGIC_W, c, Alu.mult, Alu.max))
                vf(lambda e, a=exf, b=pbf, c=mf: e.tensor_tensor(
                    a, b, c, Alu.add))
                vf(lambda e, a=exf, c=mf: e.tensor_tensor(
                    a, a, c, Alu.subtract))
                c6b = c6.unsqueeze(2).broadcast_to([128, nblk, 32])
                ng6b = ng6.unsqueeze(2).broadcast_to([128, nblk, 32])
                vf(lambda e, a=ex3, c=c6b: e.tensor_tensor(a, a, c, Alu.min))
                pq3 = Q_BUF[path][slot][:, 0:prc * cols].rearrange(
                    "p (b c) -> p b c", c=32)
                vf(lambda e, a=pq3, b=ex3, c=ng6b: e.tensor_tensor(
                    a, b, c, Alu.max), waits=tposed, inc=vs)
            else:
                pq_v = Q_BUF[path][slot][:, 0:prc * cols].rearrange(
                    "p (c w) -> p c w", c=prc)
                vf(lambda e, a=pq_v, b=pb_v: e.tensor_copy(a, b),
                   waits=[ldm] + tposed, inc=vs)
            marks[f"pp_qdone_{path}_{slot}"] = (vs, vs.v)
            return (vs, vs.v)

        pt_idx = [0, 0, 0]

        def prep_tail_piece(name, p, path, q_mark):
            src, rows, cols, mode = MATS[name]
            nrc = rows // 128
            prc = min(nrc, PCAP[path] // cols)
            ncc = cols // 128
            rows_piece = prc * 128
            slot = pt_idx[path] % 2
            pt_idx[path] += 1
            pq_v = Q_BUF[path][slot][:, 0:prc * cols].rearrange(
                "p (c w) -> p c w", c=prc)
            pt_v = T_BUF[path][slot][:, 0:ncc * rows_piece].rearrange(
                "p (c r) -> p c r", c=ncc)
            shipw = [marks[f"ship_{path}_{slot}"]] \
                if f"ship_{path}_{slot}" in marks else []
            for cc in range(ncc):
                for rc0 in range(0, prc, 8):
                    nb = min(8, prc - rc0)
                    b = (cc + rc0 // 8 + p) % 2
                    k.w("tensor", (pt[b].rd, pt[b].rd.v), q_mark, init_mark)
                    for j in range(nb):
                        k.t(lambda e, o=pt[b].ap[:, j * 128:(j + 1) * 128],
                            i=pq_v[:, rc0 + j, cc * 128:(cc + 1) * 128]:
                            e.transpose(o, i, IDENT[:]),
                            inc=(pt[b].wr if j == nb - 1 else None))
                    src3 = pt[b].ap[:, 0:nb * 128].rearrange(
                        "p (c f) -> p c f", c=nb)
                    k.v(lambda e, o=pt_v[:, cc, rc0 * 128:(rc0 + nb) * 128],
                        i=src3: e.tensor_copy(o, i),
                        waits=[(pt[b].wr, pt[b].wr.v)] + shipw,
                        inc=pt[b].rd)
            k.nop("tensor", pq)
            marks[f"tposed_{path}_{slot}"] = (pq, pq.v)
            k.nop("vector", vq)
            evac_mark = (vq, vq.v)
            dst = agin[name].ap().rearrange("(c p) r -> p c r", p=128)[
                :, :, p * rows_piece:(p + 1) * rows_piece]
            k.dma("scalar", dst, pt_v, d_sh[name][slot], waits=[evac_mark])
            marks[f"ship_{path}_{slot}"] = (d_sh[name][slot],
                                            d_sh[name][slot].v)
            marks[f"ship_{name}"] = [(d_sh[name][0], d_sh[name][0].v),
                                     (d_sh[name][1], d_sh[name][1].v)]

        def prep_ag(name):
            k.g(lambda e, a=agin[name], o=agout[name]: e.collective_compute(
                "AllGather", Alu.bypass,
                replica_groups=[[0, 1, 2, 3, 4, 5, 6, 7]],
                ins=[a.ap().opt()], outs=[o.ap().opt()]),
                waits=list(marks[f"ship_{name}"]) + [(ag, AGN[name] - 1)],
                inc=ag)
            marks[f"ag_{name}"] = (ag, AGN[name])

        def npieces(name, path=0):
            src, rows, cols, mode = MATS[name]
            nrc = rows // 128
            prc = min(nrc, PCAP[path] // cols)
            return nrc // prc

        p0_list = [("g", p) for p in range(npieces("g"))] + \
                  [("u", p) for p in range(npieces("u"))]
        p1_list = [("q", p) for p in range(npieces("q"))] + \
                  [("k", p) for p in range(npieces("k"))] + \
                  [("v", p) for p in range(npieces("v"))] + \
                  [("o", p) for p in range(npieces("o"))]
        q_marks = {}
        for i in range(max(len(p0_list), len(p1_list)) + 1):
            for lst, path in ((p0_list, 0), (p1_list, 1)):
                if 0 < i <= len(lst):
                    nm0, pp0 = lst[i - 1]
                    prep_tail_piece(nm0, pp0, path, q_marks[(nm0, pp0)])
                    if pp0 == npieces(nm0) - 1:
                        prep_ag(nm0)
                if i < len(lst):
                    nm, p = lst[i]
                    q_marks[(nm, p)] = prep_quant_piece(nm, p, path)


        d_units = []
        ND = npieces("d", 2)

        def mk_d_unit(p):
            def go():
                if p > 0:
                    prep_tail_piece("d", p - 1, 2, q_marks[("d", p - 1)])
                q_marks[("d", p)] = prep_quant_piece("d", p, 2)
            return go

        def mk_d_last():
            def go():
                prep_tail_piece("d", ND - 1, 2, q_marks[("d", ND - 1)])
                prep_ag("d")
                marks["iqt_lower_free"] = [
                    (pq, pq.v), (ddl[0], ddl[0].v), (ddl[1], ddl[1].v),
                    (d_sh["d"][0], d_sh["d"][0].v),
                    (d_sh["d"][1], d_sh["d"][1].v)]
            return go

        for p in range(ND):
            d_units.append(mk_d_unit(p))
        d_units.append(mk_d_last())

        marks["prep_p0_done"] = (vq, vq.v)
        marks["prep_gq"] = (gq, gq.v)
        marks["iqt_free_list"] = [(pq, pq.v), (gq, gq.v),
                                  (dwp[0], dwp[0].v), (dwp[1], dwp[1].v),
                                  (dgl[0], dgl[0].v), (dgl[1], dgl[1].v)] + \
            [(d_sh[m][j], d_sh[m][j].v) for m in "gudqkvo" for j in (0, 1)]
        marks["qbf_free"] = (pq, pq.v)
        marks["scr1_free"] = (d_sh["g"][0], d_sh["g"][0].v)
        marks["scr1_free2"] = (d_sh["g"][1], d_sh["g"][1].v)
        marks["scr1_free3"] = (d_sh["u"][0], d_sh["u"][0].v)
        marks["scr1_free4"] = (d_sh["u"][1], d_sh["u"][1].v)
        marks["phaseA_done"] = (pq, pq.v)
        marks["gS_free"] = (wf, 0)
        marks["uS_free"] = (wf, 0)

        def ln_stats(chunks):
            res_mark = marks["res"]
            n = len(chunks)
            for i, gc in enumerate(chunks):
                cv = RES[:, gc, :]
                k.v(lambda e, o=sm(S_SUM + i, 1), iv=cv: e.reduce_sum(
                    o, iv, AXX), waits=[res_mark] if i == 0 else [])
                k.a(lambda e, o=SCR1[:], iv=cv, acc=sm(S_SQ + i, 1):
                    e.activation(o, iv, Act.Square, accum_out=acc),
                    waits=([res_mark, marks.get("scr1_free", (vq, 0)),
                            marks.get("scr1_free2", (vq, 0)),
                            marks.get("scr1_free3", (vq, 0)),
                            marks.get("scr1_free4", (vq, 0))]
                           if i == 0 else []),
                    inc=(aq if i == n - 1 else None))
            k.v(lambda e, o=sm(S_NM, n), iv=sm(S_SUM, n): e.tensor_scalar_mul(
                o, iv, -1.0 / D))
            k.v(lambda e, o=sm(S_MSQ, n), iv=sm(S_NM, n): e.tensor_tensor(
                o, iv, iv, Alu.mult))
            k.v(lambda e, o=sm(S_VAR, n), i0=sm(S_SQ, n), i1=sm(S_MSQ, n):
                e.scalar_tensor_tensor(o, i0, 1.0 / D, i1, Alu.mult,
                                       Alu.subtract),
                waits=[(aq, aq.v)], inc=vq)
            k.a(lambda e, o=sm(S_STD, n), iv=sm(S_VAR, n): e.activation(
                o, iv, Act.Sqrt, bias=EPSC[:, 0:1]), waits=[(vq, vq.v)],
                inc=aq)
            k.v(lambda e, o=sm(S_RSTD, n), iv=sm(S_STD, n): e.reciprocal(
                o, iv), waits=[(aq, aq.v)])

        def xquant_chain(xv, nblk, am_c, ae_c, d_c, c_c, qf8v, qbfv,
                         qbf3=None, inc=None):
            x3 = xv.rearrange("p (b c) -> p b c", c=32)
            k.v(lambda e, o=sm(am_c, nblk), iv=x3: e.tensor_reduce(
                o, iv, AXX, Alu.max, apply_absolute_value=True))
            k.v(lambda e, o=smi_(ae_c, nblk), iv=smi_(am_c, nblk):
                e.tensor_single_scalar(o, iv, 0x7F800000, Alu.bitwise_and))
            k.v(lambda e, o=sm(ae_c, nblk): e.tensor_scalar_max(o, o, TINY))
            k.v(lambda e, o=sm(d_c, nblk), iv=sm(ae_c, nblk):
                e.tensor_scalar_mul(o, iv, 16.0))
            k.v(lambda e, o=sm(c_c, nblk), iv=sm(d_c, nblk): e.reciprocal(
                o, iv))
            cb = sm(c_c, nblk).unsqueeze(2).broadcast_to([128, nblk, 32])
            db = sm(d_c, nblk).unsqueeze(2).broadcast_to([128, nblk, 32])
            k.v(lambda e, o=x3, c=cb: e.tensor_tensor(o, o, c, Alu.mult))
            k.v(lambda e, o=xv: e.tensor_scalar(o, o, CLIP_A, -CLIP_A,
                                                Alu.min, Alu.max), inc=vq)
            k.a(lambda e, o=qf8v, iv=xv: e.copy(o, iv),
                waits=[(vq, vq.v)], inc=aq)
            f83 = qf8v.rearrange("p (b c) -> p b c", c=32)
            q3 = qbfv.rearrange("p (b c) -> p b c", c=32)
            k.v(lambda e, o=q3, iv=f83, d=db: e.tensor_tensor(
                o, iv, d, Alu.mult), waits=[(aq, aq.v)], inc=(inc or vq))

        def transpose_row(srcv, nt, dst_fn, bsel, evac_extra=()):
            for j0 in range(0, nt, 8):
                nb = min(8, nt - j0)
                b = (bsel + j0 // 8) % 2
                k.w("tensor", (pt[b].rd, pt[b].rd.v), (vq, vq.v))
                for j in range(nb):
                    k.t(lambda e, o=pt[b].ap[:, j * 128:(j + 1) * 128],
                        i=srcv[:, (j0 + j) * 128:(j0 + j + 1) * 128]:
                        e.transpose(o, i, IDENT[:]),
                        inc=(pt[b].wr if j == nb - 1 else None))
                src3 = pt[b].ap[:, 0:nb * 128].rearrange(
                    "p (c f) -> p c f", c=nb)
                k.v(lambda e, o=dst_fn(j0, nb), i=src3: e.tensor_copy(o, i),
                    waits=[(pt[b].wr, pt[b].wr.v)] + list(evac_extra),
                    inc=pt[b].rd)
            k.nop("vector", vq)

        CMAP = [(ib + 8) % NIB for ib in range(NIB)]

        def ffn_tile(t, sub, side=()):
            side = list(side)
            chunks = [t * TCH + c for c in range(TCH)]
            ln_stats(chunks)
            phA = marks["phaseA_done"]
            for c in range(TCH):
                k.w("vector", marks["qbf_free"])
                k.v(lambda e, o=SCR1[:], iv=RES[:, chunks[c], :],
                    s1=sm(S_NM + c, 1), s2=sm(S_RSTD + c, 1):
                    e.tensor_scalar(o, iv, s1, s2, Alu.add, Alu.mult))
                xquant_chain(SCR1[:], 64, S_AMAX, S_AE2, S_D, S_C,
                             QF8[:], QBF[:])
                ext = [phA] if c == 0 else []
                transpose_row(
                    QBF[:], NHC,
                    lambda j0, nb, c=c: x1t_v[:, j0:j0 + nb,
                                              c * 128:(c + 1) * 128],
                    bsel=c, evac_extra=ext)
                k.nop("tensor", pq)
                marks["qbf_free"] = (pq, pq.v)
            x1t_mark = (vq, vq.v)

            iq_marks = {}

            def emit_inter_transposes(ib):
                first = not marks.get(f"iqt_started_{t}_{sub}")
                for pr in range(2):
                    b = pr
                    qbp = QBF[:, pr * 1024:(pr + 1) * 1024]
                    k.w("tensor", (pt[b].rd, pt[b].rd.v), iq_marks[(ib, pr)])
                    nt = 0
                    for ch in range(4):
                        for sh in range(2):
                            nt += 1
                            k.t(lambda e,
                                o=pt[b].ap[:, (ch * 2 + sh) * 128:
                                           (ch * 2 + sh + 1) * 128],
                                i=qbp[:, sh * 512 + ch * 128:
                                      sh * 512 + (ch + 1) * 128]:
                                e.transpose(o, i, IDENT[:]),
                                inc=(pt[b].wr if nt == 8 else None))
                    src3 = pt[b].ap[:, 0:1024].rearrange(
                        "p (c f) -> p c f", c=4)
                    cm = CMAP[ib]
                    dst = iqt_v[:, 4 * cm:4 * cm + 4,
                                2 * pr * 128:(2 * pr + 2) * 128]
                    ew = [(pt[b].wr, pt[b].wr.v)]
                    if first and pr == 0:
                        ew += marks["iqt_free_list"]
                    if cm == 0 and pr == 0 and "iqt_lower_free" in marks:
                        ew += marks.pop("iqt_lower_free")
                    k.v(lambda e, o=dst, i=src3: e.tensor_copy(o, i),
                        waits=ew, inc=pt[b].rd)
                marks[f"iqt_started_{t}_{sub}"] = True
                k.nop("vector", vq)
                k.nop("tensor", pq)
                marks["qbf_free"] = (pq, pq.v)

            def emit_consume_pair(ib, pr):
                s0 = 2 * pr
                pav = pa_t[:, s0:s0 + 2, :].rearrange("p a b -> p (a b)")
                pbv = pb_t[:].rearrange("p a b -> p (a b)")
                sl1 = SCR1[:, pr * 1024:(pr + 1) * 1024]
                sl2 = SCR2[:, pr * 1024:(pr + 1) * 1024]
                qf = QF8[:, pr * 1024:(pr + 1) * 1024]
                qb = QBF[:, pr * 1024:(pr + 1) * 1024]
                k.a(lambda e, o=sl1, iv=pav: e.activation(
                    o, iv, Act.Silu),
                    waits=[(pa[s0].wr, pa[s0].wr.v),
                           (pa[s0 + 1].wr, pa[s0 + 1].wr.v),
                           marks.get(f"sl1_free_{pr}", (vq, 0))],
                    inc=pa[s0].rd)
                k.nop("scalar", pa[s0 + 1].rd)
                k.nop("scalar", aq)
                silu_mark = (aq, aq.v)
                ivw = [silu_mark, (pb[0].wr, pb[0].wr.v),
                       (pb[1].wr, pb[1].wr.v)]
                if ib == 0:
                    ivw.append(marks["qbf_free"])
                    if t == 0 and sub == 1:
                        ivw.append(marks["prep_gq"])
                k.v(lambda e, o=sl2, i0=sl1, i1=pbv: e.tensor_tensor(
                    o, i0, i1, Alu.mult), waits=ivw, inc=pb[0].rd)
                k.nop("vector", pb[1].rd)
                k.nop("vector", vq)
                marks[f"sl1_free_{pr}"] = (vq, vq.v)
                xquant_chain(sl2, 32, S_IAM, S_IAE, S_ID, S_IC, qf, qb)
                iq_marks[(ib, pr)] = (vq, vq.v)

            for ib in range(NIB):
                k.dma("sync", gS_v, wT_r["g"][:, :, ib * IB:(ib + 1) * IB],
                      d_gS, waits=[marks["gS_free"], marks["ag_g"]])
                g_ld = (d_gS, d_gS.v)
                k.dma("sync", uS_v, wT_r["u"][:, :, ib * IB:(ib + 1) * IB],
                      d_uS, waits=[marks["uS_free"], marks["ag_u"]])
                u_ld = (d_uS, d_uS.v)
                for s in range(TCH):
                    bk_ = pa[s]
                    k.w("tensor", (bk_.rd, bk_.rd.v), g_ld, x1t_mark)
                    for h in range(NHC):
                        k.t(lambda e, o=bk_.ap,
                            l=x1t_v[:, h, s * 128:(s + 1) * 128],
                            r=gS_v[:, h, :], st=(h == 0), sp=(h == NHC - 1):
                            e.matmul(o, l, r, start=st, stop=sp),
                            inc=(bk_.wr if h == NHC - 1 else None))
                k.nop("tensor", wf)
                marks["gS_free"] = (wf, wf.v)
                if ib > 0:
                    emit_inter_transposes(ib - 1)
                for _ in range(2):
                    if side:
                        side.pop(0)()
                if ib == NIB - 1:
                    while side:
                        side.pop(0)()
                for s in range(TCH):
                    bk_ = pb[s % 2]
                    k.w("tensor", (bk_.rd, bk_.rd.v), u_ld)
                    for h in range(NHC):
                        k.t(lambda e, o=bk_.ap,
                            l=x1t_v[:, h, s * 128:(s + 1) * 128],
                            r=uS_v[:, h, :], st=(h == 0), sp=(h == NHC - 1):
                            e.matmul(o, l, r, start=st, stop=sp),
                            inc=(bk_.wr if h == NHC - 1 else None))
                    if s % 2 == 1:
                        emit_consume_pair(ib, s // 2)
                k.nop("tensor", wf)
                marks["uS_free"] = (wf, wf.v)

            emit_inter_transposes(NIB - 1)
            k.nop("tensor", pq)
            marks["phaseA_done"] = (pq, pq.v)
            iqt_mark = (vq, vq.v)

            for hg in range(4):
                ldm = {}
                for pc in range(2):
                    sl = pc % 2
                    wl = [marks["gS_free"] if hg == 0 else
                          marks[f"wd_free_{sl}"], marks["ag_d"]]
                    k.dma("sync", wdS_v[sl],
                          wT_r["d"][:, pc * 8:(pc + 1) * 8,
                                    hg * 512:(hg + 1) * 512], d_wd[sl],
                          waits=wl)
                    ldm[pc] = (d_wd[sl], d_wd[sl].v)
                for pc in range(8):
                    sl = pc % 2
                    k.w("tensor", ldm[pc])
                    if pc == 0:
                        k.w("tensor", iqt_mark,
                            *[(pa[s].rd, pa[s].rd.v) for s in range(TCH)])
                    for s in range(TCH):
                        bk_ = pa[s]
                        for ic in range(8):
                            st = (pc == 0 and ic == 0)
                            sp = (pc == 7 and ic == 7)
                            lc = pc * 8 + ic
                            pchunk = 4 * CMAP[lc // 4] + lc % 4
                            k.t(lambda e, o=bk_.ap,
                                l=iqt_v[:, pchunk,
                                        s * 128:(s + 1) * 128],
                                r=wdS_v[sl][:, ic, :], st=st, sp=sp:
                                e.matmul(o, l, r, start=st, stop=sp),
                                inc=(bk_.wr if sp else None))
                    k.nop("tensor", wf)
                    marks[f"wd_free_{sl}"] = (wf, wf.v)
                    if pc + 2 < 8:
                        k.dma("sync", wdS_v[(pc + 2) % 2],
                              wT_r["d"][:, (pc + 2) * 8:(pc + 3) * 8,
                                        hg * 512:(hg + 1) * 512],
                              d_wd[(pc + 2) % 2],
                              waits=[marks[f"wd_free_{sl}"]])
                        ldm[pc + 2] = (d_wd[(pc + 2) % 2],
                                       d_wd[(pc + 2) % 2].v)
                for pr in range(2):
                    s0 = 2 * pr
                    pav3 = pa_t[:, s0:s0 + 2, :]
                    rv = RES[:, chunks[s0]:chunks[s0] + 2,
                             hg * 512:(hg + 1) * 512]
                    k.v(lambda e, o=rv, i0=pav3, i1=rv:
                        e.scalar_tensor_tensor(o, i0, 0.5, i1, Alu.mult,
                                               Alu.add),
                        waits=[(pa[s0].wr, pa[s0].wr.v),
                               (pa[s0 + 1].wr, pa[s0 + 1].wr.v)],
                        inc=pa[s0].rd)
                    k.nop("vector", pa[s0 + 1].rd)
            k.nop("vector", vq)
            marks["res"] = (vq, vq.v)
            k.nop("tensor", pq)
            marks["iqt_free_list"] = [(pq, pq.v)]
            marks["gS_free"] = (wf, wf.v)
            marks["scr1_free"] = (vq, vq.v)

        def attention():
            ln_stats(list(range(NCH)))
            for c in range(NCH):
                ew = [marks["qbf_free"]]
                k.w("vector", marks["qbf_free"])
                if c == 0:
                    k.w("vector", marks["phaseA_done"])
                k.v(lambda e, o=QBF[:], iv=RES[:, c, :], s1=sm(S_NM + c, 1),
                    s2=sm(S_RSTD + c, 1): e.tensor_scalar(
                        o, iv, s1, s2, Alu.add, Alu.mult), inc=vq)
                ext = marks["iqt_free_list"] if c == 0 else []
                transpose_row(
                    QBF[:], NHC,
                    lambda j0, nb, c=c: x2t_v[:, j0:j0 + nb,
                                              c * 128:(c + 1) * 128],
                    bsel=c, evac_extra=ext)
                k.nop("tensor", pq)
                marks["qbf_free"] = (pq, pq.v)
            x2t_mark = (vq, vq.v)
            k.nop("tensor", pq)
            x2t_pe = (pq, pq.v)

            k.dma("sync", kS_v, wT_r["k"][:], d_kS,
                  waits=[(wf, wf.v), marks["ag_k"]])
            k_ld = (d_kS, d_kS.v)
            k.dma("sync", vS_v, wT_r["v"][:], d_vS, waits=[marks["ag_v"]])
            v_ld = (d_vS, d_vS.v)

            for which, wv_, ld, outv in (("k", kS_v, k_ld, sk_v),
                                         ("v", vS_v, v_ld, vv_v)):
                for s in range(NCH):
                    bk_ = pa[s % 4]
                    k.w("tensor", (bk_.rd, bk_.rd.v), ld)
                    if which == "k" and s == 0:
                        k.w("tensor", x2t_mark)
                    for h in range(NHC):
                        k.t(lambda e, o=bk_.ap,
                            l=x2t_v[:, h, s * 128:(s + 1) * 128],
                            r=wv_[:, h, :], st=(h == 0), sp=(h == NHC - 1):
                            e.matmul(o, l, r, start=st, stop=sp),
                            inc=(bk_.wr if h == NHC - 1 else None))
                    if which == "k":
                        sl1 = SCR1[:, (s % 4) * 512:(s % 4 + 1) * 512]
                        sl2 = SCR2[:, (s % 4) * 512:(s % 4 + 1) * 512]
                        w0 = [(bk_.wr, bk_.wr.v),
                              marks.get(f"asl_{s % 4}", (vq, 0))]
                        if s == 0:
                            w0.append(marks["phaseA_done"])
                        k.v(lambda e, o=sl1, i0=bk_.ap, i1=BBK[:]:
                            e.tensor_tensor(o, i0, i1, Alu.add),
                            waits=w0, inc=bk_.rd)
                        k.v(lambda e, o=sl2, iv=sl1: e.tensor_scalar_min(
                            o, iv, 0.0), inc=vq)
                        k.a(lambda e, o=sl2, iv=sl2: e.activation(
                            o, iv, Act.Exp), waits=[(vq, vq.v)], inc=aq)
                        k.v(lambda e, o=outv[:, s, :], i0=sl1, i1=sl2:
                            e.scalar_tensor_tensor(o, i0, 0.0, i1, Alu.max,
                                                   Alu.add),
                            waits=[(aq, aq.v)], inc=vq)
                        marks[f"asl_{s % 4}"] = (vq, vq.v)
                    else:
                        k.v(lambda e, o=outv[:, s, :], i0=bk_.ap, i1=BBV[:]:
                            e.tensor_tensor(o, i0, i1, Alu.add),
                            waits=[(bk_.wr, bk_.wr.v)], inc=bk_.rd)
                k.nop("tensor", wf)
            k.nop("vector", vq)
            skv_mark = (vq, vq.v)

            k.v(lambda e: e.memset(SCR1[:, 0:528], 0.0),
                waits=[marks.get("asl_0", (vq, 0))], inc=vq)
            z_mark = (vq, vq.v)
            for g_ in range(KVH):
                bk_ = pb[g_ % 2]
                k.w("tensor", (bk_.rd, bk_.rd.v), skv_mark)
                for s in range(NCH):
                    k.t(lambda e, o=bk_.ap[:, 0:128],
                        l=sk_v[:, s, g_ * 128:(g_ + 1) * 128],
                        r=vv_v[:, s, g_ * 128:(g_ + 1) * 128], st=(s == 0):
                        e.matmul(o, l, r, start=st, stop=False))
                for s in range(NCH):
                    k.t(lambda e, o=bk_.ap[:, 128:129],
                        l=sk_v[:, s, g_ * 128:(g_ + 1) * 128],
                        r=ONESB[:, 0:1], sp=(s == NCH - 1):
                        e.matmul(o, l, r, start=False, stop=sp),
                        inc=(bk_.wr if s == NCH - 1 else None))
                k.v(lambda e, o=SCR1[:, g_ * 132:g_ * 132 + 129],
                    i=bk_.ap[:, 0:129]: e.tensor_copy(o, i),
                    waits=[(bk_.wr, bk_.wr.v), z_mark], inc=bk_.rd)
            k.nop("vector", vq)
            kvp_mark = (vq, vq.v)
            k.nop("tensor", pq)
            kvmm_pe = (pq, pq.v)
            marks["phaseA_done"] = kvmm_pe
            k.dma("gpsimd",
                  kvin.ap().rearrange("(g p) c -> p g c", p=128),
                  SCR1[:, 0:528].rearrange("p (g c) -> p g c", c=132),
                  d_kvs, waits=[kvp_mark])
            kv_ship = (d_kvs, d_kvs.v)
            k.g(lambda e: e.collective_compute(
                "AllGather", Alu.bypass,
                replica_groups=[[0, 1, 2, 3], [4, 5, 6, 7]],
                ins=[kvin.ap().opt()], outs=[kvout.ap().opt()]),
                waits=[kv_ship, (ag, AGN["kv"] - 1)], inc=ag)
            kv_ag = (ag, ag.v)

            qo_ld = {}
            for dqb in range(2):
                k.dma("sync", qoS_v[dqb % 2],
                      wT_r["q"][:, :, dqb * 512:(dqb + 1) * 512],
                      d_qo[dqb % 2], waits=[(wf, wf.v), marks["ag_q"]])
                qo_ld[dqb] = (d_qo[dqb % 2], d_qo[dqb % 2].v)
            for dqb in range(4):
                sl = dqb % 2
                for c2 in range(4):
                    dq = dqb * 4 + c2
                    for half in range(2):
                        bk_ = pa[(2 * dq + half) % 4]
                        k.w("tensor", (bk_.rd, bk_.rd.v), qo_ld[dqb])
                        for h in range(NHC):
                            k.t(lambda e, o=bk_.ap,
                                l=qoS_v[sl][:, h, c2 * 128:(c2 + 1) * 128],
                                r=x2t_v[:, h, half * 512:(half + 1) * 512],
                                st=(h == 0), sp=(h == NHC - 1):
                                e.matmul(o, l, r, start=st, stop=sp),
                                inc=(bk_.wr if h == NHC - 1 else None))
                        j = (2 * dq + half) % 4
                        sl1 = SCR1[:, j * 512:(j + 1) * 512]
                        sl2 = SCR2[:, j * 512:(j + 1) * 512]
                        w0 = [(bk_.wr, bk_.wr.v),
                              marks.get(f"asl_{j}", (vq, 0))]
                        if dqb == 0 and c2 == 0 and half == 0:
                            w0.append(kv_ship)
                        k.v(lambda e, o=sl1, iv=bk_.ap,
                            s1=BQF[:, dq:dq + 1]: e.tensor_scalar_add(
                                o, iv, s1), waits=w0, inc=bk_.rd)
                        k.v(lambda e, o=sl2, iv=sl1: e.tensor_scalar_min(
                            o, iv, 0.0), inc=vq)
                        k.a(lambda e, o=sl2, iv=sl2: e.activation(
                            o, iv, Act.Exp), waits=[(vq, vq.v)], inc=aq)
                        k.v(lambda e,
                            o=sqt_v[:, dq, half * 512:(half + 1) * 512],
                            i0=sl1, i1=sl2: e.scalar_tensor_tensor(
                                o, i0, 0.0, i1, Alu.max, Alu.add),
                            waits=[(aq, aq.v)], inc=vq)
                        marks[f"asl_{j}"] = (vq, vq.v)
                k.nop("tensor", wf)
                marks[f"qo_free_{sl}"] = (wf, wf.v)
                if dqb + 2 < 4:
                    k.dma("sync", qoS_v[sl],
                          wT_r["q"][:, :, (dqb + 2) * 512:(dqb + 3) * 512],
                          d_qo[sl], waits=[marks[f"qo_free_{sl}"]])
                    qo_ld[dqb + 2] = (d_qo[sl], d_qo[sl].v)
            sqt_mark = (vq, vq.v)

            k.dma("gpsimd", kvs1,
                  kvout.ap().rearrange("(b p) c -> p b c", p=128)[:, 0:8, :],
                  dgp, waits=[kv_ag, kvmm_pe])
            k.dma("gpsimd", kvs2,
                  kvout.ap().rearrange("(b p) c -> p b c", p=128)[:, 8:16, :],
                  dgp)
            kvg_ld = (dgp, dgp.v)
            k.v(lambda e: e.tensor_tensor(kvs1[:, 0:4, :], kvs1[:, 0:4, :],
                                          kvs1[:, 4:8, :], Alu.add),
                waits=[kvg_ld])
            k.v(lambda e: e.tensor_tensor(kvs2[:, 0:4, :], kvs2[:, 0:4, :],
                                          kvs2[:, 4:8, :], Alu.add))
            k.v(lambda e: e.tensor_tensor(kvs1[:, 0:4, :], kvs1[:, 0:4, :],
                                          kvs2[:, 0:4, :], Alu.add))
            k.v(lambda e: e.tensor_copy(kvb_v[:], kvs1[:, 0:4, :]), inc=vq)
            kvb_mark = (vq, vq.v)

            k.nop("tensor", pq)
            qproj_pe = (pq, pq.v)
            for s in range(NCH):
                for h in range(NH):
                    bk_ = pb[h % 2]
                    g_ = h // 4
                    w0 = [(bk_.rd, bk_.rd.v)]
                    if s == 0 and h == 0:
                        w0 += [kvb_mark, sqt_mark]
                    k.w("tensor", *w0)
                    k.t(lambda e, o=bk_.ap[:, 0:132],
                        l=sqt_v[:, h, s * 128:(s + 1) * 128],
                        r=kvb_v[:, g_, :]: e.matmul(o, l, r, start=True,
                                                    stop=True), inc=bk_.wr)
                    w1 = [(bk_.wr, bk_.wr.v)]
                    if h == 0:
                        w1.append(marks["qbf_free"])
                    k.v(lambda e, o=sm(S_NRM, 1), iv=bk_.ap[:, 128:129]:
                        e.tensor_scalar_add(o, iv, EPS_AT), waits=w1)
                    k.v(lambda e, o=sm(S_NRM, 1), iv=sm(S_NRM, 1):
                        e.reciprocal(o, iv))
                    k.v(lambda e, o=QBF[:, h * 128:(h + 1) * 128],
                        iv=bk_.ap[:, 0:128], s1=sm(S_NRM, 1):
                        e.tensor_scalar_mul(o, iv, s1), inc=bk_.rd)
                k.nop("vector", vq)
                ext = [qproj_pe] if s == 0 else []
                transpose_row(
                    QBF[:], NH,
                    lambda j0, nb, s=s: atnT_v[:, j0:j0 + nb,
                                               s * 128:(s + 1) * 128],
                    bsel=s, evac_extra=ext)
                k.nop("tensor", pq)
                marks["qbf_free"] = (pq, pq.v)
            atn_mark = (vq, vq.v)

            ldo = {}
            for hob in range(2):
                k.dma("sync", qoS_v[hob % 2],
                      wT_r["o"][:, :, hob * 512:(hob + 1) * 512],
                      d_qo[hob % 2],
                      waits=[marks[f"qo_free_{hob % 2}"], marks["ag_o"]])
                ldo[hob] = (d_qo[hob % 2], d_qo[hob % 2].v)
            for hob in range(4):
                sl = hob % 2
                for s in range(NCH):
                    bk_ = pa[s % 4]
                    w0 = [(bk_.rd, bk_.rd.v), ldo[hob]]
                    if hob == 0 and s == 0:
                        w0.append(atn_mark)
                    k.w("tensor", *w0)
                    for h in range(NHC):
                        k.t(lambda e, o=bk_.ap,
                            l=atnT_v[:, h, s * 128:(s + 1) * 128],
                            r=qoS_v[sl][:, h, :], st=(h == 0),
                            sp=(h == NHC - 1):
                            e.matmul(o, l, r, start=st, stop=sp),
                            inc=(bk_.wr if h == NHC - 1 else None))
                    rv = RES[:, s, hob * 512:(hob + 1) * 512]
                    k.v(lambda e, o=rv, i0=bk_.ap, i1=rv: e.tensor_tensor(
                        o, i0, i1, Alu.add),
                        waits=[(bk_.wr, bk_.wr.v)], inc=bk_.rd)
                k.nop("tensor", wf)
                marks[f"qo_free_{sl}"] = (wf, wf.v)
                if hob + 2 < 4:
                    k.dma("sync", qoS_v[sl],
                          wT_r["o"][:, :, (hob + 2) * 512:(hob + 3) * 512],
                          d_qo[sl], waits=[marks[f"qo_free_{sl}"]])
                    ldo[hob + 2] = (d_qo[sl], d_qo[sl].v)
            k.nop("vector", vq)
            marks["res"] = (vq, vq.v)
            marks["scr1_free"] = (vq, vq.v)
            k.nop("tensor", pq)
            marks["iqt_free_list"] = [(pq, pq.v)]
            marks["gS_free"] = marks["qo_free_0"]
            marks["uS_free"] = marks["qo_free_1"]

        ffn_tile(0, 1, d_units)
        ffn_tile(1, 1)
        attention()
        ffn_tile(0, 3)
        ffn_tile(1, 3)

        k.dma("gpsimd", out.rearrange("(c p) h -> p c h", p=128), RES[:],
              dgp, waits=[marks["res"]])
        k.w("gpsimd", (dgp, dgp.v))

        with nc.Block() as block:
            @block.sync
            def _(e):
                for fn in k.q["sync"]:
                    fn(e)

            @block.scalar
            def _(e):
                for fn in k.q["scalar"]:
                    fn(e)

            @block.vector
            def _(e):
                for fn in k.q["vector"]:
                    fn(e)

            @block.gpsimd
            def _(e):
                for fn in k.q["gpsimd"]:
                    fn(e)

            @block.tensor
            def _(e):
                for fn in k.q["tensor"]:
                    fn(e)
    return nc


_NC_CACHE = None


def kernel(L0_attn_hidden_states, L0_attn_Wq, L0_attn_bq, L0_attn_Wk,
           L0_attn_bk, L0_attn_Wv, L0_attn_bv, L0_attn_Wo, L0_ffn_w_gate,
           L0_ffn_w_up, L0_ffn_w_down):
    global _NC_CACHE
    x = np.ascontiguousarray(np.asarray(L0_attn_hidden_states, np.float32))
    Wq = np.asarray(L0_attn_Wq, np.float32)
    Wk = np.asarray(L0_attn_Wk, np.float32)
    Wv = np.asarray(L0_attn_Wv, np.float32)
    Wo = np.asarray(L0_attn_Wo, np.float32)
    Wg = np.asarray(L0_ffn_w_gate, np.float32)
    Wu = np.asarray(L0_ffn_w_up, np.float32)
    Wd = np.asarray(L0_ffn_w_down, np.float32)
    bq = np.asarray(L0_attn_bq, np.float32)
    bk = np.asarray(L0_attn_bk, np.float32)
    bv = np.asarray(L0_attn_bv, np.float32)
    ident = np.eye(128, dtype=ml_dtypes.bfloat16)

    in_maps = []
    for c in range(8):
        b, tc = divmod(c, 4)
        in_maps.append({
            "x_sl": np.ascontiguousarray(x[b, tc * NTOK:(tc + 1) * NTOK, :]),
            "wg_sl": np.ascontiguousarray(Wg[:, c * HSL:(c + 1) * HSL]),
            "wu_sl": np.ascontiguousarray(Wu[:, c * HSL:(c + 1) * HSL]),
            "wd_sl": np.ascontiguousarray(Wd[:, c * ISL:(c + 1) * ISL]),
            "wq_sl": np.ascontiguousarray(Wq[:, c * HSL:(c + 1) * HSL]),
            "wk_sl": np.ascontiguousarray(Wk[:, c * HSL:(c + 1) * HSL]),
            "wv_sl": np.ascontiguousarray(Wv[:, c * HSL:(c + 1) * HSL]),
            "wo_sl": np.ascontiguousarray(Wo[:, c * HSL:(c + 1) * HSL]),
            "bq_in": bq, "bk_in": bk, "bv_in": bv, "id_in": ident,
        })
    if _NC_CACHE is None:
        _NC_CACHE = build_nc()
    res = run_bass_kernel_spmd(_NC_CACHE, in_maps, core_ids=list(range(8)))
    outf = np.empty((2, 4096, D), np.float32)
    for c in range(8):
        b, tc = divmod(c, 4)
        outf[b, tc * NTOK:(tc + 1) * NTOK, :] = res.results[c]["out"]
    return outf


# version 09
# speedup vs baseline: 1.0663x; 1.0663x over previous
import numpy as np
import ml_dtypes
from contextlib import ExitStack

import concourse.bass as bass
import concourse.mybir as mybir
from concourse.bass_utils import run_bass_kernel_spmd

F32 = mybir.dt.float32
BF16 = mybir.dt.bfloat16
FP8 = mybir.dt.float8e4
I32 = mybir.dt.int32
Alu = mybir.AluOpType
Act = mybir.ActivationFunctionType
AXX = mybir.AxisListType.X

D = 2048
IFF = 8192
NTOK = 1024
NCH = 8
TCH = 4
NHC = 16
NIC = 64
IB = 512
NIB = IFF // IB
HSL = D // 8
ISL = IFF // 8
NH, KVH, HD = 16, 4, 128
EPS_LN = 1e-5
EPS_AT = 1e-6
CLIP_A = 15.0 / 128.0
MAGIC_W = 1.5 * 2.0 ** 22
TINY = 2.0 ** -100
AGN = {"q": 1, "k": 2, "v": 3, "o": 4, "g": 5, "u": 6, "d": 7, "kv": 8}

ENGS = ("sync", "scalar", "vector", "gpsimd", "tensor")


class Sem:
    __slots__ = ("h", "v", "name")

    def __init__(self, h, name):
        self.h = h
        self.v = 0
        self.name = name

    def inc(self, n=1):
        self.v += n
        return self.v


class Bank:
    def __init__(self, ap, wr, rd):
        self.ap = ap
        self.wr = wr
        self.rd = rd


class K:
    def __init__(self, nc):
        self.nc = nc
        self.q = {e: [] for e in ENGS}
        self._es = ExitStack()

    def sem(self, name):
        return Sem(self._es.enter_context(self.nc.semaphore(name)), name)

    def sb(self, name, shape, dt):
        return self._es.enter_context(self.nc.sbuf_tensor(name, shape, dt))

    def ps(self, name, shape, dt):
        return self._es.enter_context(self.nc.psum_tensor(name, shape, dt))

    def o(self, eng, fn, waits=(), inc=None, n=1, drain=False):
        wl = tuple((s.h, int(v)) for (s, v) in waits if v > 0)
        if inc is not None:
            inc.inc(n)
            semh = inc.h

            def run(e, fn=fn, wl=wl, semh=semh, n=n, drain=drain):
                for h, v in wl:
                    e.wait_ge(h, v)
                fn(e).then_inc(semh, n)
                if drain:
                    e.drain()
        else:
            def run(e, fn=fn, wl=wl, drain=drain):
                for h, v in wl:
                    e.wait_ge(h, v)
                fn(e)
                if drain:
                    e.drain()
        self.q[eng].append(run)

    def w(self, eng, *waits):
        wl = tuple((s.h, int(v)) for (s, v) in waits if v > 0)
        if not wl:
            return

        def run(e, wl=wl):
            for h, v in wl:
                e.wait_ge(h, v)
        self.q[eng].append(run)

    def nop(self, eng, inc, n=1, waits=()):
        self.q[eng].append(lambda e: e.drain())
        self.o(eng, lambda e: e.nop(), waits=waits, inc=inc, n=n)

    def v(self, fn, waits=(), inc=None, n=1):
        self.o("vector", fn, waits=waits, inc=inc, n=n, drain=True)

    def a(self, fn, waits=(), inc=None, n=1):
        self.o("scalar", fn, waits=waits, inc=inc, n=n, drain=True)

    def g(self, fn, waits=(), inc=None, n=1):
        self.o("gpsimd", fn, waits=waits, inc=inc, n=n, drain=True)

    def t(self, fn, waits=(), inc=None, n=1):
        self.o("tensor", fn, waits=waits, inc=inc, n=n)

    def dma(self, eng, out, in_, sem, waits=(), slow=False):
        if slow:
            self.o(eng, lambda e, o=out, i=in_: e.dma_start(
                out=o, in_=i, allow_slow_non_contiguous=True),
                waits=waits, inc=sem, n=16)
        else:
            self.o(eng, lambda e, o=out, i=in_: e.dma_start(out=o, in_=i),
                   waits=waits, inc=sem, n=16)


def build_nc():
    nc = bass.Bass("TRN2", target_bir_lowering=False, debug=False)

    x_sl = nc.dram_tensor("x_sl", [NTOK, D], F32, kind="ExternalInput").ap()
    wg_sl = nc.dram_tensor("wg_sl", [IFF, HSL], F32, kind="ExternalInput").ap()
    wu_sl = nc.dram_tensor("wu_sl", [IFF, HSL], F32, kind="ExternalInput").ap()
    wd_sl = nc.dram_tensor("wd_sl", [D, ISL], F32, kind="ExternalInput").ap()
    wq_sl = nc.dram_tensor("wq_sl", [D, HSL], F32, kind="ExternalInput").ap()
    wk_sl = nc.dram_tensor("wk_sl", [512, HSL], F32, kind="ExternalInput").ap()
    wv_sl = nc.dram_tensor("wv_sl", [512, HSL], F32, kind="ExternalInput").ap()
    wo_sl = nc.dram_tensor("wo_sl", [D, HSL], F32, kind="ExternalInput").ap()
    bq_in = nc.dram_tensor("bq_in", [D], F32, kind="ExternalInput").ap()
    bk_in = nc.dram_tensor("bk_in", [512], F32, kind="ExternalInput").ap()
    bv_in = nc.dram_tensor("bv_in", [512], F32, kind="ExternalInput").ap()
    id_in = nc.dram_tensor("id_in", [128, 128], BF16, kind="ExternalInput").ap()
    out = nc.dram_tensor("out", [NTOK, D], F32, kind="ExternalOutput").ap()

    agin = {
        "g": nc.dram_tensor("agin_g", [HSL, IFF], BF16),
        "u": nc.dram_tensor("agin_u", [HSL, IFF], BF16),
        "d": nc.dram_tensor("agin_d", [ISL, D], BF16),
        "q": nc.dram_tensor("agin_q", [HSL, D], BF16),
        "k": nc.dram_tensor("agin_k", [HSL, 512], BF16),
        "v": nc.dram_tensor("agin_v", [HSL, 512], BF16),
        "o": nc.dram_tensor("agin_o", [HSL, D], BF16),
    }
    from concourse.replica_groups import maybe_share_collective_output_space
    _sp8 = maybe_share_collective_output_space(
        "AllGather", [[0, 1, 2, 3, 4, 5, 6, 7]])
    _sp4 = maybe_share_collective_output_space(
        "AllGather", [[0, 1, 2, 3], [4, 5, 6, 7]])
    agout = {
        "g": nc.dram_tensor("wgT", [D, IFF], BF16, addr_space=_sp8),
        "u": nc.dram_tensor("wuT", [D, IFF], BF16, addr_space=_sp8),
        "d": nc.dram_tensor("wdT", [IFF, D], BF16, addr_space=_sp8),
        "q": nc.dram_tensor("wqT", [D, D], BF16, addr_space=_sp8),
        "k": nc.dram_tensor("wkT", [D, 512], BF16, addr_space=_sp8),
        "v": nc.dram_tensor("wvT", [D, 512], BF16, addr_space=_sp8),
        "o": nc.dram_tensor("woT", [D, D], BF16, addr_space=_sp8),
    }
    kvin = nc.dram_tensor("kvin", [4 * 128, 132], F32)
    kvout = nc.dram_tensor("kvout", [16 * 128, 132], F32, addr_space=_sp4)

    k = K(nc)
    with k._es:
        RES = k.sb("RES", [128, NCH, D], F32)
        X1T = k.sb("X1T", [128, 8192], BF16)
        IQT = k.sb("IQT", [128, 32768], BF16)
        WBUF = k.sb("WBUF", [128, 16384], BF16)
        SCR1 = k.sb("SCR1", [128, D], F32)
        SCR2 = k.sb("SCR2", [128, D], F32)
        QF8 = k.sb("QF8", [128, D], FP8)
        QBF = k.sb("QBF", [128, D], BF16)
        SMALL = k.sb("SMALL", [128, 512], F32)
        IDENT = k.sb("IDENT", [128, 128], BF16)
        BBK = k.sb("BBK", [128, 512], F32)
        BBV = k.sb("BBV", [128, 512], F32)
        BQF = k.sb("BQF", [128, 16], F32)
        KVB = k.sb("KVB", [128, 4 * 132], BF16)
        ONESB = k.sb("ONESB", [128, 8], BF16)
        EPSC = k.sb("EPSC", [128, 2], F32)

        pa_t = k.ps("pa", [128, 4, 512], F32)
        pb_t = k.ps("pb", [128, 2, 512], F32)
        pt_t = k.ps("pt", [128, 2, 1024], BF16)

        dx = k.sem("dx")
        dxx = k.sem("dxx")
        dwp = [k.sem("dwp0"), k.sem("dwp1")]
        dgl = [k.sem("dgl0"), k.sem("dgl1")]
        ddl = [k.sem("ddl0"), k.sem("ddl1")]
        d_gS = k.sem("d_gS")
        d_uS = k.sem("d_uS")
        d_wd = [k.sem("d_wd0"), k.sem("d_wd1")]
        d_kS = k.sem("d_kS")
        d_vS = k.sem("d_vS")
        d_qo = [k.sem("d_qo0"), k.sem("d_qo1")]
        d_sh = {m: [k.sem(f"d_sh_{m}0"), k.sem(f"d_sh_{m}1")]
                for m in "gudqkvo"}
        d_kvs = k.sem("d_kvs")
        dgp = k.sem("dgp")
        ag = k.sem("ag")
        vq = k.sem("vq")
        gq = k.sem("gq")
        aq = k.sem("aq")
        pq = k.sem("pq")
        wf = k.sem("wf")
        paw = [k.sem(f"paw{i}") for i in range(4)]
        par = [k.sem(f"par{i}") for i in range(4)]
        pbw = [k.sem(f"pbw{i}") for i in range(2)]
        pbr = [k.sem(f"pbr{i}") for i in range(2)]
        ptw = [k.sem(f"ptw{i}") for i in range(2)]
        ptr = [k.sem(f"ptr{i}") for i in range(2)]
        pa = [Bank(pa_t[:, i, :], paw[i], par[i]) for i in range(4)]
        pb = [Bank(pb_t[:, i, :], pbw[i], pbr[i]) for i in range(2)]
        pt = [Bank(pt_t[:, i, :], ptw[i], ptr[i]) for i in range(2)]

        x1t_v = X1T[:].rearrange("p (c f) -> p c f", c=NHC)
        iqt_v = IQT[:].rearrange("p (c f) -> p c f", c=NIC)
        sk_v = X1T[:, 0:4096].rearrange("p (c f) -> p c f", c=8)
        vv_v = X1T[:, 4096:8192].rearrange("p (c f) -> p c f", c=8)
        kvs1 = X1T[:, 0:2112].bitcast(F32).rearrange("p (c f) -> p c f", c=8)
        kvs2 = X1T[:, 2112:4224].bitcast(F32).rearrange("p (c f) -> p c f", c=8)
        x2t_v = IQT[:, 0:16384].rearrange("p (c f) -> p c f", c=NHC)
        sqt_v = IQT[:, 16384:32768].rearrange("p (c f) -> p c f", c=NHC)
        atnT_v = x2t_v
        kvb_v = KVB[:].rearrange("p (c f) -> p c f", c=4)

        def wslot(off, nch, wid):
            return WBUF[:, off:off + nch * wid].rearrange(
                "p (c f) -> p c f", c=nch)

        gS_v = wslot(0, NHC, IB)
        uS_v = wslot(8192, NHC, IB)
        wdS_v = [wslot(0, 8, 512), wslot(4096, 8, 512)]
        kS_v = wslot(0, NHC, 512)
        vS_v = wslot(8192, NHC, 512)
        qoS_v = [wslot(0, NHC, 512), wslot(8192, NHC, 512)]

        wT_r = {m: agout[m].ap().rearrange("(c p) w -> p c w", p=128)
                for m in agout}

        S_SUM, S_SQ, S_NM, S_MSQ, S_VAR, S_STD, S_RSTD = 0, 8, 16, 24, 32, 40, 48
        S_NRM = 60
        S_AMAX, S_AE2, S_D, S_C = 64, 128, 192, 256
        S_IAM, S_IAE, S_ID, S_IC = 320, 352, 384, 416

        def sm(c0, n):
            return SMALL[:, c0:c0 + n]

        smi = SMALL[:].bitcast(I32)

        def smi_(c0, n):
            return smi[:, c0:c0 + n]

        scr2i = SCR2[:].bitcast(I32)
        marks = {}

        k.dma("sync", RES[:], x_sl.rearrange("(c p) h -> p c h", p=128), dxx)
        marks["res"] = (dxx, 16)
        k.dma("sync", IDENT[:], id_in, dx)
        k.dma("sync", BQF[:], bq_in.rearrange("(c p) -> p c", p=128), dx, slow=True)
        k.dma("sync", BBK[:], bk_in.unsqueeze(0).broadcast_to([128, 512]), dx)
        k.dma("sync", BBV[:], bv_in.unsqueeze(0).broadcast_to([128, 512]), dx)
        init_mark = (dx, dx.v)
        k.v(lambda e: e.memset(ONESB[:], 1.0), inc=vq)
        k.v(lambda e: e.memset(EPSC[:], EPS_LN), inc=vq)

        def stg(base):
            return dict(
                pb=[IQT[:, base:base + 4096].bitcast(F32),
                    IQT[:, base + 4096:base + 8192].bitcast(F32)],
                ex=IQT[:, base + 8192:base + 12288].bitcast(F32),
                m=IQT[:, base + 12288:base + 16384].bitcast(F32),
            )

        STG = [stg(0), stg(16384),
               dict(pb=[IQT[:, 0:2048].bitcast(F32),
                        IQT[:, 2048:4096].bitcast(F32)],
                    ex=IQT[:, 4096:6144].bitcast(F32),
                    m=IQT[:, 6144:8192].bitcast(F32))]
        Q_BUF = [[QBF[:, 0:2048], X1T[:, 0:2048]],
                 [IQT[:, 16384 + 8192:16384 + 10240],
                  IQT[:, 16384 + 10240:16384 + 12288]],
                 [IQT[:, 8192:9216], IQT[:, 9216:10240]]]
        T_BUF = [[SCR1[:].bitcast(BF16)[:, 0:2048],
                  SCR1[:].bitcast(BF16)[:, 2048:4096]],
                 [IQT[:, 16384 + 12288:16384 + 14336],
                  IQT[:, 16384 + 14336:16384 + 16384]],
                 [IQT[:, 10240:11264], IQT[:, 11264:12288]]]
        pp_idx = [0, 0, 0]
        PCAP = [2048, 2048, 1024]
        SMREG = [(SCR2[:], 0, 64), (SCR2[:], 1024, 64),
                 (SMALL[:], 64, 32)]

        MATS = {
            "g": (wg_sl, IFF, HSL, "w"),
            "u": (wu_sl, IFF, HSL, "w"),
            "d": (wd_sl, D, ISL, "w"),
            "q": (wq_sl, D, HSL, "c"),
            "k": (wk_sl, 512, HSL, "c"),
            "v": (wv_sl, 512, HSL, "c"),
            "o": (wo_sl, D, HSL, "c"),
        }

        def prep_quant_piece(name, p, path):
            src, rows, cols, mode = MATS[name]
            nrc = rows // 128
            prc = min(nrc, PCAP[path] // cols)
            st = STG[path]
            smt, smb, sp = SMREG[path]
            nblk = (prc * cols) // 32
            src_r = src.rearrange("(c p) w -> p c w", p=128)
            gi = pp_idx[path]
            pp_idx[path] += 1
            slot = gi % 2
            pbuf = st["pb"][slot]
            pb_v = pbuf[:, 0:prc * cols].rearrange("p (c w) -> p c w", c=prc)
            lw = [marks[f"pp_qdone_{path}_{slot}"]] \
                if f"pp_qdone_{path}_{slot}" in marks else []
            if path == 2 and gi < 2:
                lw = lw + [marks["prep_p0_done"]]
            ldsem = {0: dwp, 1: dgl, 2: ddl}[path][slot]
            if path != 1:
                k.dma("sync", pb_v, src_r[:, p * prc:(p + 1) * prc, :],
                      ldsem, waits=lw)
                ldm = (ldsem, ldsem.v)
                vf, vs = k.v, vq
            else:
                k.dma("gpsimd", pb_v, src_r[:, p * prc:(p + 1) * prc, :],
                      ldsem, waits=lw)
                ldm = (ldsem, ldsem.v)
                vf, vs = k.g, gq
            tposed = [marks[f"tposed_{path}_{slot}"]] \
                if f"tposed_{path}_{slot}" in marks else []
            if mode == "w":
                pb3 = pbuf[:, 0:prc * cols].rearrange("p (b c) -> p b c", c=32)
                smi2 = smt.bitcast(I32)
                am = smt[:, smb:smb + nblk]
                ae2 = smt[:, smb + sp:smb + sp + nblk]
                bf2 = smt[:, smb + 2 * sp:smb + 2 * sp + nblk]
                c6 = smt[:, smb + 3 * sp:smb + 3 * sp + nblk]
                ng6 = smt[:, smb + 4 * sp:smb + 4 * sp + nblk]
                ami = smi2[:, smb:smb + nblk]
                ae2i = smi2[:, smb + sp:smb + sp + nblk]
                exf = st["ex"][:, 0:prc * cols]
                pbf = pbuf[:, 0:prc * cols]
                mf = st["m"][:, 0:prc * cols]
                assert path != 1, "quantized weights must use a DVE path"
                vf(lambda e, a=am, b=pb3: e.tensor_reduce(
                    a, b, AXX, Alu.max, apply_absolute_value=True),
                    waits=[ldm])
                vf(lambda e, a=ae2i, b=ami: e.tensor_single_scalar(
                    a, b, 0x7F800000, Alu.bitwise_and))
                vf(lambda e, a=exf.bitcast(I32), b=pbf.bitcast(I32):
                   e.tensor_single_scalar(a, b, 0x7F800000, Alu.bitwise_and))
                vf(lambda e, a=ae2: e.tensor_scalar_max(a, a, TINY))
                vf(lambda e, a=bf2, b=ae2: e.tensor_scalar_mul(
                    a, b, 1.5 * 2.0 ** 20))
                vf(lambda e, a=c6, b=ae2: e.tensor_scalar_mul(a, b, 1.5))
                vf(lambda e, a=ng6, b=c6: e.tensor_scalar_mul(a, b, -1.0))
                bf2b = bf2.unsqueeze(2).broadcast_to([128, nblk, 32])
                ex3 = exf.rearrange("p (b c) -> p b c", c=32)
                m3 = mf.rearrange("p (b c) -> p b c", c=32)
                vf(lambda e, a=m3, b=ex3, c=bf2b: e.scalar_tensor_tensor(
                    a, b, MAGIC_W, c, Alu.mult, Alu.max))
                vf(lambda e, a=exf, b=pbf, c=mf: e.tensor_tensor(
                    a, b, c, Alu.add))
                vf(lambda e, a=exf, c=mf: e.tensor_tensor(
                    a, a, c, Alu.subtract))
                c6b = c6.unsqueeze(2).broadcast_to([128, nblk, 32])
                ng6b = ng6.unsqueeze(2).broadcast_to([128, nblk, 32])
                vf(lambda e, a=ex3, c=c6b: e.tensor_tensor(a, a, c, Alu.min))
                pq3 = Q_BUF[path][slot][:, 0:prc * cols].rearrange(
                    "p (b c) -> p b c", c=32)
                vf(lambda e, a=pq3, b=ex3, c=ng6b: e.tensor_tensor(
                    a, b, c, Alu.max), waits=tposed, inc=vs)
            else:
                pq_v = Q_BUF[path][slot][:, 0:prc * cols].rearrange(
                    "p (c w) -> p c w", c=prc)
                vf(lambda e, a=pq_v, b=pb_v: e.tensor_copy(a, b),
                   waits=[ldm] + tposed, inc=vs)
            marks[f"pp_qdone_{path}_{slot}"] = (vs, vs.v)
            return (vs, vs.v)

        pt_idx = [0, 0, 0]

        def prep_tail_piece(name, p, path, q_mark):
            src, rows, cols, mode = MATS[name]
            nrc = rows // 128
            prc = min(nrc, PCAP[path] // cols)
            ncc = cols // 128
            rows_piece = prc * 128
            slot = pt_idx[path] % 2
            pt_idx[path] += 1
            pq_v = Q_BUF[path][slot][:, 0:prc * cols].rearrange(
                "p (c w) -> p c w", c=prc)
            pt_v = T_BUF[path][slot][:, 0:ncc * rows_piece].rearrange(
                "p (c r) -> p c r", c=ncc)
            shipw = [marks[f"ship_{path}_{slot}"]] \
                if f"ship_{path}_{slot}" in marks else []
            for cc in range(ncc):
                for rc0 in range(0, prc, 8):
                    nb = min(8, prc - rc0)
                    b = (cc + rc0 // 8 + p) % 2
                    k.w("tensor", (pt[b].rd, pt[b].rd.v), q_mark, init_mark)
                    for j in range(nb):
                        k.t(lambda e, o=pt[b].ap[:, j * 128:(j + 1) * 128],
                            i=pq_v[:, rc0 + j, cc * 128:(cc + 1) * 128]:
                            e.transpose(o, i, IDENT[:]),
                            inc=(pt[b].wr if j == nb - 1 else None))
                    src3 = pt[b].ap[:, 0:nb * 128].rearrange(
                        "p (c f) -> p c f", c=nb)
                    k.v(lambda e, o=pt_v[:, cc, rc0 * 128:(rc0 + nb) * 128],
                        i=src3: e.tensor_copy(o, i),
                        waits=[(pt[b].wr, pt[b].wr.v)] + shipw,
                        inc=pt[b].rd)
            k.nop("tensor", pq)
            marks[f"tposed_{path}_{slot}"] = (pq, pq.v)
            k.nop("vector", vq)
            evac_mark = (vq, vq.v)
            dst = agin[name].ap().rearrange("(c p) r -> p c r", p=128)[
                :, :, p * rows_piece:(p + 1) * rows_piece]
            k.dma("scalar", dst, pt_v, d_sh[name][slot], waits=[evac_mark])
            marks[f"ship_{path}_{slot}"] = (d_sh[name][slot],
                                            d_sh[name][slot].v)
            marks[f"ship_{name}"] = [(d_sh[name][0], d_sh[name][0].v),
                                     (d_sh[name][1], d_sh[name][1].v)]

        def prep_ag(name):
            k.g(lambda e, a=agin[name], o=agout[name]: e.collective_compute(
                "AllGather", Alu.bypass,
                replica_groups=[[0, 1, 2, 3, 4, 5, 6, 7]],
                ins=[a.ap().opt()], outs=[o.ap().opt()]),
                waits=list(marks[f"ship_{name}"]) + [(ag, AGN[name] - 1)],
                inc=ag)
            marks[f"ag_{name}"] = (ag, AGN[name])

        def npieces(name, path=0):
            src, rows, cols, mode = MATS[name]
            nrc = rows // 128
            prc = min(nrc, PCAP[path] // cols)
            return nrc // prc

        p0_list = [("g", p) for p in range(npieces("g"))] + \
                  [("u", p) for p in range(npieces("u"))]
        p1_list = [("q", p) for p in range(npieces("q"))] + \
                  [("k", p) for p in range(npieces("k"))] + \
                  [("v", p) for p in range(npieces("v"))] + \
                  [("o", p) for p in range(npieces("o"))]
        q_marks = {}
        for i in range(max(len(p0_list), len(p1_list)) + 1):
            for lst, path in ((p0_list, 0), (p1_list, 1)):
                if 0 < i <= len(lst):
                    nm0, pp0 = lst[i - 1]
                    prep_tail_piece(nm0, pp0, path, q_marks[(nm0, pp0)])
                    if pp0 == npieces(nm0) - 1:
                        prep_ag(nm0)
                if i < len(lst):
                    nm, p = lst[i]
                    q_marks[(nm, p)] = prep_quant_piece(nm, p, path)


        d_units = []
        ND = npieces("d", 2)

        def mk_d_unit(p):
            def go():
                if p > 0:
                    prep_tail_piece("d", p - 1, 2, q_marks[("d", p - 1)])
                q_marks[("d", p)] = prep_quant_piece("d", p, 2)
            return go

        def mk_d_last():
            def go():
                prep_tail_piece("d", ND - 1, 2, q_marks[("d", ND - 1)])
                prep_ag("d")
                marks["iqt_lower_free"] = [
                    (pq, pq.v), (ddl[0], ddl[0].v), (ddl[1], ddl[1].v),
                    (d_sh["d"][0], d_sh["d"][0].v),
                    (d_sh["d"][1], d_sh["d"][1].v)]
            return go

        for p in range(ND):
            d_units.append(mk_d_unit(p))
        d_units.append(mk_d_last())

        marks["prep_p0_done"] = (vq, vq.v)
        marks["prep_gq"] = (gq, gq.v)
        marks["iqt_free_list"] = [(pq, pq.v), (gq, gq.v),
                                  (dwp[0], dwp[0].v), (dwp[1], dwp[1].v),
                                  (dgl[0], dgl[0].v), (dgl[1], dgl[1].v)] + \
            [(d_sh[m][j], d_sh[m][j].v) for m in "gudqkvo" for j in (0, 1)]
        marks["qbf_free"] = (pq, pq.v)
        marks["scr1_free"] = (d_sh["g"][0], d_sh["g"][0].v)
        marks["scr1_free2"] = (d_sh["g"][1], d_sh["g"][1].v)
        marks["scr1_free3"] = (d_sh["u"][0], d_sh["u"][0].v)
        marks["scr1_free4"] = (d_sh["u"][1], d_sh["u"][1].v)
        marks["phaseA_done"] = (pq, pq.v)
        marks["gS_free"] = (wf, 0)
        marks["uS_free"] = (wf, 0)

        def ln_stats(chunks):
            res_mark = marks["res"]
            n = len(chunks)
            for i, gc in enumerate(chunks):
                cv = RES[:, gc, :]
                k.v(lambda e, o=sm(S_SUM + i, 1), iv=cv: e.reduce_sum(
                    o, iv, AXX), waits=[res_mark] if i == 0 else [])
                k.a(lambda e, o=SCR1[:], iv=cv, acc=sm(S_SQ + i, 1):
                    e.activation(o, iv, Act.Square, accum_out=acc),
                    waits=([res_mark, marks.get("scr1_free", (vq, 0)),
                            marks.get("scr1_free2", (vq, 0)),
                            marks.get("scr1_free3", (vq, 0)),
                            marks.get("scr1_free4", (vq, 0))]
                           if i == 0 else []),
                    inc=(aq if i == n - 1 else None))
            k.v(lambda e, o=sm(S_NM, n), iv=sm(S_SUM, n): e.tensor_scalar_mul(
                o, iv, -1.0 / D))
            k.v(lambda e, o=sm(S_MSQ, n), iv=sm(S_NM, n): e.tensor_tensor(
                o, iv, iv, Alu.mult))
            k.v(lambda e, o=sm(S_VAR, n), i0=sm(S_SQ, n), i1=sm(S_MSQ, n):
                e.scalar_tensor_tensor(o, i0, 1.0 / D, i1, Alu.mult,
                                       Alu.subtract),
                waits=[(aq, aq.v)], inc=vq)
            k.a(lambda e, o=sm(S_STD, n), iv=sm(S_VAR, n): e.activation(
                o, iv, Act.Sqrt, bias=EPSC[:, 0:1]), waits=[(vq, vq.v)],
                inc=aq)
            k.v(lambda e, o=sm(S_RSTD, n), iv=sm(S_STD, n): e.reciprocal(
                o, iv), waits=[(aq, aq.v)])

        def xquant_chain(xv, nblk, am_c, ae_c, d_c, c_c, qf8v, qbfv,
                         qbf3=None, inc=None):
            x3 = xv.rearrange("p (b c) -> p b c", c=32)
            k.v(lambda e, o=sm(am_c, nblk), iv=x3: e.tensor_reduce(
                o, iv, AXX, Alu.max, apply_absolute_value=True))
            k.v(lambda e, o=smi_(ae_c, nblk), iv=smi_(am_c, nblk):
                e.tensor_single_scalar(o, iv, 0x7F800000, Alu.bitwise_and))
            k.v(lambda e, o=sm(ae_c, nblk): e.tensor_scalar_max(o, o, TINY))
            k.v(lambda e, o=sm(d_c, nblk), iv=sm(ae_c, nblk):
                e.tensor_scalar_mul(o, iv, 16.0))
            k.v(lambda e, o=sm(c_c, nblk), iv=sm(d_c, nblk): e.reciprocal(
                o, iv))
            cb = sm(c_c, nblk).unsqueeze(2).broadcast_to([128, nblk, 32])
            db = sm(d_c, nblk).unsqueeze(2).broadcast_to([128, nblk, 32])
            k.v(lambda e, o=x3, c=cb: e.tensor_tensor(o, o, c, Alu.mult))
            k.v(lambda e, o=xv: e.tensor_scalar(o, o, CLIP_A, -CLIP_A,
                                                Alu.min, Alu.max), inc=vq)
            k.a(lambda e, o=qf8v, iv=xv: e.copy(o, iv),
                waits=[(vq, vq.v)], inc=aq)
            f83 = qf8v.rearrange("p (b c) -> p b c", c=32)
            q3 = qbfv.rearrange("p (b c) -> p b c", c=32)
            k.v(lambda e, o=q3, iv=f83, d=db: e.tensor_tensor(
                o, iv, d, Alu.mult), waits=[(aq, aq.v)], inc=(inc or vq))

        def transpose_row(srcv, nt, dst_fn, bsel, evac_extra=()):
            for j0 in range(0, nt, 8):
                nb = min(8, nt - j0)
                b = (bsel + j0 // 8) % 2
                k.w("tensor", (pt[b].rd, pt[b].rd.v), (vq, vq.v))
                for j in range(nb):
                    k.t(lambda e, o=pt[b].ap[:, j * 128:(j + 1) * 128],
                        i=srcv[:, (j0 + j) * 128:(j0 + j + 1) * 128]:
                        e.transpose(o, i, IDENT[:]),
                        inc=(pt[b].wr if j == nb - 1 else None))
                src3 = pt[b].ap[:, 0:nb * 128].rearrange(
                    "p (c f) -> p c f", c=nb)
                k.v(lambda e, o=dst_fn(j0, nb), i=src3: e.tensor_copy(o, i),
                    waits=[(pt[b].wr, pt[b].wr.v)] + list(evac_extra),
                    inc=pt[b].rd)
            k.nop("vector", vq)

        CMAP = [(ib + 8) % NIB for ib in range(NIB)]

        def ffn_tile(t, sub, side=()):
            side = list(side)
            chunks = [t * TCH + c for c in range(TCH)]
            ln_stats(chunks)
            phA = marks["phaseA_done"]
            for c in range(TCH):
                k.w("vector", marks["qbf_free"])
                k.v(lambda e, o=SCR1[:], iv=RES[:, chunks[c], :],
                    s1=sm(S_NM + c, 1), s2=sm(S_RSTD + c, 1):
                    e.tensor_scalar(o, iv, s1, s2, Alu.add, Alu.mult))
                xquant_chain(SCR1[:], 64, S_AMAX, S_AE2, S_D, S_C,
                             QF8[:], QBF[:])
                ext = [phA] if c == 0 else []
                transpose_row(
                    QBF[:], NHC,
                    lambda j0, nb, c=c: x1t_v[:, j0:j0 + nb,
                                              c * 128:(c + 1) * 128],
                    bsel=c, evac_extra=ext)
                k.nop("tensor", pq)
                marks["qbf_free"] = (pq, pq.v)
            x1t_mark = (vq, vq.v)

            iq_marks = {}

            def emit_inter_transposes(ib):
                first = not marks.get(f"iqt_started_{t}_{sub}")
                for pr in range(2):
                    b = pr
                    qbp = QBF[:, pr * 1024:(pr + 1) * 1024]
                    k.w("tensor", (pt[b].rd, pt[b].rd.v), iq_marks[(ib, pr)])
                    nt = 0
                    for ch in range(4):
                        for sh in range(2):
                            nt += 1
                            k.t(lambda e,
                                o=pt[b].ap[:, (ch * 2 + sh) * 128:
                                           (ch * 2 + sh + 1) * 128],
                                i=qbp[:, sh * 512 + ch * 128:
                                      sh * 512 + (ch + 1) * 128]:
                                e.transpose(o, i, IDENT[:]),
                                inc=(pt[b].wr if nt == 8 else None))
                    src3 = pt[b].ap[:, 0:1024].rearrange(
                        "p (c f) -> p c f", c=4)
                    cm = CMAP[ib]
                    dst = iqt_v[:, 4 * cm:4 * cm + 4,
                                2 * pr * 128:(2 * pr + 2) * 128]
                    ew = [(pt[b].wr, pt[b].wr.v)]
                    if first and pr == 0:
                        ew += marks["iqt_free_list"]
                    if cm == 0 and pr == 0 and "iqt_lower_free" in marks:
                        ew += marks.pop("iqt_lower_free")
                    k.v(lambda e, o=dst, i=src3: e.tensor_copy(o, i),
                        waits=ew, inc=pt[b].rd)
                marks[f"iqt_started_{t}_{sub}"] = True
                k.nop("vector", vq)
                k.nop("tensor", pq)
                marks["qbf_free"] = (pq, pq.v)

            def emit_consume_pair(ib, pr):
                s0 = 2 * pr
                pav = pa_t[:, s0:s0 + 2, :].rearrange("p a b -> p (a b)")
                pbv = pb_t[:].rearrange("p a b -> p (a b)")
                sl1 = SCR1[:, pr * 1024:(pr + 1) * 1024]
                sl2 = SCR2[:, pr * 1024:(pr + 1) * 1024]
                qf = QF8[:, pr * 1024:(pr + 1) * 1024]
                qb = QBF[:, pr * 1024:(pr + 1) * 1024]
                k.a(lambda e, o=sl1, iv=pav: e.activation(
                    o, iv, Act.Silu),
                    waits=[(pa[s0].wr, pa[s0].wr.v),
                           (pa[s0 + 1].wr, pa[s0 + 1].wr.v),
                           marks.get(f"sl1_free_{pr}", (vq, 0))],
                    inc=pa[s0].rd)
                k.nop("scalar", pa[s0 + 1].rd)
                k.nop("scalar", aq)
                silu_mark = (aq, aq.v)
                ivw = [silu_mark, (pb[0].wr, pb[0].wr.v),
                       (pb[1].wr, pb[1].wr.v)]
                if ib == 0:
                    ivw.append(marks["qbf_free"])
                    if t == 0 and sub == 1:
                        ivw.append(marks["prep_gq"])
                k.v(lambda e, o=sl2, i0=sl1, i1=pbv: e.tensor_tensor(
                    o, i0, i1, Alu.mult), waits=ivw, inc=pb[0].rd)
                k.nop("vector", pb[1].rd)
                k.nop("vector", vq)
                marks[f"sl1_free_{pr}"] = (vq, vq.v)
                xquant_chain(sl2, 32, S_IAM, S_IAE, S_ID, S_IC, qf, qb)
                iq_marks[(ib, pr)] = (vq, vq.v)

            for ib in range(NIB):
                k.dma("sync", gS_v, wT_r["g"][:, :, ib * IB:(ib + 1) * IB],
                      d_gS, waits=[marks["gS_free"], marks["ag_g"]])
                g_ld = (d_gS, d_gS.v)
                k.dma("sync", uS_v, wT_r["u"][:, :, ib * IB:(ib + 1) * IB],
                      d_uS, waits=[marks["uS_free"], marks["ag_u"]])
                u_ld = (d_uS, d_uS.v)
                for s in range(TCH):
                    bk_ = pa[s]
                    k.w("tensor", (bk_.rd, bk_.rd.v), g_ld, x1t_mark)
                    for h in range(NHC):
                        k.t(lambda e, o=bk_.ap,
                            l=x1t_v[:, h, s * 128:(s + 1) * 128],
                            r=gS_v[:, h, :], st=(h == 0), sp=(h == NHC - 1):
                            e.matmul(o, l, r, start=st, stop=sp),
                            inc=(bk_.wr if h == NHC - 1 else None))
                k.nop("tensor", wf)
                marks["gS_free"] = (wf, wf.v)
                if ib > 0:
                    emit_inter_transposes(ib - 1)
                for _ in range(2):
                    if side:
                        side.pop(0)()
                if ib == NIB - 1:
                    while side:
                        side.pop(0)()
                for s in range(TCH):
                    bk_ = pb[s % 2]
                    k.w("tensor", (bk_.rd, bk_.rd.v), u_ld)
                    for h in range(NHC):
                        k.t(lambda e, o=bk_.ap,
                            l=x1t_v[:, h, s * 128:(s + 1) * 128],
                            r=uS_v[:, h, :], st=(h == 0), sp=(h == NHC - 1):
                            e.matmul(o, l, r, start=st, stop=sp),
                            inc=(bk_.wr if h == NHC - 1 else None))
                    if s % 2 == 1:
                        emit_consume_pair(ib, s // 2)
                k.nop("tensor", wf)
                marks["uS_free"] = (wf, wf.v)

            emit_inter_transposes(NIB - 1)
            k.nop("tensor", pq)
            marks["phaseA_done"] = (pq, pq.v)
            iqt_mark = (vq, vq.v)

            for hg in range(4):
                ldm = {}
                for pc in range(2):
                    sl = pc % 2
                    wl = [marks["gS_free"] if hg == 0 else
                          marks[f"wd_free_{sl}"], marks["ag_d"]]
                    k.dma("sync", wdS_v[sl],
                          wT_r["d"][:, pc * 8:(pc + 1) * 8,
                                    hg * 512:(hg + 1) * 512], d_wd[sl],
                          waits=wl)
                    ldm[pc] = (d_wd[sl], d_wd[sl].v)
                for pc in range(8):
                    sl = pc % 2
                    k.w("tensor", ldm[pc])
                    if pc == 0:
                        k.w("tensor", iqt_mark,
                            *[(pa[s].rd, pa[s].rd.v) for s in range(TCH)])
                    for s in range(TCH):
                        bk_ = pa[s]
                        for ic in range(8):
                            st = (pc == 0 and ic == 0)
                            sp = (pc == 7 and ic == 7)
                            lc = pc * 8 + ic
                            pchunk = 4 * CMAP[lc // 4] + lc % 4
                            k.t(lambda e, o=bk_.ap,
                                l=iqt_v[:, pchunk,
                                        s * 128:(s + 1) * 128],
                                r=wdS_v[sl][:, ic, :], st=st, sp=sp:
                                e.matmul(o, l, r, start=st, stop=sp),
                                inc=(bk_.wr if sp else None))
                    k.nop("tensor", wf)
                    marks[f"wd_free_{sl}"] = (wf, wf.v)
                    if pc + 2 < 8:
                        k.dma("sync", wdS_v[(pc + 2) % 2],
                              wT_r["d"][:, (pc + 2) * 8:(pc + 3) * 8,
                                        hg * 512:(hg + 1) * 512],
                              d_wd[(pc + 2) % 2],
                              waits=[marks[f"wd_free_{sl}"]])
                        ldm[pc + 2] = (d_wd[(pc + 2) % 2],
                                       d_wd[(pc + 2) % 2].v)
                for pr in range(2):
                    s0 = 2 * pr
                    pav3 = pa_t[:, s0:s0 + 2, :]
                    rv = RES[:, chunks[s0]:chunks[s0] + 2,
                             hg * 512:(hg + 1) * 512]
                    k.v(lambda e, o=rv, i0=pav3, i1=rv:
                        e.scalar_tensor_tensor(o, i0, 0.5, i1, Alu.mult,
                                               Alu.add),
                        waits=[(pa[s0].wr, pa[s0].wr.v),
                               (pa[s0 + 1].wr, pa[s0 + 1].wr.v)],
                        inc=pa[s0].rd)
                    k.nop("vector", pa[s0 + 1].rd)
            k.nop("vector", vq)
            marks["res"] = (vq, vq.v)
            k.nop("tensor", pq)
            marks["iqt_free_list"] = [(pq, pq.v)]
            marks["gS_free"] = (wf, wf.v)
            marks["scr1_free"] = (vq, vq.v)

        def attention():
            ln_stats(list(range(NCH)))
            for c in range(NCH):
                ew = [marks["qbf_free"]]
                k.w("vector", marks["qbf_free"])
                if c == 0:
                    k.w("vector", marks["phaseA_done"])
                k.v(lambda e, o=QBF[:], iv=RES[:, c, :], s1=sm(S_NM + c, 1),
                    s2=sm(S_RSTD + c, 1): e.tensor_scalar(
                        o, iv, s1, s2, Alu.add, Alu.mult), inc=vq)
                ext = marks["iqt_free_list"] if c == 0 else []
                transpose_row(
                    QBF[:], NHC,
                    lambda j0, nb, c=c: x2t_v[:, j0:j0 + nb,
                                              c * 128:(c + 1) * 128],
                    bsel=c, evac_extra=ext)
                k.nop("tensor", pq)
                marks["qbf_free"] = (pq, pq.v)
            x2t_mark = (vq, vq.v)
            k.nop("tensor", pq)
            x2t_pe = (pq, pq.v)

            k.dma("sync", kS_v, wT_r["k"][:], d_kS,
                  waits=[(wf, wf.v), marks["ag_k"]])
            k_ld = (d_kS, d_kS.v)
            k.dma("sync", vS_v, wT_r["v"][:], d_vS, waits=[marks["ag_v"]])
            v_ld = (d_vS, d_vS.v)

            for which, wv_, ld, outv in (("k", kS_v, k_ld, sk_v),
                                         ("v", vS_v, v_ld, vv_v)):
                for s in range(NCH):
                    bk_ = pa[s % 4]
                    k.w("tensor", (bk_.rd, bk_.rd.v), ld)
                    if which == "k" and s == 0:
                        k.w("tensor", x2t_mark)
                    for h in range(NHC):
                        k.t(lambda e, o=bk_.ap,
                            l=x2t_v[:, h, s * 128:(s + 1) * 128],
                            r=wv_[:, h, :], st=(h == 0), sp=(h == NHC - 1):
                            e.matmul(o, l, r, start=st, stop=sp),
                            inc=(bk_.wr if h == NHC - 1 else None))
                    if which == "k":
                        sl1 = SCR1[:, (s % 4) * 512:(s % 4 + 1) * 512]
                        sl2 = SCR2[:, (s % 4) * 512:(s % 4 + 1) * 512]
                        w0 = [(bk_.wr, bk_.wr.v),
                              marks.get(f"asl_{s % 4}", (vq, 0))]
                        if s == 0:
                            w0.append(marks["phaseA_done"])
                        k.v(lambda e, o=sl1, i0=bk_.ap, i1=BBK[:]:
                            e.tensor_tensor(o, i0, i1, Alu.add),
                            waits=w0, inc=bk_.rd)
                        k.v(lambda e, o=sl2, iv=sl1: e.tensor_scalar_min(
                            o, iv, 0.0), inc=vq)
                        k.a(lambda e, o=sl2, iv=sl2: e.activation(
                            o, iv, Act.Exp), waits=[(vq, vq.v)], inc=aq)
                        k.v(lambda e, o=outv[:, s, :], i0=sl1, i1=sl2:
                            e.scalar_tensor_tensor(o, i0, 0.0, i1, Alu.max,
                                                   Alu.add),
                            waits=[(aq, aq.v)], inc=vq)
                        marks[f"asl_{s % 4}"] = (vq, vq.v)
                    else:
                        k.v(lambda e, o=outv[:, s, :], i0=bk_.ap, i1=BBV[:]:
                            e.tensor_tensor(o, i0, i1, Alu.add),
                            waits=[(bk_.wr, bk_.wr.v)], inc=bk_.rd)
                k.nop("tensor", wf)
            k.nop("vector", vq)
            skv_mark = (vq, vq.v)

            k.v(lambda e: e.memset(SCR1[:, 0:528], 0.0),
                waits=[marks.get("asl_0", (vq, 0))], inc=vq)
            z_mark = (vq, vq.v)
            for g_ in range(KVH):
                bk_ = pb[g_ % 2]
                k.w("tensor", (bk_.rd, bk_.rd.v), skv_mark)
                for s in range(NCH):
                    k.t(lambda e, o=bk_.ap[:, 0:128],
                        l=sk_v[:, s, g_ * 128:(g_ + 1) * 128],
                        r=vv_v[:, s, g_ * 128:(g_ + 1) * 128], st=(s == 0):
                        e.matmul(o, l, r, start=st, stop=False))
                for s in range(NCH):
                    k.t(lambda e, o=bk_.ap[:, 128:129],
                        l=sk_v[:, s, g_ * 128:(g_ + 1) * 128],
                        r=ONESB[:, 0:1], sp=(s == NCH - 1):
                        e.matmul(o, l, r, start=False, stop=sp),
                        inc=(bk_.wr if s == NCH - 1 else None))
                k.v(lambda e, o=SCR1[:, g_ * 132:g_ * 132 + 129],
                    i=bk_.ap[:, 0:129]: e.tensor_copy(o, i),
                    waits=[(bk_.wr, bk_.wr.v), z_mark], inc=bk_.rd)
            k.nop("vector", vq)
            kvp_mark = (vq, vq.v)
            k.nop("tensor", pq)
            kvmm_pe = (pq, pq.v)
            marks["phaseA_done"] = kvmm_pe
            k.dma("gpsimd",
                  kvin.ap().rearrange("(g p) c -> p g c", p=128),
                  SCR1[:, 0:528].rearrange("p (g c) -> p g c", c=132),
                  d_kvs, waits=[kvp_mark])
            kv_ship = (d_kvs, d_kvs.v)
            k.g(lambda e: e.collective_compute(
                "AllGather", Alu.bypass,
                replica_groups=[[0, 1, 2, 3], [4, 5, 6, 7]],
                ins=[kvin.ap().opt()], outs=[kvout.ap().opt()]),
                waits=[kv_ship, (ag, AGN["kv"] - 1)], inc=ag)
            kv_ag = (ag, ag.v)

            qo_ld = {}
            for dqb in range(2):
                k.dma("sync", qoS_v[dqb % 2],
                      wT_r["q"][:, :, dqb * 512:(dqb + 1) * 512],
                      d_qo[dqb % 2], waits=[(wf, wf.v), marks["ag_q"]])
                qo_ld[dqb] = (d_qo[dqb % 2], d_qo[dqb % 2].v)
            for dqb in range(4):
                sl = dqb % 2
                for c2 in range(4):
                    dq = dqb * 4 + c2
                    for half in range(2):
                        bk_ = pa[(2 * dq + half) % 4]
                        k.w("tensor", (bk_.rd, bk_.rd.v), qo_ld[dqb])
                        for h in range(NHC):
                            k.t(lambda e, o=bk_.ap,
                                l=qoS_v[sl][:, h, c2 * 128:(c2 + 1) * 128],
                                r=x2t_v[:, h, half * 512:(half + 1) * 512],
                                st=(h == 0), sp=(h == NHC - 1):
                                e.matmul(o, l, r, start=st, stop=sp),
                                inc=(bk_.wr if h == NHC - 1 else None))
                        j = (2 * dq + half) % 4
                        sl1 = SCR1[:, j * 512:(j + 1) * 512]
                        sl2 = SCR2[:, j * 512:(j + 1) * 512]
                        w0 = [(bk_.wr, bk_.wr.v),
                              marks.get(f"asl_{j}", (vq, 0))]
                        if dqb == 0 and c2 == 0 and half == 0:
                            w0.append(kv_ship)
                        k.v(lambda e, o=sl1, iv=bk_.ap,
                            s1=BQF[:, dq:dq + 1]: e.tensor_scalar_add(
                                o, iv, s1), waits=w0, inc=bk_.rd)
                        k.v(lambda e, o=sl2, iv=sl1: e.tensor_scalar_min(
                            o, iv, 0.0), inc=vq)
                        k.a(lambda e, o=sl2, iv=sl2: e.activation(
                            o, iv, Act.Exp), waits=[(vq, vq.v)], inc=aq)
                        k.v(lambda e,
                            o=sqt_v[:, dq, half * 512:(half + 1) * 512],
                            i0=sl1, i1=sl2: e.scalar_tensor_tensor(
                                o, i0, 0.0, i1, Alu.max, Alu.add),
                            waits=[(aq, aq.v)], inc=vq)
                        marks[f"asl_{j}"] = (vq, vq.v)
                k.nop("tensor", wf)
                marks[f"qo_free_{sl}"] = (wf, wf.v)
                if dqb + 2 < 4:
                    k.dma("sync", qoS_v[sl],
                          wT_r["q"][:, :, (dqb + 2) * 512:(dqb + 3) * 512],
                          d_qo[sl], waits=[marks[f"qo_free_{sl}"]])
                    qo_ld[dqb + 2] = (d_qo[sl], d_qo[sl].v)
            sqt_mark = (vq, vq.v)

            k.dma("gpsimd", kvs1,
                  kvout.ap().rearrange("(b p) c -> p b c", p=128)[:, 0:8, :],
                  dgp, waits=[kv_ag, kvmm_pe])
            k.dma("gpsimd", kvs2,
                  kvout.ap().rearrange("(b p) c -> p b c", p=128)[:, 8:16, :],
                  dgp)
            kvg_ld = (dgp, dgp.v)
            k.v(lambda e: e.tensor_tensor(kvs1[:, 0:4, :], kvs1[:, 0:4, :],
                                          kvs1[:, 4:8, :], Alu.add),
                waits=[kvg_ld])
            k.v(lambda e: e.tensor_tensor(kvs2[:, 0:4, :], kvs2[:, 0:4, :],
                                          kvs2[:, 4:8, :], Alu.add))
            k.v(lambda e: e.tensor_tensor(kvs1[:, 0:4, :], kvs1[:, 0:4, :],
                                          kvs2[:, 0:4, :], Alu.add))
            k.v(lambda e: e.tensor_copy(kvb_v[:], kvs1[:, 0:4, :]), inc=vq)
            kvb_mark = (vq, vq.v)

            k.nop("tensor", pq)
            qproj_pe = (pq, pq.v)
            for s in range(NCH):
                for h in range(NH):
                    bk_ = pb[h % 2]
                    g_ = h // 4
                    w0 = [(bk_.rd, bk_.rd.v)]
                    if s == 0 and h == 0:
                        w0 += [kvb_mark, sqt_mark]
                    k.w("tensor", *w0)
                    k.t(lambda e, o=bk_.ap[:, 0:132],
                        l=sqt_v[:, h, s * 128:(s + 1) * 128],
                        r=kvb_v[:, g_, :]: e.matmul(o, l, r, start=True,
                                                    stop=True), inc=bk_.wr)
                    w1 = [(bk_.wr, bk_.wr.v)]
                    if h == 0:
                        w1.append(marks["qbf_free"])
                    k.v(lambda e, o=sm(S_NRM, 1), iv=bk_.ap[:, 128:129]:
                        e.tensor_scalar_add(o, iv, EPS_AT), waits=w1)
                    k.v(lambda e, o=sm(S_NRM, 1), iv=sm(S_NRM, 1):
                        e.reciprocal(o, iv))
                    k.v(lambda e, o=QBF[:, h * 128:(h + 1) * 128],
                        iv=bk_.ap[:, 0:128], s1=sm(S_NRM, 1):
                        e.tensor_scalar_mul(o, iv, s1), inc=bk_.rd)
                k.nop("vector", vq)
                ext = [qproj_pe] if s == 0 else []
                transpose_row(
                    QBF[:], NH,
                    lambda j0, nb, s=s: atnT_v[:, j0:j0 + nb,
                                               s * 128:(s + 1) * 128],
                    bsel=s, evac_extra=ext)
                k.nop("tensor", pq)
                marks["qbf_free"] = (pq, pq.v)
            atn_mark = (vq, vq.v)

            ldo = {}
            for hob in range(2):
                k.dma("sync", qoS_v[hob % 2],
                      wT_r["o"][:, :, hob * 512:(hob + 1) * 512],
                      d_qo[hob % 2],
                      waits=[marks[f"qo_free_{hob % 2}"], marks["ag_o"]])
                ldo[hob] = (d_qo[hob % 2], d_qo[hob % 2].v)
            for hob in range(4):
                sl = hob % 2
                for s in range(NCH):
                    bk_ = pa[s % 4]
                    w0 = [(bk_.rd, bk_.rd.v), ldo[hob]]
                    if hob == 0 and s == 0:
                        w0.append(atn_mark)
                    k.w("tensor", *w0)
                    for h in range(NHC):
                        k.t(lambda e, o=bk_.ap,
                            l=atnT_v[:, h, s * 128:(s + 1) * 128],
                            r=qoS_v[sl][:, h, :], st=(h == 0),
                            sp=(h == NHC - 1):
                            e.matmul(o, l, r, start=st, stop=sp),
                            inc=(bk_.wr if h == NHC - 1 else None))
                    rv = RES[:, s, hob * 512:(hob + 1) * 512]
                    k.v(lambda e, o=rv, i0=bk_.ap, i1=rv: e.tensor_tensor(
                        o, i0, i1, Alu.add),
                        waits=[(bk_.wr, bk_.wr.v)], inc=bk_.rd)
                k.nop("tensor", wf)
                marks[f"qo_free_{sl}"] = (wf, wf.v)
                if hob + 2 < 4:
                    k.dma("sync", qoS_v[sl],
                          wT_r["o"][:, :, (hob + 2) * 512:(hob + 3) * 512],
                          d_qo[sl], waits=[marks[f"qo_free_{sl}"]])
                    ldo[hob + 2] = (d_qo[sl], d_qo[sl].v)
            k.nop("vector", vq)
            marks["res"] = (vq, vq.v)
            marks["scr1_free"] = (vq, vq.v)
            k.nop("tensor", pq)
            marks["iqt_free_list"] = [(pq, pq.v)]
            marks["gS_free"] = marks["qo_free_0"]
            marks["uS_free"] = marks["qo_free_1"]

        ffn_tile(0, 1, d_units)
        ffn_tile(1, 1)
        attention()
        ffn_tile(0, 3)
        ffn_tile(1, 3)

        k.dma("gpsimd", out.rearrange("(c p) h -> p c h", p=128), RES[:],
              dgp, waits=[marks["res"]])
        k.w("gpsimd", (dgp, dgp.v))

        with nc.Block() as block:
            @block.sync
            def _(e):
                for fn in k.q["sync"]:
                    fn(e)

            @block.scalar
            def _(e):
                for fn in k.q["scalar"]:
                    fn(e)

            @block.vector
            def _(e):
                for fn in k.q["vector"]:
                    fn(e)

            @block.gpsimd
            def _(e):
                for fn in k.q["gpsimd"]:
                    fn(e)

            @block.tensor
            def _(e):
                for fn in k.q["tensor"]:
                    fn(e)
    return nc


_NC_CACHE = None


def kernel(L0_attn_hidden_states, L0_attn_Wq, L0_attn_bq, L0_attn_Wk,
           L0_attn_bk, L0_attn_Wv, L0_attn_bv, L0_attn_Wo, L0_ffn_w_gate,
           L0_ffn_w_up, L0_ffn_w_down):
    global _NC_CACHE
    x = np.ascontiguousarray(np.asarray(L0_attn_hidden_states, np.float32))
    Wq = np.asarray(L0_attn_Wq, np.float32)
    Wk = np.asarray(L0_attn_Wk, np.float32)
    Wv = np.asarray(L0_attn_Wv, np.float32)
    Wo = np.asarray(L0_attn_Wo, np.float32)
    Wg = np.asarray(L0_ffn_w_gate, np.float32)
    Wu = np.asarray(L0_ffn_w_up, np.float32)
    Wd = np.asarray(L0_ffn_w_down, np.float32)
    bq = np.asarray(L0_attn_bq, np.float32)
    bk = np.asarray(L0_attn_bk, np.float32)
    bv = np.asarray(L0_attn_bv, np.float32)
    ident = np.eye(128, dtype=ml_dtypes.bfloat16)

    in_maps = []
    for c in range(8):
        b, tc = divmod(c, 4)
        in_maps.append({
            "x_sl": np.ascontiguousarray(x[b, tc * NTOK:(tc + 1) * NTOK, :]),
            "wg_sl": np.ascontiguousarray(Wg[:, c * HSL:(c + 1) * HSL]),
            "wu_sl": np.ascontiguousarray(Wu[:, c * HSL:(c + 1) * HSL]),
            "wd_sl": np.ascontiguousarray(Wd[:, c * ISL:(c + 1) * ISL]),
            "wq_sl": np.ascontiguousarray(Wq[:, c * HSL:(c + 1) * HSL]),
            "wk_sl": np.ascontiguousarray(Wk[:, c * HSL:(c + 1) * HSL]),
            "wv_sl": np.ascontiguousarray(Wv[:, c * HSL:(c + 1) * HSL]),
            "wo_sl": np.ascontiguousarray(Wo[:, c * HSL:(c + 1) * HSL]),
            "bq_in": bq, "bk_in": bk, "bv_in": bv, "id_in": ident,
        })
    if _NC_CACHE is None:
        _NC_CACHE = build_nc()
    res = run_bass_kernel_spmd(_NC_CACHE, in_maps, core_ids=list(range(8)))
    outf = np.empty((2, 4096, D), np.float32)
    for c in range(8):
        b, tc = divmod(c, 4)
        outf[b, tc * NTOK:(tc + 1) * NTOK, :] = res.results[c]["out"]
    return outf


# version 10
# speedup vs baseline: 1.0910x; 1.0232x over previous
import numpy as np
import ml_dtypes
from contextlib import ExitStack

import concourse.bass as bass
import concourse.mybir as mybir
from concourse.bass_utils import run_bass_kernel_spmd

F32 = mybir.dt.float32
BF16 = mybir.dt.bfloat16
FP8 = mybir.dt.float8e4
I32 = mybir.dt.int32
Alu = mybir.AluOpType
Act = mybir.ActivationFunctionType
AXX = mybir.AxisListType.X

D = 2048
IFF = 8192
NTOK = 1024
NCH = 8
TCH = 4
NHC = 16
NIC = 64
IB = 512
NIB = IFF // IB
HSL = D // 8
ISL = IFF // 8
NH, KVH, HD = 16, 4, 128
EPS_LN = 1e-5
EPS_AT = 1e-6
CLIP_A = 15.0 / 128.0
MAGIC_W = 1.5 * 2.0 ** 22
TINY = 2.0 ** -100
AGN = {"q": 1, "k": 2, "v": 3, "o": 4, "g": 5, "u": 6, "d": 7, "kv": 8}

ENGS = ("sync", "scalar", "vector", "gpsimd", "tensor")


class Sem:
    __slots__ = ("h", "v", "name")

    def __init__(self, h, name):
        self.h = h
        self.v = 0
        self.name = name

    def inc(self, n=1):
        self.v += n
        return self.v


class Bank:
    def __init__(self, ap, wr, rd):
        self.ap = ap
        self.wr = wr
        self.rd = rd


class K:
    def __init__(self, nc):
        self.nc = nc
        self.q = {e: [] for e in ENGS}
        self._es = ExitStack()

    def sem(self, name):
        return Sem(self._es.enter_context(self.nc.semaphore(name)), name)

    def sb(self, name, shape, dt):
        return self._es.enter_context(self.nc.sbuf_tensor(name, shape, dt))

    def ps(self, name, shape, dt):
        return self._es.enter_context(self.nc.psum_tensor(name, shape, dt))

    def o(self, eng, fn, waits=(), inc=None, n=1, drain=False):
        wl = tuple((s.h, int(v)) for (s, v) in waits if v > 0)
        if inc is not None:
            inc.inc(n)
            semh = inc.h

            def run(e, fn=fn, wl=wl, semh=semh, n=n, drain=drain):
                for h, v in wl:
                    e.wait_ge(h, v)
                fn(e).then_inc(semh, n)
                if drain:
                    e.drain()
        else:
            def run(e, fn=fn, wl=wl, drain=drain):
                for h, v in wl:
                    e.wait_ge(h, v)
                fn(e)
                if drain:
                    e.drain()
        self.q[eng].append(run)

    def w(self, eng, *waits):
        wl = tuple((s.h, int(v)) for (s, v) in waits if v > 0)
        if not wl:
            return

        def run(e, wl=wl):
            for h, v in wl:
                e.wait_ge(h, v)
        self.q[eng].append(run)

    def nop(self, eng, inc, n=1, waits=()):
        self.q[eng].append(lambda e: e.drain())
        self.o(eng, lambda e: e.nop(), waits=waits, inc=inc, n=n)

    def v(self, fn, waits=(), inc=None, n=1):
        self.o("vector", fn, waits=waits, inc=inc, n=n, drain=True)

    def a(self, fn, waits=(), inc=None, n=1):
        self.o("scalar", fn, waits=waits, inc=inc, n=n, drain=True)

    def g(self, fn, waits=(), inc=None, n=1):
        self.o("gpsimd", fn, waits=waits, inc=inc, n=n, drain=True)

    def t(self, fn, waits=(), inc=None, n=1):
        self.o("tensor", fn, waits=waits, inc=inc, n=n)

    def dma(self, eng, out, in_, sem, waits=(), slow=False):
        if slow:
            self.o(eng, lambda e, o=out, i=in_: e.dma_start(
                out=o, in_=i, allow_slow_non_contiguous=True),
                waits=waits, inc=sem, n=16)
        else:
            self.o(eng, lambda e, o=out, i=in_: e.dma_start(out=o, in_=i),
                   waits=waits, inc=sem, n=16)


def build_nc():
    nc = bass.Bass("TRN2", target_bir_lowering=False, debug=False)

    x_sl = nc.dram_tensor("x_sl", [NTOK, D], F32, kind="ExternalInput").ap()
    wg_sl = nc.dram_tensor("wg_sl", [IFF, HSL], F32, kind="ExternalInput").ap()
    wu_sl = nc.dram_tensor("wu_sl", [IFF, HSL], F32, kind="ExternalInput").ap()
    wd_sl = nc.dram_tensor("wd_sl", [D, ISL], F32, kind="ExternalInput").ap()
    wq_sl = nc.dram_tensor("wq_sl", [D, HSL], F32, kind="ExternalInput").ap()
    wk_sl = nc.dram_tensor("wk_sl", [512, HSL], F32, kind="ExternalInput").ap()
    wv_sl = nc.dram_tensor("wv_sl", [512, HSL], F32, kind="ExternalInput").ap()
    wo_sl = nc.dram_tensor("wo_sl", [D, HSL], F32, kind="ExternalInput").ap()
    bq_in = nc.dram_tensor("bq_in", [D], F32, kind="ExternalInput").ap()
    bk_in = nc.dram_tensor("bk_in", [512], F32, kind="ExternalInput").ap()
    bv_in = nc.dram_tensor("bv_in", [512], F32, kind="ExternalInput").ap()
    id_in = nc.dram_tensor("id_in", [128, 128], BF16, kind="ExternalInput").ap()
    out = nc.dram_tensor("out", [NTOK, D], F32, kind="ExternalOutput").ap()

    agin = {
        "g": nc.dram_tensor("agin_g", [HSL, IFF], BF16),
        "u": nc.dram_tensor("agin_u", [HSL, IFF], BF16),
        "d": nc.dram_tensor("agin_d", [ISL, D], BF16),
        "q": nc.dram_tensor("agin_q", [HSL, D], BF16),
        "k": nc.dram_tensor("agin_k", [HSL, 512], BF16),
        "v": nc.dram_tensor("agin_v", [HSL, 512], BF16),
        "o": nc.dram_tensor("agin_o", [HSL, D], BF16),
    }
    agout = {
        "g": nc.dram_tensor("wgT", [D, IFF], BF16, addr_space="Shared"),
        "u": nc.dram_tensor("wuT", [D, IFF], BF16, addr_space="Shared"),
        "d": nc.dram_tensor("wdT", [IFF, D], BF16, addr_space="Shared"),
        "q": nc.dram_tensor("wqT", [D, D], BF16, addr_space="Shared"),
        "k": nc.dram_tensor("wkT", [D, 512], BF16, addr_space="Shared"),
        "v": nc.dram_tensor("wvT", [D, 512], BF16, addr_space="Shared"),
        "o": nc.dram_tensor("woT", [D, D], BF16, addr_space="Shared"),
    }
    kvin = nc.dram_tensor("kvin", [4 * 128, 132], F32)
    kvout = nc.dram_tensor("kvout", [16 * 128, 132], F32)

    k = K(nc)
    with k._es:
        RES = k.sb("RES", [128, NCH, D], F32)
        X1T = k.sb("X1T", [128, 8192], BF16)
        IQT = k.sb("IQT", [128, 32768], BF16)
        WBUF = k.sb("WBUF", [128, 16384], BF16)
        SCR1 = k.sb("SCR1", [128, D], F32)
        SCR2 = k.sb("SCR2", [128, D], F32)
        QF8 = k.sb("QF8", [128, D], FP8)
        QBF = k.sb("QBF", [128, D], BF16)
        SMALL = k.sb("SMALL", [128, 512], F32)
        IDENT = k.sb("IDENT", [128, 128], BF16)
        BBK = k.sb("BBK", [128, 512], F32)
        BBV = k.sb("BBV", [128, 512], F32)
        BQF = k.sb("BQF", [128, 16], F32)
        KVB = k.sb("KVB", [128, 4 * 132], BF16)
        ONESB = k.sb("ONESB", [128, 8], BF16)
        EPSC = k.sb("EPSC", [128, 2], F32)

        pa_t = k.ps("pa", [128, 4, 512], F32)
        pb_t = k.ps("pb", [128, 2, 512], F32)
        pt_t = k.ps("pt", [128, 2, 1024], BF16)

        dx = k.sem("dx")
        dxx = k.sem("dxx")
        dwp = [k.sem("dwp0"), k.sem("dwp1")]
        dgl = [k.sem("dgl0"), k.sem("dgl1")]
        ddl = [k.sem("ddl0"), k.sem("ddl1")]
        d_gS = k.sem("d_gS")
        d_uS = k.sem("d_uS")
        d_wd = [k.sem("d_wd0"), k.sem("d_wd1")]
        d_kS = k.sem("d_kS")
        d_vS = k.sem("d_vS")
        d_qo = [k.sem("d_qo0"), k.sem("d_qo1")]
        d_sh = {m: [k.sem(f"d_sh_{m}0"), k.sem(f"d_sh_{m}1")]
                for m in "gudqkvo"}
        d_kvs = k.sem("d_kvs")
        dgp = k.sem("dgp")
        ag = k.sem("ag")
        vq = k.sem("vq")
        gq = k.sem("gq")
        aq = k.sem("aq")
        pq = k.sem("pq")
        wf = k.sem("wf")
        paw = [k.sem(f"paw{i}") for i in range(4)]
        par = [k.sem(f"par{i}") for i in range(4)]
        pbw = [k.sem(f"pbw{i}") for i in range(2)]
        pbr = [k.sem(f"pbr{i}") for i in range(2)]
        ptw = [k.sem(f"ptw{i}") for i in range(2)]
        ptr = [k.sem(f"ptr{i}") for i in range(2)]
        pa = [Bank(pa_t[:, i, :], paw[i], par[i]) for i in range(4)]
        pb = [Bank(pb_t[:, i, :], pbw[i], pbr[i]) for i in range(2)]
        pt = [Bank(pt_t[:, i, :], ptw[i], ptr[i]) for i in range(2)]

        x1t_v = X1T[:].rearrange("p (c f) -> p c f", c=NHC)
        iqt_v = IQT[:].rearrange("p (c f) -> p c f", c=NIC)
        sk_v = X1T[:, 0:4096].rearrange("p (c f) -> p c f", c=8)
        vv_v = X1T[:, 4096:8192].rearrange("p (c f) -> p c f", c=8)
        kvs1 = X1T[:, 0:2112].bitcast(F32).rearrange("p (c f) -> p c f", c=8)
        kvs2 = X1T[:, 2112:4224].bitcast(F32).rearrange("p (c f) -> p c f", c=8)
        x2t_v = IQT[:, 0:16384].rearrange("p (c f) -> p c f", c=NHC)
        sqt_v = IQT[:, 16384:32768].rearrange("p (c f) -> p c f", c=NHC)
        atnT_v = x2t_v
        kvb_v = KVB[:].rearrange("p (c f) -> p c f", c=4)

        def wslot(off, nch, wid):
            return WBUF[:, off:off + nch * wid].rearrange(
                "p (c f) -> p c f", c=nch)

        gS_v = wslot(0, NHC, IB)
        uS_v = wslot(8192, NHC, IB)
        wdS_v = [wslot(0, 8, 512), wslot(4096, 8, 512)]
        kS_v = wslot(0, NHC, 512)
        vS_v = wslot(8192, NHC, 512)
        qoS_v = [wslot(0, NHC, 512), wslot(8192, NHC, 512)]

        wT_r = {m: agout[m].ap().rearrange("(c p) w -> p c w", p=128)
                for m in agout}

        S_SUM, S_SQ, S_NM, S_MSQ, S_VAR, S_STD, S_RSTD = 0, 8, 16, 24, 32, 40, 48
        S_NRM = 60
        S_AMAX, S_AE2, S_D, S_C = 64, 128, 192, 256
        S_IAM, S_IAE, S_ID, S_IC = 320, 352, 384, 416

        def sm(c0, n):
            return SMALL[:, c0:c0 + n]

        smi = SMALL[:].bitcast(I32)

        def smi_(c0, n):
            return smi[:, c0:c0 + n]

        scr2i = SCR2[:].bitcast(I32)
        marks = {}

        k.dma("sync", RES[:], x_sl.rearrange("(c p) h -> p c h", p=128), dxx)
        marks["res"] = (dxx, 16)
        k.dma("sync", IDENT[:], id_in, dx)
        k.dma("sync", BQF[:], bq_in.rearrange("(c p) -> p c", p=128), dx, slow=True)
        k.dma("sync", BBK[:], bk_in.unsqueeze(0).broadcast_to([128, 512]), dx)
        k.dma("sync", BBV[:], bv_in.unsqueeze(0).broadcast_to([128, 512]), dx)
        init_mark = (dx, dx.v)
        k.v(lambda e: e.memset(ONESB[:], 1.0), inc=vq)
        k.v(lambda e: e.memset(EPSC[:], EPS_LN), inc=vq)

        def stg(base):
            return dict(
                pb=[IQT[:, base:base + 4096].bitcast(F32),
                    IQT[:, base + 4096:base + 8192].bitcast(F32)],
                ex=IQT[:, base + 8192:base + 12288].bitcast(F32),
                m=IQT[:, base + 12288:base + 16384].bitcast(F32),
            )

        STG = [stg(0), stg(16384),
               dict(pb=[IQT[:, 0:2048].bitcast(F32),
                        IQT[:, 2048:4096].bitcast(F32)],
                    ex=IQT[:, 4096:6144].bitcast(F32),
                    m=IQT[:, 6144:8192].bitcast(F32))]
        Q_BUF = [[QBF[:, 0:2048], X1T[:, 0:2048]],
                 [IQT[:, 16384 + 8192:16384 + 10240],
                  IQT[:, 16384 + 10240:16384 + 12288]],
                 [IQT[:, 8192:9216], IQT[:, 9216:10240]]]
        T_BUF = [[SCR1[:].bitcast(BF16)[:, 0:2048],
                  SCR1[:].bitcast(BF16)[:, 2048:4096]],
                 [IQT[:, 16384 + 12288:16384 + 14336],
                  IQT[:, 16384 + 14336:16384 + 16384]],
                 [IQT[:, 10240:11264], IQT[:, 11264:12288]]]
        pp_idx = [0, 0, 0]
        PCAP = [2048, 2048, 1024]
        SMREG = [(SCR2[:], 0, 64), (SCR2[:], 1024, 64),
                 (SMALL[:], 64, 32)]

        MATS = {
            "g": (wg_sl, IFF, HSL, "w"),
            "u": (wu_sl, IFF, HSL, "w"),
            "d": (wd_sl, D, ISL, "w"),
            "q": (wq_sl, D, HSL, "c"),
            "k": (wk_sl, 512, HSL, "c"),
            "v": (wv_sl, 512, HSL, "c"),
            "o": (wo_sl, D, HSL, "c"),
        }

        def prep_quant_piece(name, p, path):
            src, rows, cols, mode = MATS[name]
            nrc = rows // 128
            prc = min(nrc, PCAP[path] // cols)
            st = STG[path]
            smt, smb, sp = SMREG[path]
            nblk = (prc * cols) // 32
            src_r = src.rearrange("(c p) w -> p c w", p=128)
            gi = pp_idx[path]
            pp_idx[path] += 1
            slot = gi % 2
            pbuf = st["pb"][slot]
            pb_v = pbuf[:, 0:prc * cols].rearrange("p (c w) -> p c w", c=prc)
            lw = [marks[f"pp_qdone_{path}_{slot}"]] \
                if f"pp_qdone_{path}_{slot}" in marks else []
            if path == 2 and gi < 2:
                lw = lw + [marks["prep_p0_done"]]
            ldsem = {0: dwp, 1: dgl, 2: ddl}[path][slot]
            if path != 1:
                k.dma("sync", pb_v, src_r[:, p * prc:(p + 1) * prc, :],
                      ldsem, waits=lw)
                ldm = (ldsem, ldsem.v)
                vf, vs = k.v, vq
            else:
                k.dma("gpsimd", pb_v, src_r[:, p * prc:(p + 1) * prc, :],
                      ldsem, waits=lw)
                ldm = (ldsem, ldsem.v)
                vf, vs = k.g, gq
            tposed = [marks[f"tposed_{path}_{slot}"]] \
                if f"tposed_{path}_{slot}" in marks else []
            if mode == "w":
                pb3 = pbuf[:, 0:prc * cols].rearrange("p (b c) -> p b c", c=32)
                smi2 = smt.bitcast(I32)
                am = smt[:, smb:smb + nblk]
                ae2 = smt[:, smb + sp:smb + sp + nblk]
                bf2 = smt[:, smb + 2 * sp:smb + 2 * sp + nblk]
                c6 = smt[:, smb + 3 * sp:smb + 3 * sp + nblk]
                ng6 = smt[:, smb + 4 * sp:smb + 4 * sp + nblk]
                ami = smi2[:, smb:smb + nblk]
                ae2i = smi2[:, smb + sp:smb + sp + nblk]
                exf = st["ex"][:, 0:prc * cols]
                pbf = pbuf[:, 0:prc * cols]
                mf = st["m"][:, 0:prc * cols]
                assert path != 1, "quantized weights must use a DVE path"
                vf(lambda e, a=am, b=pb3: e.tensor_reduce(
                    a, b, AXX, Alu.max, apply_absolute_value=True),
                    waits=[ldm])
                vf(lambda e, a=ae2i, b=ami: e.tensor_single_scalar(
                    a, b, 0x7F800000, Alu.bitwise_and))
                vf(lambda e, a=exf.bitcast(I32), b=pbf.bitcast(I32):
                   e.tensor_single_scalar(a, b, 0x7F800000, Alu.bitwise_and))
                vf(lambda e, a=ae2: e.tensor_scalar_max(a, a, TINY))
                vf(lambda e, a=bf2, b=ae2: e.tensor_scalar_mul(
                    a, b, 1.5 * 2.0 ** 20))
                vf(lambda e, a=c6, b=ae2: e.tensor_scalar_mul(a, b, 1.5))
                vf(lambda e, a=ng6, b=c6: e.tensor_scalar_mul(a, b, -1.0))
                bf2b = bf2.unsqueeze(2).broadcast_to([128, nblk, 32])
                ex3 = exf.rearrange("p (b c) -> p b c", c=32)
                m3 = mf.rearrange("p (b c) -> p b c", c=32)
                vf(lambda e, a=m3, b=ex3, c=bf2b: e.scalar_tensor_tensor(
                    a, b, MAGIC_W, c, Alu.mult, Alu.max))
                vf(lambda e, a=exf, b=pbf, c=mf: e.tensor_tensor(
                    a, b, c, Alu.add))
                vf(lambda e, a=exf, c=mf: e.tensor_tensor(
                    a, a, c, Alu.subtract))
                c6b = c6.unsqueeze(2).broadcast_to([128, nblk, 32])
                ng6b = ng6.unsqueeze(2).broadcast_to([128, nblk, 32])
                vf(lambda e, a=ex3, c=c6b: e.tensor_tensor(a, a, c, Alu.min))
                pq3 = Q_BUF[path][slot][:, 0:prc * cols].rearrange(
                    "p (b c) -> p b c", c=32)
                vf(lambda e, a=pq3, b=ex3, c=ng6b: e.tensor_tensor(
                    a, b, c, Alu.max), waits=tposed, inc=vs)
            else:
                pq_v = Q_BUF[path][slot][:, 0:prc * cols].rearrange(
                    "p (c w) -> p c w", c=prc)
                vf(lambda e, a=pq_v, b=pb_v: e.tensor_copy(a, b),
                   waits=[ldm] + tposed, inc=vs)
            marks[f"pp_qdone_{path}_{slot}"] = (vs, vs.v)
            return (vs, vs.v)

        pt_idx = [0, 0, 0]

        def prep_tail_piece(name, p, path, q_mark):
            src, rows, cols, mode = MATS[name]
            nrc = rows // 128
            prc = min(nrc, PCAP[path] // cols)
            ncc = cols // 128
            rows_piece = prc * 128
            slot = pt_idx[path] % 2
            pt_idx[path] += 1
            pq_v = Q_BUF[path][slot][:, 0:prc * cols].rearrange(
                "p (c w) -> p c w", c=prc)
            pt_v = T_BUF[path][slot][:, 0:ncc * rows_piece].rearrange(
                "p (c r) -> p c r", c=ncc)
            shipw = [marks[f"ship_{path}_{slot}"]] \
                if f"ship_{path}_{slot}" in marks else []
            for cc in range(ncc):
                for rc0 in range(0, prc, 8):
                    nb = min(8, prc - rc0)
                    b = (cc + rc0 // 8 + p) % 2
                    k.w("tensor", (pt[b].rd, pt[b].rd.v), q_mark, init_mark)
                    for j in range(nb):
                        k.t(lambda e, o=pt[b].ap[:, j * 128:(j + 1) * 128],
                            i=pq_v[:, rc0 + j, cc * 128:(cc + 1) * 128]:
                            e.transpose(o, i, IDENT[:]),
                            inc=(pt[b].wr if j == nb - 1 else None))
                    src3 = pt[b].ap[:, 0:nb * 128].rearrange(
                        "p (c f) -> p c f", c=nb)
                    k.v(lambda e, o=pt_v[:, cc, rc0 * 128:(rc0 + nb) * 128],
                        i=src3: e.tensor_copy(o, i),
                        waits=[(pt[b].wr, pt[b].wr.v)] + shipw,
                        inc=pt[b].rd)
            k.nop("tensor", pq)
            marks[f"tposed_{path}_{slot}"] = (pq, pq.v)
            k.nop("vector", vq)
            evac_mark = (vq, vq.v)
            dst = agin[name].ap().rearrange("(c p) r -> p c r", p=128)[
                :, :, p * rows_piece:(p + 1) * rows_piece]
            k.dma("scalar", dst, pt_v, d_sh[name][slot], waits=[evac_mark])
            marks[f"ship_{path}_{slot}"] = (d_sh[name][slot],
                                            d_sh[name][slot].v)
            marks[f"ship_{name}"] = [(d_sh[name][0], d_sh[name][0].v),
                                     (d_sh[name][1], d_sh[name][1].v)]

        def prep_ag(name):
            k.g(lambda e, a=agin[name], o=agout[name]: e.collective_compute(
                "AllGather", Alu.bypass,
                replica_groups=[[0, 1, 2, 3, 4, 5, 6, 7]],
                ins=[a.ap().opt()], outs=[o.ap().opt()]),
                waits=list(marks[f"ship_{name}"]) + [(ag, AGN[name] - 1)],
                inc=ag)
            marks[f"ag_{name}"] = (ag, AGN[name])

        def npieces(name, path=0):
            src, rows, cols, mode = MATS[name]
            nrc = rows // 128
            prc = min(nrc, PCAP[path] // cols)
            return nrc // prc

        p0_list = [("g", p) for p in range(npieces("g"))] + \
                  [("u", p) for p in range(npieces("u"))]
        p1_list = [("q", p) for p in range(npieces("q"))] + \
                  [("k", p) for p in range(npieces("k"))] + \
                  [("v", p) for p in range(npieces("v"))] + \
                  [("o", p) for p in range(npieces("o"))]
        q_marks = {}
        for i in range(max(len(p0_list), len(p1_list)) + 1):
            for lst, path in ((p0_list, 0), (p1_list, 1)):
                if 0 < i <= len(lst):
                    nm0, pp0 = lst[i - 1]
                    prep_tail_piece(nm0, pp0, path, q_marks[(nm0, pp0)])
                    if pp0 == npieces(nm0) - 1:
                        prep_ag(nm0)
                if i < len(lst):
                    nm, p = lst[i]
                    q_marks[(nm, p)] = prep_quant_piece(nm, p, path)


        d_units = []
        ND = npieces("d", 2)

        def mk_d_unit(p):
            def go():
                if p > 0:
                    prep_tail_piece("d", p - 1, 2, q_marks[("d", p - 1)])
                q_marks[("d", p)] = prep_quant_piece("d", p, 2)
            return go

        def mk_d_last():
            def go():
                prep_tail_piece("d", ND - 1, 2, q_marks[("d", ND - 1)])
                prep_ag("d")
                marks["iqt_lower_free"] = [
                    (pq, pq.v), (ddl[0], ddl[0].v), (ddl[1], ddl[1].v),
                    (d_sh["d"][0], d_sh["d"][0].v),
                    (d_sh["d"][1], d_sh["d"][1].v)]
            return go

        for p in range(ND):
            d_units.append(mk_d_unit(p))
        d_units.append(mk_d_last())

        marks["prep_p0_done"] = (vq, vq.v)
        marks["prep_gq"] = (gq, gq.v)
        marks["iqt_free_list"] = [(pq, pq.v), (gq, gq.v),
                                  (dwp[0], dwp[0].v), (dwp[1], dwp[1].v),
                                  (dgl[0], dgl[0].v), (dgl[1], dgl[1].v)] + \
            [(d_sh[m][j], d_sh[m][j].v) for m in "gudqkvo" for j in (0, 1)]
        marks["qbf_free"] = (pq, pq.v)
        marks["scr1_free"] = (d_sh["g"][0], d_sh["g"][0].v)
        marks["scr1_free2"] = (d_sh["g"][1], d_sh["g"][1].v)
        marks["scr1_free3"] = (d_sh["u"][0], d_sh["u"][0].v)
        marks["scr1_free4"] = (d_sh["u"][1], d_sh["u"][1].v)
        marks["phaseA_done"] = (pq, pq.v)
        marks["gS_free"] = (wf, 0)
        marks["uS_free"] = (wf, 0)

        def ln_stats(chunks):
            res_mark = marks["res"]
            n = len(chunks)
            for i, gc in enumerate(chunks):
                cv = RES[:, gc, :]
                k.v(lambda e, o=sm(S_SUM + i, 1), iv=cv: e.reduce_sum(
                    o, iv, AXX), waits=[res_mark] if i == 0 else [])
                k.a(lambda e, o=SCR1[:], iv=cv, acc=sm(S_SQ + i, 1):
                    e.activation(o, iv, Act.Square, accum_out=acc),
                    waits=([res_mark, marks.get("scr1_free", (vq, 0)),
                            marks.get("scr1_free2", (vq, 0)),
                            marks.get("scr1_free3", (vq, 0)),
                            marks.get("scr1_free4", (vq, 0))]
                           if i == 0 else []),
                    inc=(aq if i == n - 1 else None))
            k.v(lambda e, o=sm(S_NM, n), iv=sm(S_SUM, n): e.tensor_scalar_mul(
                o, iv, -1.0 / D))
            k.v(lambda e, o=sm(S_MSQ, n), iv=sm(S_NM, n): e.tensor_tensor(
                o, iv, iv, Alu.mult))
            k.v(lambda e, o=sm(S_VAR, n), i0=sm(S_SQ, n), i1=sm(S_MSQ, n):
                e.scalar_tensor_tensor(o, i0, 1.0 / D, i1, Alu.mult,
                                       Alu.subtract),
                waits=[(aq, aq.v)], inc=vq)
            k.a(lambda e, o=sm(S_STD, n), iv=sm(S_VAR, n): e.activation(
                o, iv, Act.Sqrt, bias=EPSC[:, 0:1]), waits=[(vq, vq.v)],
                inc=aq)
            k.v(lambda e, o=sm(S_RSTD, n), iv=sm(S_STD, n): e.reciprocal(
                o, iv), waits=[(aq, aq.v)])

        def xquant_chain(xv, nblk, am_c, ae_c, d_c, c_c, qf8v, qbfv,
                         qbf3=None, inc=None):
            x3 = xv.rearrange("p (b c) -> p b c", c=32)
            k.v(lambda e, o=sm(am_c, nblk), iv=x3: e.tensor_reduce(
                o, iv, AXX, Alu.max, apply_absolute_value=True))
            k.v(lambda e, o=smi_(ae_c, nblk), iv=smi_(am_c, nblk):
                e.tensor_single_scalar(o, iv, 0x7F800000, Alu.bitwise_and))
            k.v(lambda e, o=sm(ae_c, nblk): e.tensor_scalar_max(o, o, TINY))
            k.v(lambda e, o=sm(d_c, nblk), iv=sm(ae_c, nblk):
                e.tensor_scalar_mul(o, iv, 16.0))
            k.v(lambda e, o=sm(c_c, nblk), iv=sm(d_c, nblk): e.reciprocal(
                o, iv))
            cb = sm(c_c, nblk).unsqueeze(2).broadcast_to([128, nblk, 32])
            db = sm(d_c, nblk).unsqueeze(2).broadcast_to([128, nblk, 32])
            k.v(lambda e, o=x3, c=cb: e.tensor_tensor(o, o, c, Alu.mult))
            k.v(lambda e, o=xv: e.tensor_scalar(o, o, CLIP_A, -CLIP_A,
                                                Alu.min, Alu.max), inc=vq)
            k.a(lambda e, o=qf8v, iv=xv: e.copy(o, iv),
                waits=[(vq, vq.v)], inc=aq)
            f83 = qf8v.rearrange("p (b c) -> p b c", c=32)
            q3 = qbfv.rearrange("p (b c) -> p b c", c=32)
            k.v(lambda e, o=q3, iv=f83, d=db: e.tensor_tensor(
                o, iv, d, Alu.mult), waits=[(aq, aq.v)], inc=(inc or vq))

        def transpose_row(srcv, nt, dst_fn, bsel, evac_extra=()):
            for j0 in range(0, nt, 8):
                nb = min(8, nt - j0)
                b = (bsel + j0 // 8) % 2
                k.w("tensor", (pt[b].rd, pt[b].rd.v), (vq, vq.v))
                for j in range(nb):
                    k.t(lambda e, o=pt[b].ap[:, j * 128:(j + 1) * 128],
                        i=srcv[:, (j0 + j) * 128:(j0 + j + 1) * 128]:
                        e.transpose(o, i, IDENT[:]),
                        inc=(pt[b].wr if j == nb - 1 else None))
                src3 = pt[b].ap[:, 0:nb * 128].rearrange(
                    "p (c f) -> p c f", c=nb)
                k.v(lambda e, o=dst_fn(j0, nb), i=src3: e.tensor_copy(o, i),
                    waits=[(pt[b].wr, pt[b].wr.v)] + list(evac_extra),
                    inc=pt[b].rd)
            k.nop("vector", vq)

        CMAP = [(ib + 8) % NIB for ib in range(NIB)]

        def ffn_tile(t, sub, side=()):
            side = list(side)
            chunks = [t * TCH + c for c in range(TCH)]
            ln_stats(chunks)
            phA = marks["phaseA_done"]
            for c in range(TCH):
                k.w("vector", marks["qbf_free"])
                k.v(lambda e, o=SCR1[:], iv=RES[:, chunks[c], :],
                    s1=sm(S_NM + c, 1), s2=sm(S_RSTD + c, 1):
                    e.tensor_scalar(o, iv, s1, s2, Alu.add, Alu.mult))
                xquant_chain(SCR1[:], 64, S_AMAX, S_AE2, S_D, S_C,
                             QF8[:], QBF[:])
                ext = [phA] if c == 0 else []
                transpose_row(
                    QBF[:], NHC,
                    lambda j0, nb, c=c: x1t_v[:, j0:j0 + nb,
                                              c * 128:(c + 1) * 128],
                    bsel=c, evac_extra=ext)
                k.nop("tensor", pq)
                marks["qbf_free"] = (pq, pq.v)
            x1t_mark = (vq, vq.v)

            iq_marks = {}

            def emit_inter_transposes(ib):
                first = not marks.get(f"iqt_started_{t}_{sub}")
                for pr in range(2):
                    b = pr
                    qbp = QBF[:, pr * 1024:(pr + 1) * 1024]
                    k.w("tensor", (pt[b].rd, pt[b].rd.v), iq_marks[(ib, pr)])
                    nt = 0
                    for ch in range(4):
                        for sh in range(2):
                            nt += 1
                            k.t(lambda e,
                                o=pt[b].ap[:, (ch * 2 + sh) * 128:
                                           (ch * 2 + sh + 1) * 128],
                                i=qbp[:, sh * 512 + ch * 128:
                                      sh * 512 + (ch + 1) * 128]:
                                e.transpose(o, i, IDENT[:]),
                                inc=(pt[b].wr if nt == 8 else None))
                    src3 = pt[b].ap[:, 0:1024].rearrange(
                        "p (c f) -> p c f", c=4)
                    cm = CMAP[ib]
                    dst = iqt_v[:, 4 * cm:4 * cm + 4,
                                2 * pr * 128:(2 * pr + 2) * 128]
                    ew = [(pt[b].wr, pt[b].wr.v)]
                    if first and pr == 0:
                        ew += marks["iqt_free_list"]
                    if cm == 0 and pr == 0 and "iqt_lower_free" in marks:
                        ew += marks.pop("iqt_lower_free")
                    k.v(lambda e, o=dst, i=src3: e.tensor_copy(o, i),
                        waits=ew, inc=pt[b].rd)
                marks[f"iqt_started_{t}_{sub}"] = True
                k.nop("vector", vq)
                k.nop("tensor", pq)
                marks["qbf_free"] = (pq, pq.v)

            def emit_consume_pair(ib, pr):
                s0 = 2 * pr
                pav = pa_t[:, s0:s0 + 2, :].rearrange("p a b -> p (a b)")
                pbv = pb_t[:].rearrange("p a b -> p (a b)")
                sl1 = SCR1[:, pr * 1024:(pr + 1) * 1024]
                sl2 = SCR2[:, pr * 1024:(pr + 1) * 1024]
                qf = QF8[:, pr * 1024:(pr + 1) * 1024]
                qb = QBF[:, pr * 1024:(pr + 1) * 1024]
                k.a(lambda e, o=sl1, iv=pav: e.activation(
                    o, iv, Act.Silu),
                    waits=[(pa[s0].wr, pa[s0].wr.v),
                           (pa[s0 + 1].wr, pa[s0 + 1].wr.v),
                           marks.get(f"sl1_free_{pr}", (vq, 0))],
                    inc=pa[s0].rd)
                k.nop("scalar", pa[s0 + 1].rd)
                k.nop("scalar", aq)
                silu_mark = (aq, aq.v)
                ivw = [silu_mark, (pb[0].wr, pb[0].wr.v),
                       (pb[1].wr, pb[1].wr.v)]
                if ib == 0:
                    ivw.append(marks["qbf_free"])
                    if t == 0 and sub == 1:
                        ivw.append(marks["prep_gq"])
                k.v(lambda e, o=sl2, i0=sl1, i1=pbv: e.tensor_tensor(
                    o, i0, i1, Alu.mult), waits=ivw, inc=pb[0].rd)
                k.nop("vector", pb[1].rd)
                k.nop("vector", vq)
                marks[f"sl1_free_{pr}"] = (vq, vq.v)
                xquant_chain(sl2, 32, S_IAM, S_IAE, S_ID, S_IC, qf, qb)
                iq_marks[(ib, pr)] = (vq, vq.v)

            for ib in range(NIB):
                k.dma("sync", gS_v, wT_r["g"][:, :, ib * IB:(ib + 1) * IB],
                      d_gS, waits=[marks["gS_free"], marks["ag_g"]])
                g_ld = (d_gS, d_gS.v)
                k.dma("sync", uS_v, wT_r["u"][:, :, ib * IB:(ib + 1) * IB],
                      d_uS, waits=[marks["uS_free"], marks["ag_u"]])
                u_ld = (d_uS, d_uS.v)
                for s in range(TCH):
                    bk_ = pa[s]
                    k.w("tensor", (bk_.rd, bk_.rd.v), g_ld, x1t_mark)
                    for h in range(NHC):
                        k.t(lambda e, o=bk_.ap,
                            l=x1t_v[:, h, s * 128:(s + 1) * 128],
                            r=gS_v[:, h, :], st=(h == 0), sp=(h == NHC - 1):
                            e.matmul(o, l, r, start=st, stop=sp),
                            inc=(bk_.wr if h == NHC - 1 else None))
                k.nop("tensor", wf)
                marks["gS_free"] = (wf, wf.v)
                if ib > 0:
                    emit_inter_transposes(ib - 1)
                for _ in range(2):
                    if side:
                        side.pop(0)()
                if ib == NIB - 1:
                    while side:
                        side.pop(0)()
                for s in range(TCH):
                    bk_ = pb[s % 2]
                    k.w("tensor", (bk_.rd, bk_.rd.v), u_ld)
                    for h in range(NHC):
                        k.t(lambda e, o=bk_.ap,
                            l=x1t_v[:, h, s * 128:(s + 1) * 128],
                            r=uS_v[:, h, :], st=(h == 0), sp=(h == NHC - 1):
                            e.matmul(o, l, r, start=st, stop=sp),
                            inc=(bk_.wr if h == NHC - 1 else None))
                    if s % 2 == 1:
                        emit_consume_pair(ib, s // 2)
                k.nop("tensor", wf)
                marks["uS_free"] = (wf, wf.v)

            emit_inter_transposes(NIB - 1)
            k.nop("tensor", pq)
            marks["phaseA_done"] = (pq, pq.v)
            iqt_mark = (vq, vq.v)

            for hg in range(4):
                ldm = {}
                for pc in range(2):
                    sl = pc % 2
                    wl = [marks["gS_free"] if hg == 0 else
                          marks[f"wd_free_{sl}"], marks["ag_d"]]
                    k.dma("sync", wdS_v[sl],
                          wT_r["d"][:, pc * 8:(pc + 1) * 8,
                                    hg * 512:(hg + 1) * 512], d_wd[sl],
                          waits=wl)
                    ldm[pc] = (d_wd[sl], d_wd[sl].v)
                for pc in range(8):
                    sl = pc % 2
                    k.w("tensor", ldm[pc])
                    if pc == 0:
                        k.w("tensor", iqt_mark,
                            *[(pa[s].rd, pa[s].rd.v) for s in range(TCH)])
                    for s in range(TCH):
                        bk_ = pa[s]
                        for ic in range(8):
                            st = (pc == 0 and ic == 0)
                            sp = (pc == 7 and ic == 7)
                            lc = pc * 8 + ic
                            pchunk = 4 * CMAP[lc // 4] + lc % 4
                            k.t(lambda e, o=bk_.ap,
                                l=iqt_v[:, pchunk,
                                        s * 128:(s + 1) * 128],
                                r=wdS_v[sl][:, ic, :], st=st, sp=sp:
                                e.matmul(o, l, r, start=st, stop=sp),
                                inc=(bk_.wr if sp else None))
                    k.nop("tensor", wf)
                    marks[f"wd_free_{sl}"] = (wf, wf.v)
                    if pc + 2 < 8:
                        k.dma("sync", wdS_v[(pc + 2) % 2],
                              wT_r["d"][:, (pc + 2) * 8:(pc + 3) * 8,
                                        hg * 512:(hg + 1) * 512],
                              d_wd[(pc + 2) % 2],
                              waits=[marks[f"wd_free_{sl}"]])
                        ldm[pc + 2] = (d_wd[(pc + 2) % 2],
                                       d_wd[(pc + 2) % 2].v)
                for pr in range(2):
                    s0 = 2 * pr
                    pav3 = pa_t[:, s0:s0 + 2, :]
                    rv = RES[:, chunks[s0]:chunks[s0] + 2,
                             hg * 512:(hg + 1) * 512]
                    k.v(lambda e, o=rv, i0=pav3, i1=rv:
                        e.scalar_tensor_tensor(o, i0, 0.5, i1, Alu.mult,
                                               Alu.add),
                        waits=[(pa[s0].wr, pa[s0].wr.v),
                               (pa[s0 + 1].wr, pa[s0 + 1].wr.v)],
                        inc=pa[s0].rd)
                    k.nop("vector", pa[s0 + 1].rd)
            k.nop("vector", vq)
            marks["res"] = (vq, vq.v)
            k.nop("tensor", pq)
            marks["iqt_free_list"] = [(pq, pq.v)]
            marks["gS_free"] = (wf, wf.v)
            marks["scr1_free"] = (vq, vq.v)

        def attention():
            ln_stats(list(range(NCH)))
            for c in range(NCH):
                ew = [marks["qbf_free"]]
                k.w("vector", marks["qbf_free"])
                if c == 0:
                    k.w("vector", marks["phaseA_done"])
                k.v(lambda e, o=QBF[:], iv=RES[:, c, :], s1=sm(S_NM + c, 1),
                    s2=sm(S_RSTD + c, 1): e.tensor_scalar(
                        o, iv, s1, s2, Alu.add, Alu.mult), inc=vq)
                ext = marks["iqt_free_list"] if c == 0 else []
                transpose_row(
                    QBF[:], NHC,
                    lambda j0, nb, c=c: x2t_v[:, j0:j0 + nb,
                                              c * 128:(c + 1) * 128],
                    bsel=c, evac_extra=ext)
                k.nop("tensor", pq)
                marks["qbf_free"] = (pq, pq.v)
            x2t_mark = (vq, vq.v)
            k.nop("tensor", pq)
            x2t_pe = (pq, pq.v)

            k.dma("sync", kS_v, wT_r["k"][:], d_kS,
                  waits=[(wf, wf.v), marks["ag_k"]])
            k_ld = (d_kS, d_kS.v)
            k.dma("sync", vS_v, wT_r["v"][:], d_vS, waits=[marks["ag_v"]])
            v_ld = (d_vS, d_vS.v)

            for which, wv_, ld, outv in (("k", kS_v, k_ld, sk_v),
                                         ("v", vS_v, v_ld, vv_v)):
                for s in range(NCH):
                    bk_ = pa[s % 4]
                    k.w("tensor", (bk_.rd, bk_.rd.v), ld)
                    if which == "k" and s == 0:
                        k.w("tensor", x2t_mark)
                    for h in range(NHC):
                        k.t(lambda e, o=bk_.ap,
                            l=x2t_v[:, h, s * 128:(s + 1) * 128],
                            r=wv_[:, h, :], st=(h == 0), sp=(h == NHC - 1):
                            e.matmul(o, l, r, start=st, stop=sp),
                            inc=(bk_.wr if h == NHC - 1 else None))
                    if which == "k":
                        sl1 = SCR1[:, (s % 4) * 512:(s % 4 + 1) * 512]
                        sl2 = SCR2[:, (s % 4) * 512:(s % 4 + 1) * 512]
                        w0 = [(bk_.wr, bk_.wr.v),
                              marks.get(f"asl_{s % 4}", (vq, 0))]
                        if s == 0:
                            w0.append(marks["phaseA_done"])
                        k.v(lambda e, o=sl1, i0=bk_.ap, i1=BBK[:]:
                            e.tensor_tensor(o, i0, i1, Alu.add),
                            waits=w0, inc=bk_.rd)
                        k.v(lambda e, o=sl2, iv=sl1: e.tensor_scalar_min(
                            o, iv, 0.0), inc=vq)
                        k.a(lambda e, o=sl2, iv=sl2: e.activation(
                            o, iv, Act.Exp), waits=[(vq, vq.v)], inc=aq)
                        k.v(lambda e, o=outv[:, s, :], i0=sl1, i1=sl2:
                            e.scalar_tensor_tensor(o, i0, 0.0, i1, Alu.max,
                                                   Alu.add),
                            waits=[(aq, aq.v)], inc=vq)
                        marks[f"asl_{s % 4}"] = (vq, vq.v)
                    else:
                        k.v(lambda e, o=outv[:, s, :], i0=bk_.ap, i1=BBV[:]:
                            e.tensor_tensor(o, i0, i1, Alu.add),
                            waits=[(bk_.wr, bk_.wr.v)], inc=bk_.rd)
                k.nop("tensor", wf)
            k.nop("vector", vq)
            skv_mark = (vq, vq.v)

            k.v(lambda e: e.memset(SCR1[:, 0:528], 0.0),
                waits=[marks.get("asl_0", (vq, 0))], inc=vq)
            z_mark = (vq, vq.v)
            for g_ in range(KVH):
                bk_ = pb[g_ % 2]
                k.w("tensor", (bk_.rd, bk_.rd.v), skv_mark)
                for s in range(NCH):
                    k.t(lambda e, o=bk_.ap[:, 0:128],
                        l=sk_v[:, s, g_ * 128:(g_ + 1) * 128],
                        r=vv_v[:, s, g_ * 128:(g_ + 1) * 128], st=(s == 0):
                        e.matmul(o, l, r, start=st, stop=False))
                for s in range(NCH):
                    k.t(lambda e, o=bk_.ap[:, 128:129],
                        l=sk_v[:, s, g_ * 128:(g_ + 1) * 128],
                        r=ONESB[:, 0:1], sp=(s == NCH - 1):
                        e.matmul(o, l, r, start=False, stop=sp),
                        inc=(bk_.wr if s == NCH - 1 else None))
                k.v(lambda e, o=SCR1[:, g_ * 132:g_ * 132 + 129],
                    i=bk_.ap[:, 0:129]: e.tensor_copy(o, i),
                    waits=[(bk_.wr, bk_.wr.v), z_mark], inc=bk_.rd)
            k.nop("vector", vq)
            kvp_mark = (vq, vq.v)
            k.nop("tensor", pq)
            kvmm_pe = (pq, pq.v)
            marks["phaseA_done"] = kvmm_pe
            k.dma("gpsimd",
                  kvin.ap().rearrange("(g p) c -> p g c", p=128),
                  SCR1[:, 0:528].rearrange("p (g c) -> p g c", c=132),
                  d_kvs, waits=[kvp_mark])
            kv_ship = (d_kvs, d_kvs.v)
            k.g(lambda e: e.collective_compute(
                "AllGather", Alu.bypass,
                replica_groups=[[0, 1, 2, 3], [4, 5, 6, 7]],
                ins=[kvin.ap().opt()], outs=[kvout.ap().opt()]),
                waits=[kv_ship, (ag, AGN["kv"] - 1)], inc=ag)
            kv_ag = (ag, ag.v)

            qo_ld = {}
            for dqb in range(2):
                k.dma("sync", qoS_v[dqb % 2],
                      wT_r["q"][:, :, dqb * 512:(dqb + 1) * 512],
                      d_qo[dqb % 2], waits=[(wf, wf.v), marks["ag_q"]])
                qo_ld[dqb] = (d_qo[dqb % 2], d_qo[dqb % 2].v)
            for dqb in range(4):
                sl = dqb % 2
                for c2 in range(4):
                    dq = dqb * 4 + c2
                    for half in range(2):
                        bk_ = pa[(2 * dq + half) % 4]
                        k.w("tensor", (bk_.rd, bk_.rd.v), qo_ld[dqb])
                        for h in range(NHC):
                            k.t(lambda e, o=bk_.ap,
                                l=qoS_v[sl][:, h, c2 * 128:(c2 + 1) * 128],
                                r=x2t_v[:, h, half * 512:(half + 1) * 512],
                                st=(h == 0), sp=(h == NHC - 1):
                                e.matmul(o, l, r, start=st, stop=sp),
                                inc=(bk_.wr if h == NHC - 1 else None))
                        j = (2 * dq + half) % 4
                        sl1 = SCR1[:, j * 512:(j + 1) * 512]
                        sl2 = SCR2[:, j * 512:(j + 1) * 512]
                        w0 = [(bk_.wr, bk_.wr.v),
                              marks.get(f"asl_{j}", (vq, 0))]
                        if dqb == 0 and c2 == 0 and half == 0:
                            w0.append(kv_ship)
                        k.v(lambda e, o=sl1, iv=bk_.ap,
                            s1=BQF[:, dq:dq + 1]: e.tensor_scalar_add(
                                o, iv, s1), waits=w0, inc=bk_.rd)
                        k.v(lambda e, o=sl2, iv=sl1: e.tensor_scalar_min(
                            o, iv, 0.0), inc=vq)
                        k.a(lambda e, o=sl2, iv=sl2: e.activation(
                            o, iv, Act.Exp), waits=[(vq, vq.v)], inc=aq)
                        k.v(lambda e,
                            o=sqt_v[:, dq, half * 512:(half + 1) * 512],
                            i0=sl1, i1=sl2: e.scalar_tensor_tensor(
                                o, i0, 0.0, i1, Alu.max, Alu.add),
                            waits=[(aq, aq.v)], inc=vq)
                        marks[f"asl_{j}"] = (vq, vq.v)
                k.nop("tensor", wf)
                marks[f"qo_free_{sl}"] = (wf, wf.v)
                if dqb + 2 < 4:
                    k.dma("sync", qoS_v[sl],
                          wT_r["q"][:, :, (dqb + 2) * 512:(dqb + 3) * 512],
                          d_qo[sl], waits=[marks[f"qo_free_{sl}"]])
                    qo_ld[dqb + 2] = (d_qo[sl], d_qo[sl].v)
            sqt_mark = (vq, vq.v)

            k.dma("gpsimd", kvs1,
                  kvout.ap().rearrange("(b p) c -> p b c", p=128)[:, 0:8, :],
                  dgp, waits=[kv_ag, kvmm_pe])
            k.dma("gpsimd", kvs2,
                  kvout.ap().rearrange("(b p) c -> p b c", p=128)[:, 8:16, :],
                  dgp)
            kvg_ld = (dgp, dgp.v)
            k.v(lambda e: e.tensor_tensor(kvs1[:, 0:4, :], kvs1[:, 0:4, :],
                                          kvs1[:, 4:8, :], Alu.add),
                waits=[kvg_ld])
            k.v(lambda e: e.tensor_tensor(kvs2[:, 0:4, :], kvs2[:, 0:4, :],
                                          kvs2[:, 4:8, :], Alu.add))
            k.v(lambda e: e.tensor_tensor(kvs1[:, 0:4, :], kvs1[:, 0:4, :],
                                          kvs2[:, 0:4, :], Alu.add))
            k.v(lambda e: e.tensor_copy(kvb_v[:], kvs1[:, 0:4, :]), inc=vq)
            kvb_mark = (vq, vq.v)

            k.nop("tensor", pq)
            qproj_pe = (pq, pq.v)
            for s in range(NCH):
                for h in range(NH):
                    bk_ = pb[h % 2]
                    g_ = h // 4
                    w0 = [(bk_.rd, bk_.rd.v)]
                    if s == 0 and h == 0:
                        w0 += [kvb_mark, sqt_mark]
                    k.w("tensor", *w0)
                    k.t(lambda e, o=bk_.ap[:, 0:132],
                        l=sqt_v[:, h, s * 128:(s + 1) * 128],
                        r=kvb_v[:, g_, :]: e.matmul(o, l, r, start=True,
                                                    stop=True), inc=bk_.wr)
                    w1 = [(bk_.wr, bk_.wr.v)]
                    if h == 0:
                        w1.append(marks["qbf_free"])
                    k.v(lambda e, o=sm(S_NRM, 1), iv=bk_.ap[:, 128:129]:
                        e.tensor_scalar_add(o, iv, EPS_AT), waits=w1)
                    k.v(lambda e, o=sm(S_NRM, 1), iv=sm(S_NRM, 1):
                        e.reciprocal(o, iv))
                    k.v(lambda e, o=QBF[:, h * 128:(h + 1) * 128],
                        iv=bk_.ap[:, 0:128], s1=sm(S_NRM, 1):
                        e.tensor_scalar_mul(o, iv, s1), inc=bk_.rd)
                k.nop("vector", vq)
                ext = [qproj_pe] if s == 0 else []
                transpose_row(
                    QBF[:], NH,
                    lambda j0, nb, s=s: atnT_v[:, j0:j0 + nb,
                                               s * 128:(s + 1) * 128],
                    bsel=s, evac_extra=ext)
                k.nop("tensor", pq)
                marks["qbf_free"] = (pq, pq.v)
            atn_mark = (vq, vq.v)

            ldo = {}
            for hob in range(2):
                k.dma("sync", qoS_v[hob % 2],
                      wT_r["o"][:, :, hob * 512:(hob + 1) * 512],
                      d_qo[hob % 2],
                      waits=[marks[f"qo_free_{hob % 2}"], marks["ag_o"]])
                ldo[hob] = (d_qo[hob % 2], d_qo[hob % 2].v)
            for hob in range(4):
                sl = hob % 2
                for s in range(NCH):
                    bk_ = pa[s % 4]
                    w0 = [(bk_.rd, bk_.rd.v), ldo[hob]]
                    if hob == 0 and s == 0:
                        w0.append(atn_mark)
                    k.w("tensor", *w0)
                    for h in range(NHC):
                        k.t(lambda e, o=bk_.ap,
                            l=atnT_v[:, h, s * 128:(s + 1) * 128],
                            r=qoS_v[sl][:, h, :], st=(h == 0),
                            sp=(h == NHC - 1):
                            e.matmul(o, l, r, start=st, stop=sp),
                            inc=(bk_.wr if h == NHC - 1 else None))
                    rv = RES[:, s, hob * 512:(hob + 1) * 512]
                    k.v(lambda e, o=rv, i0=bk_.ap, i1=rv: e.tensor_tensor(
                        o, i0, i1, Alu.add),
                        waits=[(bk_.wr, bk_.wr.v)], inc=bk_.rd)
                k.nop("tensor", wf)
                marks[f"qo_free_{sl}"] = (wf, wf.v)
                if hob + 2 < 4:
                    k.dma("sync", qoS_v[sl],
                          wT_r["o"][:, :, (hob + 2) * 512:(hob + 3) * 512],
                          d_qo[sl], waits=[marks[f"qo_free_{sl}"]])
                    ldo[hob + 2] = (d_qo[sl], d_qo[sl].v)
            k.nop("vector", vq)
            marks["res"] = (vq, vq.v)
            marks["scr1_free"] = (vq, vq.v)
            k.nop("tensor", pq)
            marks["iqt_free_list"] = [(pq, pq.v)]
            marks["gS_free"] = marks["qo_free_0"]
            marks["uS_free"] = marks["qo_free_1"]

        ffn_tile(0, 1, d_units)
        ffn_tile(1, 1)
        attention()
        ffn_tile(0, 3)
        ffn_tile(1, 3)

        k.dma("gpsimd", out.rearrange("(c p) h -> p c h", p=128), RES[:],
              dgp, waits=[marks["res"]])
        k.w("gpsimd", (dgp, dgp.v))

        with nc.Block() as block:
            @block.sync
            def _(e):
                for fn in k.q["sync"]:
                    fn(e)

            @block.scalar
            def _(e):
                for fn in k.q["scalar"]:
                    fn(e)

            @block.vector
            def _(e):
                for fn in k.q["vector"]:
                    fn(e)

            @block.gpsimd
            def _(e):
                for fn in k.q["gpsimd"]:
                    fn(e)

            @block.tensor
            def _(e):
                for fn in k.q["tensor"]:
                    fn(e)
    return nc


_NC_CACHE = None


def kernel(L0_attn_hidden_states, L0_attn_Wq, L0_attn_bq, L0_attn_Wk,
           L0_attn_bk, L0_attn_Wv, L0_attn_bv, L0_attn_Wo, L0_ffn_w_gate,
           L0_ffn_w_up, L0_ffn_w_down):
    global _NC_CACHE
    x = np.ascontiguousarray(np.asarray(L0_attn_hidden_states, np.float32))
    Wq = np.asarray(L0_attn_Wq, np.float32)
    Wk = np.asarray(L0_attn_Wk, np.float32)
    Wv = np.asarray(L0_attn_Wv, np.float32)
    Wo = np.asarray(L0_attn_Wo, np.float32)
    Wg = np.asarray(L0_ffn_w_gate, np.float32)
    Wu = np.asarray(L0_ffn_w_up, np.float32)
    Wd = np.asarray(L0_ffn_w_down, np.float32)
    bq = np.asarray(L0_attn_bq, np.float32)
    bk = np.asarray(L0_attn_bk, np.float32)
    bv = np.asarray(L0_attn_bv, np.float32)
    ident = np.eye(128, dtype=ml_dtypes.bfloat16)

    in_maps = []
    for c in range(8):
        b, tc = divmod(c, 4)
        in_maps.append({
            "x_sl": np.ascontiguousarray(x[b, tc * NTOK:(tc + 1) * NTOK, :]),
            "wg_sl": np.ascontiguousarray(Wg[:, c * HSL:(c + 1) * HSL]),
            "wu_sl": np.ascontiguousarray(Wu[:, c * HSL:(c + 1) * HSL]),
            "wd_sl": np.ascontiguousarray(Wd[:, c * ISL:(c + 1) * ISL]),
            "wq_sl": np.ascontiguousarray(Wq[:, c * HSL:(c + 1) * HSL]),
            "wk_sl": np.ascontiguousarray(Wk[:, c * HSL:(c + 1) * HSL]),
            "wv_sl": np.ascontiguousarray(Wv[:, c * HSL:(c + 1) * HSL]),
            "wo_sl": np.ascontiguousarray(Wo[:, c * HSL:(c + 1) * HSL]),
            "bq_in": bq, "bk_in": bk, "bv_in": bv, "id_in": ident,
        })
    if _NC_CACHE is None:
        _NC_CACHE = build_nc()
    res = run_bass_kernel_spmd(_NC_CACHE, in_maps, core_ids=list(range(8)))
    outf = np.empty((2, 4096, D), np.float32)
    for c in range(8):
        b, tc = divmod(c, 4)
        outf[b, tc * NTOK:(tc + 1) * NTOK, :] = res.results[c]["out"]
    return outf


# version 11 (final)
# speedup vs baseline: 1.1144x; 1.0214x over previous
import numpy as np
import ml_dtypes
from contextlib import ExitStack

import concourse.bass as bass
import concourse.mybir as mybir
from concourse.bass_utils import run_bass_kernel_spmd

F32 = mybir.dt.float32
BF16 = mybir.dt.bfloat16
FP8 = mybir.dt.float8e4
I32 = mybir.dt.int32
Alu = mybir.AluOpType
Act = mybir.ActivationFunctionType
AXX = mybir.AxisListType.X

D = 2048
IFF = 8192
NTOK = 1024
NCH = 8
TCH = 4
NHC = 16
NIC = 64
IB = 512
NIB = IFF // IB
HSL = D // 8
ISL = IFF // 8
NH, KVH, HD = 16, 4, 128
EPS_LN = 1e-5
EPS_AT = 1e-6
CLIP_A = 15.0 / 128.0
MAGIC_W = 1.5 * 2.0 ** 22
TINY = 2.0 ** -100
AGN = {"q": 1, "k": 2, "v": 3, "o": 4, "g": 5, "u": 6, "d": 7, "kv": 8}

ENGS = ("sync", "scalar", "vector", "gpsimd", "tensor")


class Sem:
    __slots__ = ("h", "v", "name")

    def __init__(self, h, name):
        self.h = h
        self.v = 0
        self.name = name

    def inc(self, n=1):
        self.v += n
        return self.v


class Bank:
    def __init__(self, ap, wr, rd):
        self.ap = ap
        self.wr = wr
        self.rd = rd


class K:
    def __init__(self, nc):
        self.nc = nc
        self.q = {e: [] for e in ENGS}
        self._es = ExitStack()

    def sem(self, name):
        return Sem(self._es.enter_context(self.nc.semaphore(name)), name)

    def sb(self, name, shape, dt):
        return self._es.enter_context(self.nc.sbuf_tensor(name, shape, dt))

    def ps(self, name, shape, dt):
        return self._es.enter_context(self.nc.psum_tensor(name, shape, dt))

    def o(self, eng, fn, waits=(), inc=None, n=1, drain=False):
        wl = tuple((s.h, int(v)) for (s, v) in waits if v > 0)
        if inc is not None:
            inc.inc(n)
            semh = inc.h

            def run(e, fn=fn, wl=wl, semh=semh, n=n, drain=drain):
                for h, v in wl:
                    e.wait_ge(h, v)
                fn(e).then_inc(semh, n)
                if drain:
                    e.drain()
        else:
            def run(e, fn=fn, wl=wl, drain=drain):
                for h, v in wl:
                    e.wait_ge(h, v)
                fn(e)
                if drain:
                    e.drain()
        self.q[eng].append(run)

    def w(self, eng, *waits):
        wl = tuple((s.h, int(v)) for (s, v) in waits if v > 0)
        if not wl:
            return

        def run(e, wl=wl):
            for h, v in wl:
                e.wait_ge(h, v)
        self.q[eng].append(run)

    def nop(self, eng, inc, n=1, waits=()):
        self.q[eng].append(lambda e: e.drain())
        self.o(eng, lambda e: e.nop(), waits=waits, inc=inc, n=n)

    def v(self, fn, waits=(), inc=None, n=1):
        self.o("vector", fn, waits=waits, inc=inc, n=n, drain=True)

    def a(self, fn, waits=(), inc=None, n=1):
        self.o("scalar", fn, waits=waits, inc=inc, n=n, drain=True)

    def g(self, fn, waits=(), inc=None, n=1):
        self.o("gpsimd", fn, waits=waits, inc=inc, n=n, drain=True)

    def t(self, fn, waits=(), inc=None, n=1):
        self.o("tensor", fn, waits=waits, inc=inc, n=n)

    def dma(self, eng, out, in_, sem, waits=(), slow=False):
        if slow:
            self.o(eng, lambda e, o=out, i=in_: e.dma_start(
                out=o, in_=i, allow_slow_non_contiguous=True),
                waits=waits, inc=sem, n=16)
        else:
            self.o(eng, lambda e, o=out, i=in_: e.dma_start(out=o, in_=i),
                   waits=waits, inc=sem, n=16)


def build_nc():
    nc = bass.Bass("TRN2", target_bir_lowering=False, debug=False)

    x_sl = nc.dram_tensor("x_sl", [NTOK, D], F32, kind="ExternalInput").ap()
    wg_sl = nc.dram_tensor("wg_sl", [IFF, HSL], F32, kind="ExternalInput").ap()
    wu_sl = nc.dram_tensor("wu_sl", [IFF, HSL], F32, kind="ExternalInput").ap()
    wd_sl = nc.dram_tensor("wd_sl", [D, ISL], F32, kind="ExternalInput").ap()
    wq_sl = nc.dram_tensor("wq_sl", [D, HSL], F32, kind="ExternalInput").ap()
    wk_sl = nc.dram_tensor("wk_sl", [512, HSL], F32, kind="ExternalInput").ap()
    wv_sl = nc.dram_tensor("wv_sl", [512, HSL], F32, kind="ExternalInput").ap()
    wo_sl = nc.dram_tensor("wo_sl", [D, HSL], F32, kind="ExternalInput").ap()
    bq_in = nc.dram_tensor("bq_in", [D], F32, kind="ExternalInput").ap()
    bk_in = nc.dram_tensor("bk_in", [512], F32, kind="ExternalInput").ap()
    bv_in = nc.dram_tensor("bv_in", [512], F32, kind="ExternalInput").ap()
    id_in = nc.dram_tensor("id_in", [128, 128], BF16, kind="ExternalInput").ap()
    out = nc.dram_tensor("out", [NTOK, D], F32, kind="ExternalOutput").ap()

    agin = {
        "g": nc.dram_tensor("agin_g", [HSL, IFF], BF16),
        "u": nc.dram_tensor("agin_u", [HSL, IFF], BF16),
        "d": nc.dram_tensor("agin_d", [ISL, D], BF16),
        "q": nc.dram_tensor("agin_q", [HSL, D], BF16),
        "k": nc.dram_tensor("agin_k", [HSL, 512], BF16),
        "v": nc.dram_tensor("agin_v", [HSL, 512], BF16),
        "o": nc.dram_tensor("agin_o", [HSL, D], BF16),
    }
    agout = {
        "g": nc.dram_tensor("wgT", [D, IFF], BF16, addr_space="Shared"),
        "u": nc.dram_tensor("wuT", [D, IFF], BF16, addr_space="Shared"),
        "d": nc.dram_tensor("wdT", [IFF, D], BF16, addr_space="Shared"),
        "q": nc.dram_tensor("wqT", [D, D], BF16, addr_space="Shared"),
        "k": nc.dram_tensor("wkT", [D, 512], BF16, addr_space="Shared"),
        "v": nc.dram_tensor("wvT", [D, 512], BF16, addr_space="Shared"),
        "o": nc.dram_tensor("woT", [D, D], BF16, addr_space="Shared"),
    }
    kvin = nc.dram_tensor("kvin", [4 * 128, 132], F32)
    kvout = nc.dram_tensor("kvout", [16 * 128, 132], F32)

    k = K(nc)
    with k._es:
        RES = k.sb("RES", [128, NCH, D], F32)
        X1T = k.sb("X1T", [128, 8192], BF16)
        IQT = k.sb("IQT", [128, 32768], BF16)
        WBUF = k.sb("WBUF", [128, 16384], BF16)
        SCR1 = k.sb("SCR1", [128, D], F32)
        SCR2 = k.sb("SCR2", [128, D], F32)
        QF8 = k.sb("QF8", [128, D], FP8)
        QBF = k.sb("QBF", [128, D], BF16)
        SMALL = k.sb("SMALL", [128, 512], F32)
        IDENT = k.sb("IDENT", [128, 128], BF16)
        BBK = k.sb("BBK", [128, 512], F32)
        BBV = k.sb("BBV", [128, 512], F32)
        BQF = k.sb("BQF", [128, 16], F32)
        KVB = k.sb("KVB", [128, 4 * 132], BF16)
        ONESB = k.sb("ONESB", [128, 8], BF16)
        EPSC = k.sb("EPSC", [128, 2], F32)

        pa_t = k.ps("pa", [128, 4, 512], F32)
        pb_t = k.ps("pb", [128, 2, 512], F32)
        pt_t = k.ps("pt", [128, 2, 1024], BF16)

        dx = k.sem("dx")
        dxx = k.sem("dxx")
        dwp = [k.sem("dwp0"), k.sem("dwp1")]
        dgl = [k.sem("dgl0"), k.sem("dgl1")]
        ddl = [k.sem("ddl0"), k.sem("ddl1")]
        d_gS = k.sem("d_gS")
        d_uS = k.sem("d_uS")
        d_wd = [k.sem("d_wd0"), k.sem("d_wd1")]
        d_kS = k.sem("d_kS")
        d_vS = k.sem("d_vS")
        d_qo = [k.sem("d_qo0"), k.sem("d_qo1")]
        d_sh = {m: [k.sem(f"d_sh_{m}0"), k.sem(f"d_sh_{m}1")]
                for m in "gudqkvo"}
        d_kvs = k.sem("d_kvs")
        dgp = k.sem("dgp")
        ag = k.sem("ag")
        vq = k.sem("vq")
        gq = k.sem("gq")
        aq = k.sem("aq")
        pq = k.sem("pq")
        wf = k.sem("wf")
        paw = [k.sem(f"paw{i}") for i in range(4)]
        par = [k.sem(f"par{i}") for i in range(4)]
        pbw = [k.sem(f"pbw{i}") for i in range(2)]
        pbr = [k.sem(f"pbr{i}") for i in range(2)]
        ptw = [k.sem(f"ptw{i}") for i in range(2)]
        ptr = [k.sem(f"ptr{i}") for i in range(2)]
        pa = [Bank(pa_t[:, i, :], paw[i], par[i]) for i in range(4)]
        pb = [Bank(pb_t[:, i, :], pbw[i], pbr[i]) for i in range(2)]
        pt = [Bank(pt_t[:, i, :], ptw[i], ptr[i]) for i in range(2)]

        x1t_v = X1T[:].rearrange("p (c f) -> p c f", c=NHC)
        iqt_v = IQT[:].rearrange("p (c f) -> p c f", c=NIC)
        sk_v = X1T[:, 0:4096].rearrange("p (c f) -> p c f", c=8)
        vv_v = X1T[:, 4096:8192].rearrange("p (c f) -> p c f", c=8)
        kvs1 = X1T[:, 0:2112].bitcast(F32).rearrange("p (c f) -> p c f", c=8)
        kvs2 = X1T[:, 2112:4224].bitcast(F32).rearrange("p (c f) -> p c f", c=8)
        x2t_v = IQT[:, 0:16384].rearrange("p (c f) -> p c f", c=NHC)
        sqt_v = IQT[:, 16384:32768].rearrange("p (c f) -> p c f", c=NHC)
        atnT_v = x2t_v
        kvb_v = KVB[:].rearrange("p (c f) -> p c f", c=4)

        def wslot(off, nch, wid):
            return WBUF[:, off:off + nch * wid].rearrange(
                "p (c f) -> p c f", c=nch)

        gS_v = wslot(0, NHC, IB)
        uS_v = wslot(8192, NHC, IB)
        wdS_v = [wslot(0, 8, 512), wslot(4096, 8, 512)]
        kS_v = wslot(0, NHC, 512)
        vS_v = wslot(8192, NHC, 512)
        qoS_v = [wslot(0, NHC, 512), wslot(8192, NHC, 512)]

        wT_r = {m: agout[m].ap().rearrange("(c p) w -> p c w", p=128)
                for m in agout}

        S_SUM, S_SQ, S_NM, S_MSQ, S_VAR, S_STD, S_RSTD = 0, 8, 16, 24, 32, 40, 48
        S_NRM = 60
        S_AMAX, S_AE2, S_D, S_C = 64, 128, 192, 256
        S_IAM, S_IAE, S_ID, S_IC = 320, 352, 384, 416

        def sm(c0, n):
            return SMALL[:, c0:c0 + n]

        smi = SMALL[:].bitcast(I32)

        def smi_(c0, n):
            return smi[:, c0:c0 + n]

        scr2i = SCR2[:].bitcast(I32)
        marks = {}

        k.dma("sync", RES[:], x_sl.rearrange("(c p) h -> p c h", p=128), dxx)
        marks["res"] = (dxx, 16)
        k.dma("sync", IDENT[:], id_in, dx)
        k.dma("sync", BQF[:], bq_in.rearrange("(c p) -> p c", p=128), dx, slow=True)
        k.dma("sync", BBK[:], bk_in.unsqueeze(0).broadcast_to([128, 512]), dx)
        k.dma("sync", BBV[:], bv_in.unsqueeze(0).broadcast_to([128, 512]), dx)
        init_mark = (dx, dx.v)
        k.v(lambda e: e.memset(ONESB[:], 1.0), inc=vq)
        k.v(lambda e: e.memset(EPSC[:], EPS_LN), inc=vq)

        def stg(base):
            return dict(
                pb=[IQT[:, base:base + 4096].bitcast(F32),
                    IQT[:, base + 4096:base + 8192].bitcast(F32)],
                ex=IQT[:, base + 8192:base + 12288].bitcast(F32),
                m=IQT[:, base + 12288:base + 16384].bitcast(F32),
            )

        STG = [stg(0), stg(16384),
               dict(pb=[IQT[:, 0:2048].bitcast(F32),
                        IQT[:, 2048:4096].bitcast(F32)],
                    ex=IQT[:, 4096:6144].bitcast(F32),
                    m=IQT[:, 6144:8192].bitcast(F32))]
        Q_BUF = [[QBF[:, 0:2048], X1T[:, 0:2048]],
                 [IQT[:, 16384 + 8192:16384 + 10240],
                  IQT[:, 16384 + 10240:16384 + 12288]],
                 [IQT[:, 8192:9216], IQT[:, 9216:10240]]]
        T_BUF = [[SCR1[:].bitcast(BF16)[:, 0:2048],
                  SCR1[:].bitcast(BF16)[:, 2048:4096]],
                 [IQT[:, 16384 + 12288:16384 + 14336],
                  IQT[:, 16384 + 14336:16384 + 16384]],
                 [IQT[:, 10240:11264], IQT[:, 11264:12288]]]
        pp_idx = [0, 0, 0]
        PCAP = [2048, 2048, 1024]
        SMREG = [(SCR2[:], 0, 64), (SCR2[:], 1024, 64),
                 (SMALL[:], 64, 32)]

        MATS = {
            "g": (wg_sl, IFF, HSL, "w"),
            "u": (wu_sl, IFF, HSL, "w"),
            "d": (wd_sl, D, ISL, "w"),
            "q": (wq_sl, D, HSL, "c"),
            "k": (wk_sl, 512, HSL, "c"),
            "v": (wv_sl, 512, HSL, "c"),
            "o": (wo_sl, D, HSL, "c"),
        }

        def prep_quant_piece(name, p, path):
            src, rows, cols, mode = MATS[name]
            nrc = rows // 128
            prc = min(nrc, PCAP[path] // cols)
            st = STG[path]
            smt, smb, sp = SMREG[path]
            nblk = (prc * cols) // 32
            src_r = src.rearrange("(c p) w -> p c w", p=128)
            gi = pp_idx[path]
            pp_idx[path] += 1
            slot = gi % 2
            pbuf = st["pb"][slot]
            pb_v = pbuf[:, 0:prc * cols].rearrange("p (c w) -> p c w", c=prc)
            lw = [marks[f"pp_qdone_{path}_{slot}"]] \
                if f"pp_qdone_{path}_{slot}" in marks else []
            if path == 2 and gi < 2:
                lw = lw + [marks["prep_p0_done"]]
            ldsem = {0: dwp, 1: dgl, 2: ddl}[path][slot]
            if path != 1:
                k.dma("sync", pb_v, src_r[:, p * prc:(p + 1) * prc, :],
                      ldsem, waits=lw)
                ldm = (ldsem, ldsem.v)
                vf, vs = k.v, vq
            else:
                k.dma("gpsimd", pb_v, src_r[:, p * prc:(p + 1) * prc, :],
                      ldsem, waits=lw)
                ldm = (ldsem, ldsem.v)
                vf, vs = k.g, gq
            tposed = [marks[f"tposed_{path}_{slot}"]] \
                if f"tposed_{path}_{slot}" in marks else []
            if mode == "w":
                pb3 = pbuf[:, 0:prc * cols].rearrange("p (b c) -> p b c", c=32)
                smi2 = smt.bitcast(I32)
                am = smt[:, smb:smb + nblk]
                ae2 = smt[:, smb + sp:smb + sp + nblk]
                bf2 = smt[:, smb + 2 * sp:smb + 2 * sp + nblk]
                c6 = smt[:, smb + 3 * sp:smb + 3 * sp + nblk]
                ng6 = smt[:, smb + 4 * sp:smb + 4 * sp + nblk]
                ami = smi2[:, smb:smb + nblk]
                ae2i = smi2[:, smb + sp:smb + sp + nblk]
                exf = st["ex"][:, 0:prc * cols]
                pbf = pbuf[:, 0:prc * cols]
                mf = st["m"][:, 0:prc * cols]
                assert path != 1, "quantized weights must use a DVE path"
                vf(lambda e, a=am, b=pb3: e.tensor_reduce(
                    a, b, AXX, Alu.max, apply_absolute_value=True),
                    waits=[ldm])
                vf(lambda e, a=ae2i, b=ami: e.tensor_single_scalar(
                    a, b, 0x7F800000, Alu.bitwise_and))
                vf(lambda e, a=exf.bitcast(I32), b=pbf.bitcast(I32):
                   e.tensor_single_scalar(a, b, 0x7F800000, Alu.bitwise_and))
                vf(lambda e, a=ae2: e.tensor_scalar_max(a, a, TINY))
                vf(lambda e, a=bf2, b=ae2: e.tensor_scalar_mul(
                    a, b, 1.5 * 2.0 ** 20))
                vf(lambda e, a=c6, b=ae2: e.tensor_scalar_mul(a, b, 1.5))
                vf(lambda e, a=ng6, b=c6: e.tensor_scalar_mul(a, b, -1.0))
                bf2b = bf2.unsqueeze(2).broadcast_to([128, nblk, 32])
                ex3 = exf.rearrange("p (b c) -> p b c", c=32)
                m3 = mf.rearrange("p (b c) -> p b c", c=32)
                vf(lambda e, a=m3, b=ex3, c=bf2b: e.scalar_tensor_tensor(
                    a, b, MAGIC_W, c, Alu.mult, Alu.max))
                vf(lambda e, a=exf, b=pbf, c=mf: e.tensor_tensor(
                    a, b, c, Alu.add))
                vf(lambda e, a=exf, c=mf: e.tensor_tensor(
                    a, a, c, Alu.subtract))
                c6b = c6.unsqueeze(2).broadcast_to([128, nblk, 32])
                ng6b = ng6.unsqueeze(2).broadcast_to([128, nblk, 32])
                vf(lambda e, a=ex3, c=c6b: e.tensor_tensor(a, a, c, Alu.min))
                pq3 = Q_BUF[path][slot][:, 0:prc * cols].rearrange(
                    "p (b c) -> p b c", c=32)
                vf(lambda e, a=pq3, b=ex3, c=ng6b: e.tensor_tensor(
                    a, b, c, Alu.max), waits=tposed, inc=vs)
            else:
                pq_v = Q_BUF[path][slot][:, 0:prc * cols].rearrange(
                    "p (c w) -> p c w", c=prc)
                vf(lambda e, a=pq_v, b=pb_v: e.tensor_copy(a, b),
                   waits=[ldm] + tposed, inc=vs)
            marks[f"pp_qdone_{path}_{slot}"] = (vs, vs.v)
            return (vs, vs.v)

        pt_idx = [0, 0, 0]

        def prep_tail_piece(name, p, path, q_mark):
            src, rows, cols, mode = MATS[name]
            nrc = rows // 128
            prc = min(nrc, PCAP[path] // cols)
            ncc = cols // 128
            rows_piece = prc * 128
            slot = pt_idx[path] % 2
            pt_idx[path] += 1
            pq_v = Q_BUF[path][slot][:, 0:prc * cols].rearrange(
                "p (c w) -> p c w", c=prc)
            pt_v = T_BUF[path][slot][:, 0:ncc * rows_piece].rearrange(
                "p (c r) -> p c r", c=ncc)
            shipw = [marks[f"ship_{path}_{slot}"]] \
                if f"ship_{path}_{slot}" in marks else []
            for cc in range(ncc):
                for rc0 in range(0, prc, 8):
                    nb = min(8, prc - rc0)
                    b = (cc + rc0 // 8 + p) % 2
                    k.w("tensor", (pt[b].rd, pt[b].rd.v), q_mark, init_mark)
                    for j in range(nb):
                        k.t(lambda e, o=pt[b].ap[:, j * 128:(j + 1) * 128],
                            i=pq_v[:, rc0 + j, cc * 128:(cc + 1) * 128]:
                            e.transpose(o, i, IDENT[:]),
                            inc=(pt[b].wr if j == nb - 1 else None))
                    src3 = pt[b].ap[:, 0:nb * 128].rearrange(
                        "p (c f) -> p c f", c=nb)
                    k.v(lambda e, o=pt_v[:, cc, rc0 * 128:(rc0 + nb) * 128],
                        i=src3: e.tensor_copy(o, i),
                        waits=[(pt[b].wr, pt[b].wr.v)] + shipw,
                        inc=pt[b].rd)
            k.nop("tensor", pq)
            marks[f"tposed_{path}_{slot}"] = (pq, pq.v)
            k.nop("vector", vq)
            evac_mark = (vq, vq.v)
            dst = agin[name].ap().rearrange("(c p) r -> p c r", p=128)[
                :, :, p * rows_piece:(p + 1) * rows_piece]
            k.dma("scalar", dst, pt_v, d_sh[name][slot], waits=[evac_mark])
            marks[f"ship_{path}_{slot}"] = (d_sh[name][slot],
                                            d_sh[name][slot].v)
            marks[f"ship_{name}"] = [(d_sh[name][0], d_sh[name][0].v),
                                     (d_sh[name][1], d_sh[name][1].v)]

        def prep_ag(name):
            k.g(lambda e, a=agin[name], o=agout[name]: e.collective_compute(
                "AllGather", Alu.bypass,
                replica_groups=[[0, 1, 2, 3, 4, 5, 6, 7]],
                ins=[a.ap().opt()], outs=[o.ap().opt()]),
                waits=list(marks[f"ship_{name}"]) + [(ag, AGN[name] - 1)],
                inc=ag)
            marks[f"ag_{name}"] = (ag, AGN[name])

        def npieces(name, path=0):
            src, rows, cols, mode = MATS[name]
            nrc = rows // 128
            prc = min(nrc, PCAP[path] // cols)
            return nrc // prc

        p0_list = [("g", p) for p in range(npieces("g"))] + \
                  [("u", p) for p in range(npieces("u"))]
        p1_list = [("q", p) for p in range(npieces("q"))] + \
                  [("k", p) for p in range(npieces("k"))] + \
                  [("v", p) for p in range(npieces("v"))] + \
                  [("o", p) for p in range(npieces("o"))]
        q_marks = {}
        for i in range(max(len(p0_list), len(p1_list)) + 1):
            for lst, path in ((p0_list, 0), (p1_list, 1)):
                if 0 < i <= len(lst):
                    nm0, pp0 = lst[i - 1]
                    prep_tail_piece(nm0, pp0, path, q_marks[(nm0, pp0)])
                    if pp0 == npieces(nm0) - 1:
                        prep_ag(nm0)
                if i < len(lst):
                    nm, p = lst[i]
                    q_marks[(nm, p)] = prep_quant_piece(nm, p, path)


        d_units = []
        ND = npieces("d", 2)

        def mk_d_unit(p):
            def go():
                if p > 0:
                    prep_tail_piece("d", p - 1, 2, q_marks[("d", p - 1)])
                q_marks[("d", p)] = prep_quant_piece("d", p, 2)
            return go

        def mk_d_last():
            def go():
                prep_tail_piece("d", ND - 1, 2, q_marks[("d", ND - 1)])
                prep_ag("d")
                marks["iqt_lower_free"] = [
                    (pq, pq.v), (ddl[0], ddl[0].v), (ddl[1], ddl[1].v),
                    (d_sh["d"][0], d_sh["d"][0].v),
                    (d_sh["d"][1], d_sh["d"][1].v)]
            return go

        for p in range(ND):
            d_units.append(mk_d_unit(p))
        d_units.append(mk_d_last())

        marks["prep_p0_done"] = (vq, vq.v)
        marks["prep_gq"] = (gq, gq.v)
        marks["iqt_free_list"] = [(pq, pq.v), (gq, gq.v),
                                  (dwp[0], dwp[0].v), (dwp[1], dwp[1].v),
                                  (dgl[0], dgl[0].v), (dgl[1], dgl[1].v)] + \
            [(d_sh[m][j], d_sh[m][j].v) for m in "gudqkvo" for j in (0, 1)]
        marks["qbf_free"] = (pq, pq.v)
        marks["scr1_free"] = (d_sh["g"][0], d_sh["g"][0].v)
        marks["scr1_free2"] = (d_sh["g"][1], d_sh["g"][1].v)
        marks["scr1_free3"] = (d_sh["u"][0], d_sh["u"][0].v)
        marks["scr1_free4"] = (d_sh["u"][1], d_sh["u"][1].v)
        marks["phaseA_done"] = (pq, pq.v)
        marks["gS_free"] = (wf, 0)
        marks["uS_free"] = (wf, 0)

        def ln_stats(chunks):
            res_mark = marks["res"]
            n = len(chunks)
            for i, gc in enumerate(chunks):
                cv = RES[:, gc, :]
                k.v(lambda e, o=sm(S_SUM + i, 1), iv=cv: e.reduce_sum(
                    o, iv, AXX), waits=[res_mark] if i == 0 else [])
                k.a(lambda e, o=SCR1[:], iv=cv, acc=sm(S_SQ + i, 1):
                    e.activation(o, iv, Act.Square, accum_out=acc),
                    waits=([res_mark, marks.get("scr1_free", (vq, 0)),
                            marks.get("scr1_free2", (vq, 0)),
                            marks.get("scr1_free3", (vq, 0)),
                            marks.get("scr1_free4", (vq, 0))]
                           if i == 0 else []),
                    inc=(aq if i == n - 1 else None))
            k.v(lambda e, o=sm(S_NM, n), iv=sm(S_SUM, n): e.tensor_scalar_mul(
                o, iv, -1.0 / D))
            k.v(lambda e, o=sm(S_MSQ, n), iv=sm(S_NM, n): e.tensor_tensor(
                o, iv, iv, Alu.mult))
            k.v(lambda e, o=sm(S_VAR, n), i0=sm(S_SQ, n), i1=sm(S_MSQ, n):
                e.scalar_tensor_tensor(o, i0, 1.0 / D, i1, Alu.mult,
                                       Alu.subtract),
                waits=[(aq, aq.v)], inc=vq)
            k.a(lambda e, o=sm(S_STD, n), iv=sm(S_VAR, n): e.activation(
                o, iv, Act.Sqrt, bias=EPSC[:, 0:1]), waits=[(vq, vq.v)],
                inc=aq)
            k.v(lambda e, o=sm(S_RSTD, n), iv=sm(S_STD, n): e.reciprocal(
                o, iv), waits=[(aq, aq.v)])

        def xquant_chain(xv, nblk, am_c, ae_c, d_c, c_c, qf8v, qbfv,
                         qbf3=None, inc=None):
            x3 = xv.rearrange("p (b c) -> p b c", c=32)
            k.v(lambda e, o=sm(am_c, nblk), iv=x3: e.tensor_reduce(
                o, iv, AXX, Alu.max, apply_absolute_value=True))
            k.v(lambda e, o=smi_(ae_c, nblk), iv=smi_(am_c, nblk):
                e.tensor_single_scalar(o, iv, 0x7F800000, Alu.bitwise_and))
            k.v(lambda e, o=sm(ae_c, nblk): e.tensor_scalar_max(o, o, TINY))
            k.v(lambda e, o=sm(d_c, nblk), iv=sm(ae_c, nblk):
                e.tensor_scalar_mul(o, iv, 16.0))
            k.v(lambda e, o=sm(c_c, nblk), iv=sm(d_c, nblk): e.reciprocal(
                o, iv))
            cb = sm(c_c, nblk).unsqueeze(2).broadcast_to([128, nblk, 32])
            db = sm(d_c, nblk).unsqueeze(2).broadcast_to([128, nblk, 32])
            k.v(lambda e, o=x3, c=cb: e.tensor_tensor(o, o, c, Alu.mult))
            k.v(lambda e, o=xv: e.tensor_scalar(o, o, CLIP_A, -CLIP_A,
                                                Alu.min, Alu.max), inc=vq)
            k.a(lambda e, o=qf8v, iv=xv: e.copy(o, iv),
                waits=[(vq, vq.v)], inc=aq)
            f83 = qf8v.rearrange("p (b c) -> p b c", c=32)
            q3 = qbfv.rearrange("p (b c) -> p b c", c=32)
            k.v(lambda e, o=q3, iv=f83, d=db: e.tensor_tensor(
                o, iv, d, Alu.mult), waits=[(aq, aq.v)], inc=(inc or vq))

        def transpose_row(srcv, nt, dst_fn, bsel, evac_extra=()):
            for j0 in range(0, nt, 8):
                nb = min(8, nt - j0)
                b = (bsel + j0 // 8) % 2
                k.w("tensor", (pt[b].rd, pt[b].rd.v), (vq, vq.v))
                for j in range(nb):
                    k.t(lambda e, o=pt[b].ap[:, j * 128:(j + 1) * 128],
                        i=srcv[:, (j0 + j) * 128:(j0 + j + 1) * 128]:
                        e.transpose(o, i, IDENT[:]),
                        inc=(pt[b].wr if j == nb - 1 else None))
                src3 = pt[b].ap[:, 0:nb * 128].rearrange(
                    "p (c f) -> p c f", c=nb)
                k.v(lambda e, o=dst_fn(j0, nb), i=src3: e.tensor_copy(o, i),
                    waits=[(pt[b].wr, pt[b].wr.v)] + list(evac_extra),
                    inc=pt[b].rd)
            k.nop("vector", vq)

        CMAP = [(ib + 8) % NIB for ib in range(NIB)]

        def ffn_tile(t, sub, side=()):
            side = list(side)
            chunks = [t * TCH + c for c in range(TCH)]
            if sub == 1:
                saved = marks["res"]
                marks["res"] = (dxx, 16)
                ln_stats(chunks)
                marks["res"] = saved
            else:
                ln_stats(chunks)
            phA = marks["phaseA_done"]
            for c in range(TCH):
                k.w("vector", marks["qbf_free"])
                k.v(lambda e, o=SCR1[:], iv=RES[:, chunks[c], :],
                    s1=sm(S_NM + c, 1), s2=sm(S_RSTD + c, 1):
                    e.tensor_scalar(o, iv, s1, s2, Alu.add, Alu.mult))
                xquant_chain(SCR1[:], 64, S_AMAX, S_AE2, S_D, S_C,
                             QF8[:], QBF[:])
                ext = [phA] if c == 0 else []
                transpose_row(
                    QBF[:], NHC,
                    lambda j0, nb, c=c: x1t_v[:, j0:j0 + nb,
                                              c * 128:(c + 1) * 128],
                    bsel=c, evac_extra=ext)
                k.nop("tensor", pq)
                marks["qbf_free"] = (pq, pq.v)
            x1t_mark = (vq, vq.v)

            iq_marks = {}

            def emit_inter_transposes(ib):
                first = not marks.get(f"iqt_started_{t}_{sub}")
                for pr in range(2):
                    b = pr
                    qbp = QBF[:, pr * 1024:(pr + 1) * 1024]
                    k.w("tensor", (pt[b].rd, pt[b].rd.v), iq_marks[(ib, pr)])
                    nt = 0
                    for ch in range(4):
                        for sh in range(2):
                            nt += 1
                            k.t(lambda e,
                                o=pt[b].ap[:, (ch * 2 + sh) * 128:
                                           (ch * 2 + sh + 1) * 128],
                                i=qbp[:, sh * 512 + ch * 128:
                                      sh * 512 + (ch + 1) * 128]:
                                e.transpose(o, i, IDENT[:]),
                                inc=(pt[b].wr if nt == 8 else None))
                    src3 = pt[b].ap[:, 0:1024].rearrange(
                        "p (c f) -> p c f", c=4)
                    cm = CMAP[ib]
                    dst = iqt_v[:, 4 * cm:4 * cm + 4,
                                2 * pr * 128:(2 * pr + 2) * 128]
                    ew = [(pt[b].wr, pt[b].wr.v)]
                    if first and pr == 0:
                        ew += marks["iqt_free_list"]
                    if cm == 0 and pr == 0 and "iqt_lower_free" in marks:
                        ew += marks.pop("iqt_lower_free")
                    k.v(lambda e, o=dst, i=src3: e.tensor_copy(o, i),
                        waits=ew, inc=pt[b].rd)
                marks[f"iqt_started_{t}_{sub}"] = True
                k.nop("vector", vq)
                k.nop("tensor", pq)
                marks["qbf_free"] = (pq, pq.v)

            def emit_consume_pair(ib, pr):
                s0 = 2 * pr
                pav = pa_t[:, s0:s0 + 2, :].rearrange("p a b -> p (a b)")
                pbv = pb_t[:].rearrange("p a b -> p (a b)")
                sl1 = SCR1[:, pr * 1024:(pr + 1) * 1024]
                sl2 = SCR2[:, pr * 1024:(pr + 1) * 1024]
                qf = QF8[:, pr * 1024:(pr + 1) * 1024]
                qb = QBF[:, pr * 1024:(pr + 1) * 1024]
                k.a(lambda e, o=sl1, iv=pav: e.activation(
                    o, iv, Act.Silu),
                    waits=[(pa[s0].wr, pa[s0].wr.v),
                           (pa[s0 + 1].wr, pa[s0 + 1].wr.v),
                           marks.get(f"sl1_free_{pr}", (vq, 0))],
                    inc=pa[s0].rd)
                k.nop("scalar", pa[s0 + 1].rd)
                k.nop("scalar", aq)
                silu_mark = (aq, aq.v)
                ivw = [silu_mark, (pb[0].wr, pb[0].wr.v),
                       (pb[1].wr, pb[1].wr.v)]
                if ib == 0:
                    ivw.append(marks["qbf_free"])
                    if t == 0 and sub == 1:
                        ivw.append(marks["prep_gq"])
                k.v(lambda e, o=sl2, i0=sl1, i1=pbv: e.tensor_tensor(
                    o, i0, i1, Alu.mult), waits=ivw, inc=pb[0].rd)
                k.nop("vector", pb[1].rd)
                k.nop("vector", vq)
                marks[f"sl1_free_{pr}"] = (vq, vq.v)
                xquant_chain(sl2, 32, S_IAM, S_IAE, S_ID, S_IC, qf, qb)
                iq_marks[(ib, pr)] = (vq, vq.v)

            for ib in range(NIB):
                gw = marks["gS_free"]
                gw = (list(gw) if isinstance(gw, list) else [gw])
                k.dma("sync", gS_v, wT_r["g"][:, :, ib * IB:(ib + 1) * IB],
                      d_gS, waits=gw + [marks["ag_g"]])
                g_ld = (d_gS, d_gS.v)
                uw = marks["uS_free"]
                uw = (list(uw) if isinstance(uw, list) else [uw])
                k.dma("sync", uS_v, wT_r["u"][:, :, ib * IB:(ib + 1) * IB],
                      d_uS, waits=uw + [marks["ag_u"]])
                u_ld = (d_uS, d_uS.v)
                for s in range(TCH):
                    bk_ = pa[s]
                    k.w("tensor", (bk_.rd, bk_.rd.v), g_ld, x1t_mark)
                    for h in range(NHC):
                        k.t(lambda e, o=bk_.ap,
                            l=x1t_v[:, h, s * 128:(s + 1) * 128],
                            r=gS_v[:, h, :], st=(h == 0), sp=(h == NHC - 1):
                            e.matmul(o, l, r, start=st, stop=sp),
                            inc=(bk_.wr if h == NHC - 1 else None))
                marks["gS_free"] = [(pa[s].wr, pa[s].wr.v)
                                    for s in range(TCH)]
                if ib > 0:
                    emit_inter_transposes(ib - 1)
                for _ in range(2):
                    if side:
                        side.pop(0)()
                if ib == NIB - 1:
                    while side:
                        side.pop(0)()
                for s in range(TCH):
                    bk_ = pb[s % 2]
                    k.w("tensor", (bk_.rd, bk_.rd.v), u_ld)
                    for h in range(NHC):
                        k.t(lambda e, o=bk_.ap,
                            l=x1t_v[:, h, s * 128:(s + 1) * 128],
                            r=uS_v[:, h, :], st=(h == 0), sp=(h == NHC - 1):
                            e.matmul(o, l, r, start=st, stop=sp),
                            inc=(bk_.wr if h == NHC - 1 else None))
                    if s % 2 == 1:
                        emit_consume_pair(ib, s // 2)
                marks["uS_free"] = [(pb[j].wr, pb[j].wr.v) for j in (0, 1)]

            emit_inter_transposes(NIB - 1)
            k.nop("tensor", pq)
            marks["phaseA_done"] = (pq, pq.v)
            iqt_mark = (vq, vq.v)

            for hg in range(4):
                ldm = {}
                for pc in range(2):
                    sl = pc % 2
                    w0 = marks["gS_free"] if hg == 0 else \
                        marks[f"wd_free_{sl}"]
                    wl = (list(w0) if isinstance(w0, list) else [w0]) + \
                        [marks["ag_d"]]
                    k.dma("sync", wdS_v[sl],
                          wT_r["d"][:, pc * 8:(pc + 1) * 8,
                                    hg * 512:(hg + 1) * 512], d_wd[sl],
                          waits=wl)
                    ldm[pc] = (d_wd[sl], d_wd[sl].v)
                for pc in range(8):
                    sl = pc % 2
                    k.w("tensor", ldm[pc])
                    if pc == 0:
                        k.w("tensor", iqt_mark,
                            *[(pa[s].rd, pa[s].rd.v) for s in range(TCH)])
                    for s in range(TCH):
                        bk_ = pa[s]
                        for ic in range(8):
                            st = (pc == 0 and ic == 0)
                            sp = (pc == 7 and ic == 7)
                            lc = pc * 8 + ic
                            pchunk = 4 * CMAP[lc // 4] + lc % 4
                            last_of_pc = (s == TCH - 1 and ic == 7)
                            k.t(lambda e, o=bk_.ap,
                                l=iqt_v[:, pchunk,
                                        s * 128:(s + 1) * 128],
                                r=wdS_v[sl][:, ic, :], st=st, sp=sp:
                                e.matmul(o, l, r, start=st, stop=sp),
                                inc=(bk_.wr if sp else
                                     (wf if last_of_pc else None)))
                    if pc == 7:
                        marks[f"wd_free_{sl}"] = [(pa[s].wr, pa[s].wr.v)
                                                  for s in range(TCH)]
                    else:
                        marks[f"wd_free_{sl}"] = (wf, wf.v)
                    if pc + 2 < 8:
                        w1 = marks[f"wd_free_{sl}"]
                        k.dma("sync", wdS_v[(pc + 2) % 2],
                              wT_r["d"][:, (pc + 2) * 8:(pc + 3) * 8,
                                        hg * 512:(hg + 1) * 512],
                              d_wd[(pc + 2) % 2],
                              waits=(list(w1) if isinstance(w1, list)
                                     else [w1]))
                        ldm[pc + 2] = (d_wd[(pc + 2) % 2],
                                       d_wd[(pc + 2) % 2].v)
                for pr in range(2):
                    s0 = 2 * pr
                    pav3 = pa_t[:, s0:s0 + 2, :]
                    rv = RES[:, chunks[s0]:chunks[s0] + 2,
                             hg * 512:(hg + 1) * 512]
                    k.v(lambda e, o=rv, i0=pav3, i1=rv:
                        e.scalar_tensor_tensor(o, i0, 0.5, i1, Alu.mult,
                                               Alu.add),
                        waits=[(pa[s0].wr, pa[s0].wr.v),
                               (pa[s0 + 1].wr, pa[s0 + 1].wr.v)],
                        inc=pa[s0].rd)
                    k.nop("vector", pa[s0 + 1].rd)
            k.nop("vector", vq)
            marks["res"] = (vq, vq.v)
            k.nop("tensor", pq)
            marks["iqt_free_list"] = [(pq, pq.v)]
            marks["gS_free"] = [(pa[s].wr, pa[s].wr.v) for s in range(TCH)]
            marks["scr1_free"] = (vq, vq.v)
            marks["scr1_free2"] = (vq, 0)
            marks["scr1_free3"] = (vq, 0)
            marks["scr1_free4"] = (vq, 0)

        def attention():
            ln_stats(list(range(NCH)))
            for c in range(NCH):
                ew = [marks["qbf_free"]]
                k.w("vector", marks["qbf_free"])
                if c == 0:
                    k.w("vector", marks["phaseA_done"])
                k.v(lambda e, o=QBF[:], iv=RES[:, c, :], s1=sm(S_NM + c, 1),
                    s2=sm(S_RSTD + c, 1): e.tensor_scalar(
                        o, iv, s1, s2, Alu.add, Alu.mult), inc=vq)
                ext = marks["iqt_free_list"] if c == 0 else []
                transpose_row(
                    QBF[:], NHC,
                    lambda j0, nb, c=c: x2t_v[:, j0:j0 + nb,
                                              c * 128:(c + 1) * 128],
                    bsel=c, evac_extra=ext)
                k.nop("tensor", pq)
                marks["qbf_free"] = (pq, pq.v)
            x2t_mark = (vq, vq.v)
            k.nop("tensor", pq)
            x2t_pe = (pq, pq.v)

            gw_ = marks["gS_free"]
            k.dma("sync", kS_v, wT_r["k"][:], d_kS,
                  waits=(list(gw_) if isinstance(gw_, list) else [gw_])
                  + [marks["ag_k"]])
            k_ld = (d_kS, d_kS.v)
            k.dma("sync", vS_v, wT_r["v"][:], d_vS, waits=[marks["ag_v"]])
            v_ld = (d_vS, d_vS.v)

            for which, wv_, ld, outv in (("k", kS_v, k_ld, sk_v),
                                         ("v", vS_v, v_ld, vv_v)):
                for s in range(NCH):
                    bk_ = pa[s % 4]
                    k.w("tensor", (bk_.rd, bk_.rd.v), ld)
                    if which == "k" and s == 0:
                        k.w("tensor", x2t_mark)
                    for h in range(NHC):
                        k.t(lambda e, o=bk_.ap,
                            l=x2t_v[:, h, s * 128:(s + 1) * 128],
                            r=wv_[:, h, :], st=(h == 0), sp=(h == NHC - 1):
                            e.matmul(o, l, r, start=st, stop=sp),
                            inc=(bk_.wr if h == NHC - 1 else None))
                    if which == "k":
                        sl1 = SCR1[:, (s % 4) * 512:(s % 4 + 1) * 512]
                        sl2 = SCR2[:, (s % 4) * 512:(s % 4 + 1) * 512]
                        w0 = [(bk_.wr, bk_.wr.v),
                              marks.get(f"asl_{s % 4}", (vq, 0))]
                        if s == 0:
                            w0.append(marks["phaseA_done"])
                        k.v(lambda e, o=sl1, i0=bk_.ap, i1=BBK[:]:
                            e.tensor_tensor(o, i0, i1, Alu.add),
                            waits=w0, inc=bk_.rd)
                        k.v(lambda e, o=sl2, iv=sl1: e.tensor_scalar_min(
                            o, iv, 0.0), inc=vq)
                        k.a(lambda e, o=sl2, iv=sl2: e.activation(
                            o, iv, Act.Exp), waits=[(vq, vq.v)], inc=aq)
                        k.v(lambda e, o=outv[:, s, :], i0=sl1, i1=sl2:
                            e.scalar_tensor_tensor(o, i0, 0.0, i1, Alu.max,
                                                   Alu.add),
                            waits=[(aq, aq.v)], inc=vq)
                        marks[f"asl_{s % 4}"] = (vq, vq.v)
                    else:
                        k.v(lambda e, o=outv[:, s, :], i0=bk_.ap, i1=BBV[:]:
                            e.tensor_tensor(o, i0, i1, Alu.add),
                            waits=[(bk_.wr, bk_.wr.v)], inc=bk_.rd)
                k.nop("tensor", wf)
            k.nop("vector", vq)
            skv_mark = (vq, vq.v)

            k.v(lambda e: e.memset(SCR1[:, 0:528], 0.0),
                waits=[marks.get("asl_0", (vq, 0))], inc=vq)
            z_mark = (vq, vq.v)
            for g_ in range(KVH):
                bk_ = pb[g_ % 2]
                k.w("tensor", (bk_.rd, bk_.rd.v), skv_mark)
                for s in range(NCH):
                    k.t(lambda e, o=bk_.ap[:, 0:128],
                        l=sk_v[:, s, g_ * 128:(g_ + 1) * 128],
                        r=vv_v[:, s, g_ * 128:(g_ + 1) * 128], st=(s == 0):
                        e.matmul(o, l, r, start=st, stop=False))
                for s in range(NCH):
                    k.t(lambda e, o=bk_.ap[:, 128:129],
                        l=sk_v[:, s, g_ * 128:(g_ + 1) * 128],
                        r=ONESB[:, 0:1], sp=(s == NCH - 1):
                        e.matmul(o, l, r, start=False, stop=sp),
                        inc=(bk_.wr if s == NCH - 1 else None))
                k.v(lambda e, o=SCR1[:, g_ * 132:g_ * 132 + 129],
                    i=bk_.ap[:, 0:129]: e.tensor_copy(o, i),
                    waits=[(bk_.wr, bk_.wr.v), z_mark], inc=bk_.rd)
            k.nop("vector", vq)
            kvp_mark = (vq, vq.v)
            k.nop("tensor", pq)
            kvmm_pe = (pq, pq.v)
            marks["phaseA_done"] = kvmm_pe
            k.dma("gpsimd",
                  kvin.ap().rearrange("(g p) c -> p g c", p=128),
                  SCR1[:, 0:528].rearrange("p (g c) -> p g c", c=132),
                  d_kvs, waits=[kvp_mark])
            kv_ship = (d_kvs, d_kvs.v)
            k.g(lambda e: e.collective_compute(
                "AllGather", Alu.bypass,
                replica_groups=[[0, 1, 2, 3], [4, 5, 6, 7]],
                ins=[kvin.ap().opt()], outs=[kvout.ap().opt()]),
                waits=[kv_ship, (ag, AGN["kv"] - 1)], inc=ag)
            kv_ag = (ag, ag.v)

            qo_ld = {}
            for dqb in range(2):
                k.dma("sync", qoS_v[dqb % 2],
                      wT_r["q"][:, :, dqb * 512:(dqb + 1) * 512],
                      d_qo[dqb % 2], waits=[(wf, wf.v), marks["ag_q"]])
                qo_ld[dqb] = (d_qo[dqb % 2], d_qo[dqb % 2].v)
            for dqb in range(4):
                sl = dqb % 2
                for c2 in range(4):
                    dq = dqb * 4 + c2
                    for half in range(2):
                        bk_ = pa[(2 * dq + half) % 4]
                        k.w("tensor", (bk_.rd, bk_.rd.v), qo_ld[dqb])
                        for h in range(NHC):
                            k.t(lambda e, o=bk_.ap,
                                l=qoS_v[sl][:, h, c2 * 128:(c2 + 1) * 128],
                                r=x2t_v[:, h, half * 512:(half + 1) * 512],
                                st=(h == 0), sp=(h == NHC - 1):
                                e.matmul(o, l, r, start=st, stop=sp),
                                inc=(bk_.wr if h == NHC - 1 else None))
                        j = (2 * dq + half) % 4
                        sl1 = SCR1[:, j * 512:(j + 1) * 512]
                        sl2 = SCR2[:, j * 512:(j + 1) * 512]
                        w0 = [(bk_.wr, bk_.wr.v),
                              marks.get(f"asl_{j}", (vq, 0))]
                        if dqb == 0 and c2 == 0 and half == 0:
                            w0.append(kv_ship)
                        k.v(lambda e, o=sl1, iv=bk_.ap,
                            s1=BQF[:, dq:dq + 1]: e.tensor_scalar_add(
                                o, iv, s1), waits=w0, inc=bk_.rd)
                        k.v(lambda e, o=sl2, iv=sl1: e.tensor_scalar_min(
                            o, iv, 0.0), inc=vq)
                        k.a(lambda e, o=sl2, iv=sl2: e.activation(
                            o, iv, Act.Exp), waits=[(vq, vq.v)], inc=aq)
                        k.v(lambda e,
                            o=sqt_v[:, dq, half * 512:(half + 1) * 512],
                            i0=sl1, i1=sl2: e.scalar_tensor_tensor(
                                o, i0, 0.0, i1, Alu.max, Alu.add),
                            waits=[(aq, aq.v)], inc=vq)
                        marks[f"asl_{j}"] = (vq, vq.v)
                k.nop("tensor", wf)
                marks[f"qo_free_{sl}"] = (wf, wf.v)
                if dqb + 2 < 4:
                    k.dma("sync", qoS_v[sl],
                          wT_r["q"][:, :, (dqb + 2) * 512:(dqb + 3) * 512],
                          d_qo[sl], waits=[marks[f"qo_free_{sl}"]])
                    qo_ld[dqb + 2] = (d_qo[sl], d_qo[sl].v)
            sqt_mark = (vq, vq.v)

            k.dma("gpsimd", kvs1,
                  kvout.ap().rearrange("(b p) c -> p b c", p=128)[:, 0:8, :],
                  dgp, waits=[kv_ag, kvmm_pe])
            k.dma("gpsimd", kvs2,
                  kvout.ap().rearrange("(b p) c -> p b c", p=128)[:, 8:16, :],
                  dgp)
            kvg_ld = (dgp, dgp.v)
            k.v(lambda e: e.tensor_tensor(kvs1[:, 0:4, :], kvs1[:, 0:4, :],
                                          kvs1[:, 4:8, :], Alu.add),
                waits=[kvg_ld])
            k.v(lambda e: e.tensor_tensor(kvs2[:, 0:4, :], kvs2[:, 0:4, :],
                                          kvs2[:, 4:8, :], Alu.add))
            k.v(lambda e: e.tensor_tensor(kvs1[:, 0:4, :], kvs1[:, 0:4, :],
                                          kvs2[:, 0:4, :], Alu.add))
            k.v(lambda e: e.tensor_copy(kvb_v[:], kvs1[:, 0:4, :]), inc=vq)
            kvb_mark = (vq, vq.v)

            k.nop("tensor", pq)
            qproj_pe = (pq, pq.v)
            for s in range(NCH):
                for h in range(NH):
                    bk_ = pb[h % 2]
                    g_ = h // 4
                    w0 = [(bk_.rd, bk_.rd.v)]
                    if s == 0 and h == 0:
                        w0 += [kvb_mark, sqt_mark]
                    k.w("tensor", *w0)
                    k.t(lambda e, o=bk_.ap[:, 0:132],
                        l=sqt_v[:, h, s * 128:(s + 1) * 128],
                        r=kvb_v[:, g_, :]: e.matmul(o, l, r, start=True,
                                                    stop=True), inc=bk_.wr)
                    w1 = [(bk_.wr, bk_.wr.v)]
                    if h == 0:
                        w1.append(marks["qbf_free"])
                    k.v(lambda e, o=sm(S_NRM, 1), iv=bk_.ap[:, 128:129]:
                        e.tensor_scalar_add(o, iv, EPS_AT), waits=w1)
                    k.v(lambda e, o=sm(S_NRM, 1), iv=sm(S_NRM, 1):
                        e.reciprocal(o, iv))
                    k.v(lambda e, o=QBF[:, h * 128:(h + 1) * 128],
                        iv=bk_.ap[:, 0:128], s1=sm(S_NRM, 1):
                        e.tensor_scalar_mul(o, iv, s1), inc=bk_.rd)
                k.nop("vector", vq)
                ext = [qproj_pe] if s == 0 else []
                transpose_row(
                    QBF[:], NH,
                    lambda j0, nb, s=s: atnT_v[:, j0:j0 + nb,
                                               s * 128:(s + 1) * 128],
                    bsel=s, evac_extra=ext)
                k.nop("tensor", pq)
                marks["qbf_free"] = (pq, pq.v)
            atn_mark = (vq, vq.v)

            ldo = {}
            for hob in range(2):
                k.dma("sync", qoS_v[hob % 2],
                      wT_r["o"][:, :, hob * 512:(hob + 1) * 512],
                      d_qo[hob % 2],
                      waits=[marks[f"qo_free_{hob % 2}"], marks["ag_o"]])
                ldo[hob] = (d_qo[hob % 2], d_qo[hob % 2].v)
            for hob in range(4):
                sl = hob % 2
                for s in range(NCH):
                    bk_ = pa[s % 4]
                    w0 = [(bk_.rd, bk_.rd.v), ldo[hob]]
                    if hob == 0 and s == 0:
                        w0.append(atn_mark)
                    k.w("tensor", *w0)
                    for h in range(NHC):
                        k.t(lambda e, o=bk_.ap,
                            l=atnT_v[:, h, s * 128:(s + 1) * 128],
                            r=qoS_v[sl][:, h, :], st=(h == 0),
                            sp=(h == NHC - 1):
                            e.matmul(o, l, r, start=st, stop=sp),
                            inc=(bk_.wr if h == NHC - 1 else None))
                    rv = RES[:, s, hob * 512:(hob + 1) * 512]
                    k.v(lambda e, o=rv, i0=bk_.ap, i1=rv: e.tensor_tensor(
                        o, i0, i1, Alu.add),
                        waits=[(bk_.wr, bk_.wr.v)], inc=bk_.rd)
                k.nop("tensor", wf)
                marks[f"qo_free_{sl}"] = (wf, wf.v)
                if hob + 2 < 4:
                    k.dma("sync", qoS_v[sl],
                          wT_r["o"][:, :, (hob + 2) * 512:(hob + 3) * 512],
                          d_qo[sl], waits=[marks[f"qo_free_{sl}"]])
                    ldo[hob + 2] = (d_qo[sl], d_qo[sl].v)
            k.nop("vector", vq)
            marks["res"] = (vq, vq.v)
            marks["scr1_free"] = (vq, vq.v)
            k.nop("tensor", pq)
            marks["iqt_free_list"] = [(pq, pq.v)]
            marks["gS_free"] = marks["qo_free_0"]
            marks["uS_free"] = marks["qo_free_1"]

        ffn_tile(0, 1, d_units)
        ffn_tile(1, 1)
        attention()
        ffn_tile(0, 3)
        ffn_tile(1, 3)

        k.dma("gpsimd", out.rearrange("(c p) h -> p c h", p=128), RES[:],
              dgp, waits=[marks["res"]])
        k.w("gpsimd", (dgp, dgp.v))

        with nc.Block() as block:
            @block.sync
            def _(e):
                for fn in k.q["sync"]:
                    fn(e)

            @block.scalar
            def _(e):
                for fn in k.q["scalar"]:
                    fn(e)

            @block.vector
            def _(e):
                for fn in k.q["vector"]:
                    fn(e)

            @block.gpsimd
            def _(e):
                for fn in k.q["gpsimd"]:
                    fn(e)

            @block.tensor
            def _(e):
                for fn in k.q["tensor"]:
                    fn(e)
    return nc


_NC_CACHE = None


def kernel(L0_attn_hidden_states, L0_attn_Wq, L0_attn_bq, L0_attn_Wk,
           L0_attn_bk, L0_attn_Wv, L0_attn_bv, L0_attn_Wo, L0_ffn_w_gate,
           L0_ffn_w_up, L0_ffn_w_down):
    global _NC_CACHE
    x = np.ascontiguousarray(np.asarray(L0_attn_hidden_states, np.float32))
    Wq = np.asarray(L0_attn_Wq, np.float32)
    Wk = np.asarray(L0_attn_Wk, np.float32)
    Wv = np.asarray(L0_attn_Wv, np.float32)
    Wo = np.asarray(L0_attn_Wo, np.float32)
    Wg = np.asarray(L0_ffn_w_gate, np.float32)
    Wu = np.asarray(L0_ffn_w_up, np.float32)
    Wd = np.asarray(L0_ffn_w_down, np.float32)
    bq = np.asarray(L0_attn_bq, np.float32)
    bk = np.asarray(L0_attn_bk, np.float32)
    bv = np.asarray(L0_attn_bv, np.float32)
    ident = np.eye(128, dtype=ml_dtypes.bfloat16)

    in_maps = []
    for c in range(8):
        b, tc = divmod(c, 4)
        in_maps.append({
            "x_sl": np.ascontiguousarray(x[b, tc * NTOK:(tc + 1) * NTOK, :]),
            "wg_sl": np.ascontiguousarray(Wg[:, c * HSL:(c + 1) * HSL]),
            "wu_sl": np.ascontiguousarray(Wu[:, c * HSL:(c + 1) * HSL]),
            "wd_sl": np.ascontiguousarray(Wd[:, c * ISL:(c + 1) * ISL]),
            "wq_sl": np.ascontiguousarray(Wq[:, c * HSL:(c + 1) * HSL]),
            "wk_sl": np.ascontiguousarray(Wk[:, c * HSL:(c + 1) * HSL]),
            "wv_sl": np.ascontiguousarray(Wv[:, c * HSL:(c + 1) * HSL]),
            "wo_sl": np.ascontiguousarray(Wo[:, c * HSL:(c + 1) * HSL]),
            "bq_in": bq, "bk_in": bk, "bv_in": bv, "id_in": ident,
        })
    if _NC_CACHE is None:
        _NC_CACHE = build_nc()
    res = run_bass_kernel_spmd(_NC_CACHE, in_maps, core_ids=list(range(8)))
    outf = np.empty((2, 4096, D), np.float32)
    for c in range(8):
        b, tc = divmod(c, 4)
        outf[b, tc * NTOK:(tc + 1) * NTOK, :] = res.results[c]["out"]
    return outf
